# version 03
# baseline (speedup 1.0000x reference)
import numpy as np
import ml_dtypes
from contextlib import ExitStack

import concourse.bass as bass
import concourse.mybir as mybir
from concourse.bass_utils import run_bass_kernel_spmd
from concourse.alu_op_type import AluOpType as OP

F32 = mybir.dt.float32
BF16 = mybir.dt.bfloat16
AF = mybir.ActivationFunctionType
AX = mybir.AxisListType.X

NCORES = 8
T = 2048
D = 1024
DT = D // 128
BF = np.dtype(ml_dtypes.bfloat16)

E1 = 2048
NE1 = 16
NS1 = 16
FF = 4096
LORA_SC = 2.0
MAGIC = 12582912.0

I2 = 2048
NH2 = 32
P2 = 64
N2 = 64
CONV2 = 2176
PROJ2 = 4256
Q2 = 128
NCH2 = 16

NEXP = 8
HME = 2048
TEST_NO_INTER = False

ENGINES = ("tensor", "vector", "scalar", "gpsimd", "sync")

def _ap3(ap2d, dims, offset_extra=0):
    return type(ap2d)(ap2d.tensor, ap2d.offset + offset_extra, [list(d) for d in dims])


def tile_kmajor(ap2d, p=128):
    (rs, R), (cs, C) = ap2d.ap
    assert R % p == 0
    return _ap3(ap2d, [[rs, p], [rs * p, R // p], [cs, C]])


def tile_colmajor(ap2d, p=128):
    (rs, H), (cs, Cc) = ap2d.ap
    assert Cc % p == 0
    return _ap3(ap2d, [[cs, p], [cs * p, Cc // p], [rs, H]])




class Tok:
    __slots__ = ("sem", "val")
    def __init__(self, sem, val):
        self.sem, self.val = sem, val


class Buf:
    __slots__ = ("name", "w", "r")
    def __init__(self, name):
        self.name, self.w, self.r = name, None, []


class Sched:
    def __init__(self, nc):
        self.nc = nc
        self.prog = {e: [] for e in ENGINES}
        self.sem = {}
        self.cnt = {}
        self.waited = {e: {} for e in ENGINES}
        self.es = None
        self.bufs = {}

    def buf(self, name):
        if name not in self.bufs:
            self.bufs[name] = Buf(name)
        return self.bufs[name]

    def new_sem(self, name):
        self.sem[name] = self.es.enter_context(self.nc.semaphore(name))
        self.cnt[name] = 0

    DMA_QS = ("qs", "qg", "qv")
    SAFE = False

    def _safe_pre(self, eng):
        if not self.SAFE:
            return
        for sem in ("tensor", "vector", "scalar", "gpsimd", "sync", "qs", "qg", "qv", "cc"):
            if sem == eng:
                continue
            c = self.cnt.get(sem, 0)
            if c > 0:
                self._wait(eng, Tok(sem, c))

    def _wait(self, eng, tok):
        if tok is None or tok.sem == eng:
            return
        val = tok.val
        if tok.sem in self.DMA_QS:
            val = self.cnt[tok.sem]
        w = self.waited[eng]
        if w.get(tok.sem, 0) >= val:
            return
        w[tok.sem] = val
        sem = self.sem[tok.sem]
        self.prog[eng].append(lambda e, val=val: e.wait_ge(sem, val))

    def _pre(self, eng, reads, writes):
        for b in reads:
            self._wait(eng, b.w)
        for b in writes:
            self._wait(eng, b.w)
            for t_ in b.r:
                self._wait(eng, t_)

    def _post(self, tok, reads, writes):
        for b in reads:
            b.r.append(tok)
        for b in writes:
            b.w, b.r = tok, []

    def op(self, eng, fn, reads=(), writes=()):
        self._safe_pre(eng)
        self._pre(eng, reads, writes)
        self.cnt[eng] += 1
        sem = self.sem[eng]
        tok = Tok(eng, self.cnt[eng])

        def run(e, fn=fn, sem=sem):
            fn(e).then_inc(sem, 1)
        self.prog[eng].append(run)
        self._post(tok, reads, writes)
        return tok

    def dma(self, eng, qsem, out, in_, reads=(), writes=(), slow=False, accum=False):
        self._safe_pre(eng)
        self._pre(eng, reads, writes)
        self.cnt[qsem] += 16
        sem = self.sem[qsem]
        tok = Tok(qsem, self.cnt[qsem])

        def run(e, out=out, in_=in_, sem=sem, slow=slow, accum=accum):
            kw = dict(allow_slow_non_contiguous=slow)
            if accum:
                kw["accum_op"] = OP.add
            e.dma_start(out=out, in_=in_, **kw).then_inc(sem, 16)
        self.prog[eng].append(run)
        self._post(tok, reads, writes)
        return tok

    def cc(self, kind, op, ins, outs, reads=(), writes=(), barrier=None):
        eng = "gpsimd"
        self._safe_pre(eng)
        self._pre(eng, reads, writes)
        sem = self.sem["cc"]
        self.cnt["cc"] += 1

        def run(e, sem=sem):
            e.collective_compute(kind, op, replica_groups=[list(range(NCORES))],
                                 ins=ins, outs=outs).then_inc(sem, 1)
        self.prog[eng].append(run)
        self.cnt["cc"] += 1
        self.prog[eng].append(run)
        if barrier is not None:
            bin_, bout = barrier
            self.cnt["cc"] += 1

            def run2(e, sem=sem):
                e.collective_compute("AllGather", OP.bypass,
                                     replica_groups=[list(range(NCORES))],
                                     ins=[bin_], outs=[bout]).then_inc(sem, 1)
            self.prog[eng].append(run2)
        tok = Tok("cc", self.cnt["cc"])
        self._post(tok, reads, writes)
        return tok


def build_nc(upto="all", taps=False):
    nc = bass.Bass("TRN2", target_bir_lowering=False)
    S = Sched(nc)
    tap_list = []

    def inp(name, shape, dt=F32):
        return nc.dram_tensor(name, shape, dt, kind="ExternalInput").ap()

    x_in = inp("x_in", [D, T])
    consts = inp("consts", [1, 32])
    identf_in = inp("identf", [128, 128])
    trimask_in = inp("trimask", [128, 128])
    constf_in = inp("constf", [1, 8])

    w_in = inp("W_in", [D, 2 * E1])
    conv_w0 = inp("conv_w0", [128, 64])
    conv_b0 = inp("conv_b0", [128, 16])
    w_xp = inp("W_xp", [E1, 96])
    w_dt = inp("W_dt", [64, E1])
    dt_bias0 = inp("dt_bias0", [128, 16])
    a_log0 = inp("A_log0", [E1, NS1])
    d_skip0 = inp("D_skip0", [128, 16])
    w_out0 = inp("W_out0", [E1, D])

    w_fc = inp("W_fc_t", [D, FF])
    b_fc = inp("b_fc", [128, 32])
    a_fc = inp("A_fc_t", [D, 16])
    b_fc_l = inp("B_fc_t", [16, FF])
    w_proj = inp("W_proj_t", [FF, D])
    b_proj = inp("b_proj", [128, 8])
    a_proj = inp("A_proj_t", [FF, 16])
    b_proj_l = inp("B_proj_t", [16, D])

    in_proj1 = inp("in_proj1", [2 * D, PROJ2])
    out_proj1 = inp("out_proj1", [2 * I2, D])
    m2small = inp("m2small", [256, 128])
    m2drow = inp("m2drow", [2, 2048])
    normf_cols = inp("normf_cols", [128, 8])

    wg = inp("Wg", [D, NEXP])
    w1e = inp("W1e", [NEXP, D, HME])
    b1e = inp("b1e", [NEXP, HME])
    w2e = inp("W2e", [NEXP, HME, D])
    b2e = inp("b2e", [NEXP, D])
    b1cols = inp("b1cols", [128, 128])

    out = nc.dram_tensor("out", [D, T], F32, kind="ExternalOutput").ap()

    def scratch(name, shape, dt=F32):
        return nc.dram_tensor(name, shape, dt).ap()

    d_xpark = scratch("d_xpark", [D, T])
    d_u = scratch("d_u", [E1, T], BF16)
    d_z = scratch("d_z", [E1, T], BF16)
    d_y1 = scratch("d_y1", [E1, T])
    d_yhat = scratch("d_yhat", [E1, T], BF16)
    d_bc1 = scratch("d_bc1", [32, T], BF16)
    d_row = scratch("d_row", [8, 4096])
    d_wq = scratch("d_wq", [FF + D, max(FF, D)], BF16)
    d_gate = scratch("d_gate", [I2, T], BF16)
    d_m2st = scratch("d_m2st", [NCH2 * 16 * 128, N2])
    d_wqm = scratch("d_wqm", [NEXP * 3072, 2048], BF16)

    agbar_in = scratch("agbar_in", [1, 16])
    agbar_out = scratch("agbar_out", [NCORES, 16])
    aghx_in = scratch("aghx_in", [128, 64])
    aghx_out = scratch("aghx_out", [NCORES * 128, 64])
    ag1_in = scratch("ag1_in", [128, 272])
    ag1_out = scratch("ag1_out", [NCORES * 128, 272])
    agh_in = [scratch(f"agh{i}_in", [3, D]) for i in range(2)]
    agh_out = [scratch(f"agh{i}_out", [NCORES * 3, D]) for i in range(2)]
    ag2_in = [scratch(f"ag2{i}_in", [128, 1056]) for i in range(2)]
    ag2_out = [scratch(f"ag2{i}_out", [NCORES * 128, 1056]) for i in range(2)]

    es = ExitStack()
    S.es = es
    SBA = lambda name, shape, dt=F32: es.enter_context(nc.sbuf_tensor(name, shape, dt))
    PSA = lambda name, shape, dt=F32: es.enter_context(nc.psum_tensor(name, shape, dt))

    with es:
        for e in ENGINES:
            S.new_sem(e)
        for q in ("qs", "qg", "qv", "cc"):
            S.new_sem(q)

        s_res = SBA("s_res", [128, DT * T])
        s_bf = SBA("s_bf", [128, 32768], BF16)
        s_wk = SBA("s_wk", [128, 4 * T + 8])
        s_wst = SBA("s_wst", [128, 2048])
        s_wsm = SBA("s_wsm", [128, 4096], BF16)
        s_stats = SBA("s_stats", [128, 2 * T])

        s_ident = SBA("s_ident", [128, 128])
        s_identb = SBA("s_identb", [128, 128], BF16)
        s_onesb = SBA("s_onesb", [128, 128], BF16)
        s_tri = SBA("s_tri", [128, 128])
        s_small = SBA("s_small", [128, 128])
        s_small2 = SBA("s_small2", [128, 128])
        s_cmask = SBA("s_cmask", [128, 32])
        s_const = SBA("s_const", [128, 8])
        s_hx3 = SBA("s_hx3", [128, 160])
        s_qs = SBA("s_qs", [128, 176])
        s_m2a = SBA("s_m2a", [128, 32])
        s_m2p = SBA("s_m2p", [128, 128])
        s_A = SBA("s_A", [128, 256])
        s_hend = SBA("s_hend", [128, 272])
        s_h0 = SBA("s_h0", [128, 256])
        s_xdbl = SBA("s_xdbl", [96, T])

        p_big = PSA("p_big", [128, 2048])
        pb0 = PSA("pb0", [128, 512])
        pb1 = PSA("pb1", [128, 512])
        pt_f = PSA("pt_f", [128, 256])
        pt_b = PSA("pt_b", [128, 256], BF16)

        blk = es.enter_context(nc.Block())

        buf = S.buf
        def BFS(off, ncols):
            return [buf(f"bf{j}") for j in range(off // 2048, (off + ncols + 2047) // 2048)]
        def WKB(i):
            return [buf(f"wk{i}")]
        def WSTB(off, ncols):
            return [buf(f"wst{j}") for j in range(off // 1024, (off + ncols + 1023) // 1024)]
        def STB(off, ncols):
            return [buf(f"st{j}") for j in range(off // 512, (off + ncols + 511) // 512)]
        def PBG(off, ncols):
            return [buf(f"pB{j}") for j in range(off // 512, (off + ncols + 511) // 512)]

        WK = lambda i: s_wk[:, i * T:(i + 1) * T]

        v = lambda fn, reads=(), writes=(): S.op("vector", fn, reads, writes)
        sc = lambda fn, reads=(), writes=(): S.op("scalar", fn, reads, writes)
        te = lambda fn, reads=(), writes=(): S.op("tensor", fn, reads, writes)
        g = lambda fn, reads=(), writes=(): S.op("gpsimd", fn, reads, writes)
        dmas = lambda o, i, reads=(), writes=(), slow=False: S.dma("sync", "qs", o, i, reads, writes, slow)
        dmag = lambda o, i, reads=(), writes=(), slow=False: S.dma("gpsimd", "qg", o, i, reads, writes, slow)
        dmav = lambda o, i, reads=(), writes=(), slow=False: S.dma("gpsimd", "qv", o, i, reads, writes, slow)

        def dbg_tap(name, src_ap, reads, dt=F32, shape=None):
            if not taps:
                return
            shp = shape or [src_ap.shape[0], src_ap.shape[-1]]
            tdr = nc.dram_tensor("tap_" + name, shp, dt, kind="ExternalOutput").ap()
            dmag(tdr, src_ap, reads=list(reads))
            tap_list.append(name)

        def mm(pp, lhsT, rhs, first, last, reads, pwb):
            te(lambda e, pp=pp, lhsT=lhsT, rhs=rhs, first=first, last=last: e.matmul(
                pp, lhsT, rhs, start=first, stop=last),
               reads, pwb if (first or last) else [])

        dmas(s_ident[:], identf_in, writes=[buf("ident")])
        dmas(s_tri[:], trimask_in, writes=[buf("tri")])
        dmas(s_cmask[:], consts.broadcast_to((128, 32)), writes=[buf("cmask")])
        dmas(s_const[:], constf_in.broadcast_to((128, 8)), writes=[buf("cf")])
        CC = lambda i: s_const[:, i:i + 1]
        v(lambda e: e.tensor_copy(s_identb[:], s_ident[:]), [buf("ident")], [buf("identb")])
        v(lambda e: e.memset(s_onesb[:], 1.0), (), [buf("onesb")])
        dmas(s_res[:], tile_kmajor(x_in), writes=[buf("x")])
        dmas(s_A[:], tile_kmajor(a_log0), writes=[buf("A")])
        sc(lambda e: e.activation(s_A[:], s_A[:], AF.Exp), [buf("A")], [buf("A")])
        v(lambda e: e.tensor_single_scalar(s_A[:], s_A[:], -1.0, OP.mult), [buf("A")], [buf("A")])
        dmas(s_small[:, 0:64], conv_w0, writes=[buf("cw0")])
        dmas(s_small[:, 64:80], conv_b0, writes=[buf("cb0")])
        dmas(s_small[:, 80:96], dt_bias0, writes=[buf("dtb0")])
        dmas(s_small[:, 96:112], d_skip0, writes=[buf("dsk0")])

        SB_SQ = 20480

        def ln_stats(src_sb, srcb, ntiles, rms=False, eps=1e-5, statb_name="stats"):
            st = STB(0, 4096)
            Dw = ntiles * 128
            for c in range(T // 512):
                w0 = c * 512
                xb = s_bf[:, SB_SQ:SB_SQ + ntiles * 512]
                qb = s_bf[:, SB_SQ + 4096:SB_SQ + 4096 + ntiles * 512]
                for i in range(ntiles):
                    sc(lambda e, i=i, w0=w0, xb=xb: e.activation(
                        xb[:, i * 512:(i + 1) * 512],
                        src_sb[:, i * T + w0:i * T + w0 + 512], AF.Identity),
                       [srcb], BFS(SB_SQ, 4096))
                    sc(lambda e, i=i, w0=w0, qb=qb: e.activation(
                        qb[:, i * 512:(i + 1) * 512],
                        src_sb[:, i * T + w0:i * T + w0 + 512], AF.Square),
                       [srcb], BFS(SB_SQ + 4096, 4096))
                if not rms:
                    for i in range(ntiles):
                        mm(p_big[:, 0:512], s_onesb[:], xb[:, i * 512:(i + 1) * 512],
                           i == 0, i == ntiles - 1,
                           BFS(SB_SQ, 4096) + [buf("onesb")], PBG(0, 512))
                for i in range(ntiles):
                    mm(p_big[:, 512:1024], s_onesb[:], qb[:, i * 512:(i + 1) * 512],
                       i == 0, i == ntiles - 1,
                       BFS(SB_SQ + 4096, 4096) + [buf("onesb")], PBG(512, 512))
                mu = s_stats[:, w0:w0 + 512]
                rs = s_stats[:, T + w0:T + w0 + 512]
                if rms:
                    v(lambda e, rs=rs: e.tensor_single_scalar(
                        rs, p_big[:, 512:1024], 1.0 / Dw, OP.mult), PBG(512, 512), st)
                else:
                    v(lambda e, mu=mu: e.tensor_single_scalar(
                        mu, p_big[:, 0:512], 1.0 / Dw, OP.mult), PBG(0, 512), st)
                    v(lambda e, mu=mu, rs=rs: e.tensor_tensor(rs, mu, mu, OP.mult), st, st)
                    v(lambda e, rs=rs: e.scalar_tensor_tensor(
                        rs, p_big[:, 512:1024], 1.0 / Dw, rs, OP.mult, OP.subtract),
                      PBG(512, 512) + st, st)
                sc(lambda e, rs=rs: e.activation(rs, rs, AF.Ln, bias=CC(0)), st + [buf("cf")], st)
                sc(lambda e, rs=rs: e.activation(rs, rs, AF.Exp, scale=CC(5)), st + [buf("cf")], st)

        def norm_chunk(src_sb, srcb, dst_off, ntiles, c, width, rms=False,
                       statb_name="stats", tmp_slot=3, dst_stride=None, dst_coff=0):
            st = STB(0, 4096)
            if dst_stride is None:
                dst_stride = width
            w0 = c * width
            for i in range(ntiles):
                src_t = src_sb[:, i * T + w0:i * T + w0 + width]
                d0c = dst_off + i * dst_stride + dst_coff
                dst_t = s_bf[:, d0c:d0c + width]
                if rms:
                    v(lambda e, s_=src_t, d=dst_t: e.tensor_tensor(
                        d, s_, s_stats[:, T + w0:T + w0 + width], OP.mult),
                      [srcb] + st, BFS(d0c, width))
                else:
                    tmp = s_wk[:, tmp_slot * T:tmp_slot * T + width]
                    v(lambda e, s_=src_t, tmp=tmp: e.tensor_tensor(
                        tmp, s_, s_stats[:, w0:w0 + width], OP.subtract),
                      [srcb] + st, WKB(tmp_slot))
                    v(lambda e, tmp=tmp, d=dst_t: e.tensor_tensor(
                        d, tmp, s_stats[:, T + w0:T + w0 + width], OP.mult),
                      WKB(tmp_slot) + st, BFS(d0c, width))

        XH = 0
        UZST = 16384
        HXH = 30720

        ln_stats(s_res, buf("x"), DT)
        for c in range(4):
            norm_chunk(s_res, buf("x"), XH, DT, c, 512, dst_stride=T, dst_coff=c * 512)
        dbg_tap("stats", s_stats[:], [buf("st%d" % i) for i in range(8)])
        dbg_tap("xhat", s_bf[:, XH:XH + 16384], BFS(XH, 16384), dt=BF16)


        convin = s_wk[:, 0:T + 8]
        cacc = WK(2)
        CVB = WKB(0)
        CAB = WKB(2)

        def conv_tile_m1(m, wap, wmb):
            v(lambda e: e.memset(convin[:, 0:4], 0.0), (), CVB)
            v(lambda e, m=m: e.tensor_copy(s_hx3[:, 48 + 3 * m:51 + 3 * m], convin[:, 4:7]),
              CVB, [buf("hx3")])
            v(lambda e, m=m: e.tensor_copy(s_hx3[:, 3 * m:3 * m + 3], convin[:, 4 + T - 3:4 + T]),
              CVB, [buf("hx3")])
            for k in range(4):
                src = convin[:, k + 1:k + 1 + T]
                tap = s_small[:, 4 * m + k:4 * m + k + 1]
                if k == 0:
                    v(lambda e, src=src, tap=tap: e.tensor_single_scalar(cacc, src, tap, OP.mult),
                      CVB + [buf("cw0")], CAB)
                else:
                    v(lambda e, src=src, tap=tap: e.scalar_tensor_tensor(
                        cacc, src, tap, cacc, OP.mult, OP.add),
                      CVB + [buf("cw0")] + CAB, CAB)
            if m == 0 and taps:
                dbg_tap("convin0", convin, CVB)
            ub = s_bf[:, UZST + (m % 2) * 2048:UZST + (m % 2) * 2048 + T]
            sc(lambda e, ub=ub, m=m: e.activation(ub, cacc, AF.Silu,
                                                  bias=s_small[:, 64 + m:65 + m]),
               CAB + [buf("cb0")], BFS(UZST + (m % 2) * 2048, T))
            dmav(d_u[m * 128:(m + 1) * 128, :], ub, BFS(UZST + (m % 2) * 2048, T), [buf("d_u")])

        def mamba1_inproj():
            for m in range(32):
                wmap = tile_kmajor(w_in[:, m * 128:(m + 1) * 128])
                stg = s_wst[:, (m % 2) * 1024:(m % 2) * 1024 + 1024]
                dmas(stg, wmap, writes=WSTB((m % 2) * 1024, 1024))
                wap = s_wsm[:, (m % 2) * 1024:(m % 2) * 1024 + 1024]
                wmb = buf(f"wsm{m % 2}")
                v(lambda e, wap=wap, stg=stg: e.tensor_copy(wap, stg),
                  WSTB((m % 2) * 1024, 1024), [wmb])
                for c in range(4):
                    pp = p_big[:, (c % 2) * 512:(c % 2) * 512 + 512]
                    ppb = PBG((c % 2) * 512, 512)
                    for k in range(8):
                        mm(pp, wap[:, k * 128:(k + 1) * 128],
                           s_bf[:, XH + k * T + c * 512:XH + k * T + (c + 1) * 512],
                           k == 0, k == 7, [wmb] + BFS(XH + k * T + c * 512, 512), ppb)
                    if m < 16:
                        dst = convin[:, 4 + c * 512:4 + (c + 1) * 512]
                        sc(lambda e, dst=dst, pp=pp: e.copy(dst, pp), ppb, CVB)
                    else:
                        zb = s_bf[:, UZST + 2048 + (c % 2) * 512:UZST + 2048 + (c % 2) * 512 + 512]
                        sc(lambda e, zb=zb, pp=pp: e.copy(zb, pp), ppb,
                           BFS(UZST + 2048 + (c % 2) * 512, 512))
                        dmav(d_z[(m - 16) * 128:(m - 15) * 128, c * 512:(c + 1) * 512], zb,
                             BFS(UZST + 2048 + (c % 2) * 512, 512), [buf("d_z")])
                if m < 16:
                    conv_tile_m1(m, wap, wmb)
        mamba1_inproj()

        dmag(aghx_in[:, 0:48], s_hx3[:, 0:48], [buf("hx3")], [buf("aghxi")])
        S.cc("AllGather", OP.bypass, [aghx_in], [aghx_out], [buf("aghxi")], [buf("aghxo")], barrier=(agbar_in, agbar_out))
        HB = 96
        v(lambda e: e.memset(s_hx3[:, HB:HB + 48], 0.0), (), [buf("hx3")])
        for j in range(NCORES - 1):
            agj = s_wst[:, 0:48]
            dmag(agj, aghx_out[j * 128:(j + 1) * 128, 0:48],
                 [buf("aghxo")] + WSTB(0, 48), WSTB(0, 48))
            oc = s_cmask[:, 8 + j:9 + j]
            v(lambda e, agj=agj, oc=oc: e.scalar_tensor_tensor(
                s_hx3[:, HB:HB + 48], agj, oc, s_hx3[:, HB:HB + 48], OP.mult, OP.add),
              WSTB(0, 48) + [buf("cmask"), buf("hx3")], [buf("hx3")])
        for m in range(NE1):
            v(lambda e, m=m: e.tensor_copy(s_hx3[:, 144:147], s_hx3[:, HB + 3 * m:HB + 3 * m + 3]),
              [buf("hx3")], [buf("hx3")])
            v(lambda e, m=m: e.tensor_copy(s_hx3[:, 147:150], s_hx3[:, 48 + 3 * m:51 + 3 * m]),
              [buf("hx3")], [buf("hx3")])
            acc3 = s_hx3[:, 152:155]
            for k in range(4):
                srcw = s_hx3[:, 144 + k:147 + k]
                tapc = s_small[:, 4 * m + k:4 * m + k + 1]
                if k == 0:
                    v(lambda e, srcw=srcw, tapc=tapc: e.tensor_single_scalar(
                        acc3, srcw, tapc, OP.mult),
                      [buf("hx3"), buf("cw0")], [buf("hx3")])
                else:
                    v(lambda e, srcw=srcw, tapc=tapc: e.scalar_tensor_tensor(
                        acc3, srcw, tapc, acc3, OP.mult, OP.add),
                      [buf("hx3"), buf("cw0")], [buf("hx3")])
            u3 = s_hx3[:, 156:159]
            sc(lambda e, m=m, u3=u3: e.activation(u3, acc3, AF.Silu,
                                                  bias=s_small[:, 64 + m:65 + m]),
               [buf("hx3"), buf("cb0")], [buf("hx3")])
            ub3 = s_bf[:, UZST:UZST + 4]
            v(lambda e, u3=u3, ub3=ub3: e.tensor_copy(ub3[:, 0:3], u3),
              [buf("hx3")], BFS(UZST, 4))
            dmav(d_u[m * 128:(m + 1) * 128, 0:3], ub3[:, 0:3], BFS(UZST, 4), [buf("d_u")])

        dmas(s_wst[:, 0:NE1 * 96], tile_kmajor(w_xp),
             writes=WSTB(0, NE1 * 96))
        wxp_b = s_wsm[:, 0:NE1 * 96]
        v(lambda e: e.tensor_copy(wxp_b, s_wst[:, 0:NE1 * 96]),
          WSTB(0, NE1 * 96), [buf("wsm0"), buf("wsm1")])
        for c in range(4):
            for k in range(NE1):
                ub = s_bf[:, UZST + (k % 2) * 2048:UZST + (k % 2) * 2048 + 512]
                dmas(ub, d_u[k * 128:(k + 1) * 128, c * 512:(c + 1) * 512],
                     [buf("d_u")], BFS(UZST + (k % 2) * 2048, 512))
                mm(p_big[0:96, 0:512], wxp_b[:, k * 96:(k + 1) * 96], ub,
                   k == 0, k == NE1 - 1,
                   [buf("wsm0"), buf("wsm1")] + BFS(UZST + (k % 2) * 2048, 512), PBG(0, 512))
            sc(lambda e, c=c: e.copy(s_xdbl[:, c * 512:(c + 1) * 512], p_big[0:96, 0:512]),
               PBG(0, 512), [buf("xdbl")])
        dbg_tap("xdbl", s_xdbl[:], [buf("xdbl")])
        dmav(d_bc1, s_xdbl[64:96, :], [buf("xdbl")], [buf("d_bc1")])
        dmav(tile_kmajor(d_xpark), s_res[:], [buf("x")], [buf("d_xp")])

        def delta_tile(i, dst, dstb):
            wdtt = s_wst[:, 1024:1152]
            dmas(wdtt[0:64, :], w_dt[:, i * 128:(i + 1) * 128], writes=WSTB(1024, 128))
            for c in range(4):
                te(lambda e, c=c, wdtt=wdtt: e.matmul(
                    pb0[:], wdtt[0:64, :],
                    s_xdbl[0:64, c * 512:(c + 1) * 512], start=True, stop=True),
                   WSTB(1024, 128) + [buf("xdbl")], [buf("pb0")])
                ch = dst[:, c * 512:(c + 1) * 512]
                ab = s_stats[:, 1536:2048]
                sp = STB(1536, 512)
                sc(lambda e, ch=ch, i=i: e.activation(ch, pb0[:], AF.Relu,
                                                      bias=s_small[:, 80 + i:81 + i]),
                   [buf("pb0"), buf("dtb0")], list(dstb))
                sc(lambda e, ab=ab, i=i: e.activation(ab, pb0[:], AF.Abs,
                                                      bias=s_small[:, 80 + i:81 + i]),
                   [buf("pb0"), buf("dtb0")], sp)
                sc(lambda e, ab=ab: e.activation(ab, ab, AF.Exp, scale=CC(4)), sp + [buf("cf")], sp)
                sc(lambda e, ab=ab: e.activation(ab, ab, AF.Ln, bias=CC(1)), sp + [buf("cf")], sp)
                v(lambda e, ch=ch, ab=ab: e.tensor_tensor(ch, ch, ab, OP.add),
                  sp + list(dstb), list(dstb))

        UB, BB, CB, DBU, TMP = 16384, 18432, 20480, 22528, 24576

        def mamba1_scan():
            v(lambda e: e.memset(s_hend[:], 0.0), (), [buf("hend")])
            for i in range(NE1):
                dl = WK(0)
                delta_tile(i, dl, WKB(0))
                v(lambda e, i=i: e.tensor_reduce(s_hend[:, 256 + i:257 + i], dl, AX, OP.add),
                  WKB(0), [buf("hend")])
                ub = s_bf[:, UB:UB + T]
                dmas(ub, d_u[i * 128:(i + 1) * 128, :], [buf("d_u")], BFS(UB, T))
                du = WK(1)
                v(lambda e, du=du, dl=dl, ub=ub: e.tensor_tensor(du, dl, ub, OP.mult),
                  WKB(0) + BFS(UB, T), WKB(1))
                yt = WK(2)
                for n in range(NS1):
                    bb = s_bf[:, BB:BB + T]
                    cb_ = s_bf[:, CB:CB + T]
                    dmas(bb, d_bc1[n:n + 1, :].broadcast_to((128, T)), [buf("d_bc1")], BFS(BB, T))
                    dmas(cb_, d_bc1[16 + n:17 + n, :].broadcast_to((128, T)), [buf("d_bc1")], BFS(CB, T))
                    sc(lambda e, dl=dl, i=i, n=n: e.activation(
                        p_big[:], dl, AF.Exp, scale=s_A[:, 16 * i + n:16 * i + n + 1]),
                       WKB(0) + [buf("A")], PBG(0, 2048))
                    dbu = s_bf[:, DBU:DBU + T]
                    v(lambda e, dbu=dbu, du=du, bb=bb: e.tensor_tensor(dbu, du, bb, OP.mult),
                      WKB(1) + BFS(BB, T), BFS(DBU, T))
                    h = WK(3)
                    v(lambda e, h=h, dbu=dbu: e.tensor_tensor_scan(
                        h, p_big[:], dbu, 0.0, OP.mult, OP.add),
                      PBG(0, 2048) + BFS(DBU, T), WKB(3))
                    if i == 0 and n == 0 and taps:
                        dbg_tap("h00", h, WKB(3))
                        dbg_tap("dbu00", s_bf[:, DBU:DBU + T], BFS(DBU, T), dt=BF16)
                    sc(lambda e, i=i, n=n, h=h: e.copy(
                        s_hend[:, 16 * i + n:16 * i + n + 1], h[:, T - 1:T]),
                       WKB(3), [buf("hend")])
                    if i == 0 and n == 0 and taps:
                        dbg_tap("hend00", s_hend[:, 0:16], [buf("hend")])
                    if n == 0:
                        v(lambda e, yt=yt, h=h, cb_=cb_: e.tensor_tensor(yt, h, cb_, OP.mult),
                          WKB(3) + BFS(CB, T), WKB(2))
                    else:
                        tmp = s_bf[:, TMP:TMP + T]
                        v(lambda e, tmp=tmp, h=h, cb_=cb_: e.tensor_tensor(tmp, h, cb_, OP.mult),
                          WKB(3) + BFS(CB, T), BFS(TMP, T))
                        v(lambda e, yt=yt, tmp=tmp: e.tensor_tensor(yt, yt, tmp, OP.add),
                          BFS(TMP, T) + WKB(2), WKB(2))
                dmav(d_y1[i * 128:(i + 1) * 128, :], yt, WKB(2), [buf("d_y1")])
        mamba1_scan()
        dbg_tap("u", d_u, [buf("d_u")], dt=BF16, shape=[E1, T])
        dbg_tap("z", d_z, [buf("d_z")], dt=BF16, shape=[E1, T])
        dbg_tap("y1", d_y1, [buf("d_y1")], shape=[E1, T])
        dbg_tap("hend", s_hend[:], [buf("hend")])

        dmag(ag1_in, s_hend[:], [buf("hend")], [buf("ag1i")])
        S.cc("AllGather", OP.bypass, [ag1_in], [ag1_out], [buf("ag1i")], [buf("ag1o")], barrier=(agbar_in, agbar_out))

        def mamba1_prefix():
            v(lambda e: e.memset(s_h0[:], 0.0), (), [buf("h0")])
            Tj = s_stats[:, 0:256]
            tmp = s_stats[:, 256:512]
            tb = STB(0, 512)
            for j in range(NCORES - 1):
                agj = s_wst[:, 0:272]
                dmag(agj, ag1_out[j * 128:(j + 1) * 128, :],
                     [buf("ag1o")] + WSTB(0, 272), WSTB(0, 272))
                for i in range(NE1):
                    v(lambda e, i=i, agj=agj: e.tensor_single_scalar(
                        Tj[:, 16 * i:16 * i + 16], s_A[:, 16 * i:16 * i + 16],
                        agj[:, 256 + i:257 + i], OP.mult),
                      [buf("A")] + WSTB(0, 272), tb)
                sc(lambda e: e.activation(Tj, Tj, AF.Exp), tb, tb)
                mcol = s_cmask[:, j:j + 1]
                v(lambda e, mcol=mcol: e.tensor_scalar(Tj, Tj, 1.0, mcol, OP.subtract, OP.mult),
                  tb + [buf("cmask")], tb)
                v(lambda e: e.tensor_single_scalar(Tj, Tj, 1.0, OP.add), tb, tb)
                v(lambda e: e.tensor_tensor(s_h0[:], s_h0[:], Tj, OP.mult),
                  tb + [buf("h0")], [buf("h0")])
                v(lambda e, mcol=mcol, agj=agj: e.scalar_tensor_tensor(
                    tmp, agj[:, 0:256], mcol, s_h0[:], OP.mult, OP.add),
                  WSTB(0, 272) + [buf("cmask"), buf("h0")] + tb, tb)
                v(lambda e: e.tensor_copy(s_h0[:], tmp), tb, [buf("h0")])
        mamba1_prefix()
        dbg_tap("h0", s_h0[:], [buf("h0")])
        dbg_tap("ag1o", ag1_out, [buf("ag1o")], shape=[NCORES * 128, 272])

        def mamba1_fix():
            for i in range(NE1):
                dl = WK(0)
                delta_tile(i, dl, WKB(0))
                onesw = WK(1)
                v(lambda e: e.memset(onesw, 1.0), (), WKB(1))
                Sw = WK(3)
                v(lambda e, Sw=Sw, dl=dl: e.tensor_tensor_scan(
                    Sw, onesw, dl, 0.0, OP.mult, OP.add),
                  WKB(0) + WKB(1), WKB(3))
                yt = WK(2)
                dmas(yt, d_y1[i * 128:(i + 1) * 128, :], [buf("d_y1")], WKB(2))
                for n in range(NS1):
                    cb_ = s_bf[:, CB:CB + T]
                    dmas(cb_, d_bc1[16 + n:17 + n, :].broadcast_to((128, T)),
                         [buf("d_bc1")], BFS(CB, T))
                    sc(lambda e, Sw=Sw, i=i, n=n: e.activation(
                        p_big[:], Sw, AF.Exp, scale=s_A[:, 16 * i + n:16 * i + n + 1]),
                       WKB(3) + [buf("A")], PBG(0, 2048))
                    tmp = s_bf[:, TMP:TMP + T]
                    v(lambda e, tmp=tmp, cb_=cb_, i=i, n=n: e.scalar_tensor_tensor(
                        tmp, p_big[:], s_h0[:, 16 * i + n:16 * i + n + 1], cb_,
                        OP.mult, OP.mult),
                      PBG(0, 2048) + [buf("h0")] + BFS(CB, T), BFS(TMP, T))
                    v(lambda e, yt=yt, tmp=tmp: e.tensor_tensor(yt, yt, tmp, OP.add),
                      BFS(TMP, T) + WKB(2), WKB(2))
                ub = s_bf[:, UB:UB + T]
                dmas(ub, d_u[i * 128:(i + 1) * 128, :], [buf("d_u")], BFS(UB, T))
                v(lambda e, yt=yt, ub=ub, i=i: e.scalar_tensor_tensor(
                    yt, ub, s_small[:, 96 + i:97 + i], yt, OP.mult, OP.add),
                  BFS(UB, T) + [buf("dsk0")] + WKB(2), WKB(2))
                zb = s_bf[:, BB:BB + T]
                dmas(zb, d_z[i * 128:(i + 1) * 128, :], [buf("d_z")], BFS(BB, T))
                sz = WK(1)
                sc(lambda e, sz=sz, zb=zb: e.activation(sz, zb, AF.Silu),
                   BFS(BB, T), WKB(1))
                yhb = s_bf[:, DBU:DBU + T]
                v(lambda e, yhb=yhb, yt=yt, sz=sz: e.tensor_tensor(yhb, yt, sz, OP.mult),
                  WKB(1) + WKB(2), BFS(DBU, T))
                dmav(d_yhat[i * 128:(i + 1) * 128, :], yhb, BFS(DBU, T), [buf("d_yh")])
        mamba1_fix()

        def mamba1_outproj():
            for i in range(NE1):
                stg = s_wst[:, (i % 2) * 1024:(i % 2) * 1024 + D]
                dmas(stg, w_out0[i * 128:(i + 1) * 128, :], writes=WSTB((i % 2) * 1024, D))
                dst = s_bf[:, i * D:(i + 1) * D]
                v(lambda e, dst=dst, stg=stg: e.tensor_copy(dst, stg),
                  WSTB((i % 2) * 1024, D), BFS(i * D, D))
            woutb = [buf(f"bf{j}") for j in range(8)]
            for c in range(T // 512):
                rh = s_bf[:, 16384 + (c % 2) * 8192:16384 + (c % 2) * 8192 + NE1 * 512]
                rhb = BFS(16384 + (c % 2) * 8192, 8192)
                for i in range(NE1):
                    dmas(rh[:, i * 512:(i + 1) * 512],
                         d_yhat[i * 128:(i + 1) * 128, c * 512:(c + 1) * 512],
                         [buf("d_yh")], [rhb[(i * 512) // 2048]])
                for dmt in range(8):
                    pp = p_big[:, (dmt % 4) * 512:(dmt % 4) * 512 + 512]
                    ppb = PBG((dmt % 4) * 512, 512)
                    for i in range(NE1):
                        mm(pp, s_bf[:, i * D + dmt * 128:i * D + (dmt + 1) * 128],
                           rh[:, i * 512:(i + 1) * 512], i == 0, i == NE1 - 1,
                           woutb + rhb, ppb)
                    xt = s_res[:, dmt * T + c * 512:dmt * T + (c + 1) * 512]
                    v(lambda e, xt=xt, pp=pp: e.tensor_tensor(xt, xt, pp, OP.add),
                      ppb + [buf("x")], [buf("x")])
        mamba1_outproj()
        dbg_tap("yhat", d_yhat, [buf("d_yh")], dt=BF16, shape=[E1, T])
        dbg_tap("xattn", s_res[:], [buf("x")])

        XC = 0
        H1 = 4096
        R1 = 20480
        R2 = 22528
        BFL = 23040
        BPL = 27136
        AFB = 28160
        APR = 28288
        WQS = 30720

        def qat_scale(tiles, scol):
            qb = [buf("qs")]
            for j, ap_ in enumerate(tiles):
                stg = s_wst[:, (j % 2) * 1024:(j % 2) * 1024 + 1024]
                dmas(stg, ap_, writes=WSTB((j % 2) * 1024, 1024))
                v(lambda e, stg=stg, j=j: e.tensor_reduce(
                    s_qs[:, j:j + 1], stg, AX, OP.max, apply_absolute_value=True),
                  WSTB((j % 2) * 1024, 1024), qb)
            v(lambda e: e.tensor_reduce(s_qs[:, 32:33], s_qs[:, 0:32], AX, OP.max), qb, qb)
            dmav(d_row[0:1, 0:128], s_qs[:, 32:33], qb, [buf("drow")])
            dmas(s_qs[0:1, 40:168], d_row[0:1, 0:128], [buf("drow")], qb)
            v(lambda e: e.tensor_reduce(s_qs[0:1, 33:34], s_qs[0:1, 40:168], AX, OP.max), qb, qb)
            gmsrc = s_qs[0:1, 33:34]
            gmrep = type(gmsrc)(gmsrc.tensor, gmsrc.offset,
                                [list(gmsrc.ap[0]), [0, 8], list(gmsrc.ap[1])])
            dmav(d_row[1:2, 0:8], gmrep, qb, [buf("drow")])
            dmas(s_qs[:, 48:56], d_row[1:2, 0:8].broadcast_to((128, 8)),
                 [buf("drow")], qb)
            sc(lambda e: e.activation(s_qs[:, 56:64], s_qs[:, 48:56], AF.Identity,
                                      scale=CC(7)), qb + [buf("cf")], qb)
            sc(lambda e: e.activation(s_qs[:, 64:72], s_qs[:, 56:64], AF.Ln), qb, qb)
            sc(lambda e: e.activation(s_qs[:, 64:72], s_qs[:, 64:72], AF.Exp,
                                      scale=CC(4)), qb + [buf("cf")], qb)
            v(lambda e: e.tensor_tensor(s_qs[:, 72:80], s_qs[:, 56:64],
                                        s_qs[:, 64:72], OP.mult), qb, qb)
            v(lambda e: e.tensor_scalar(s_qs[:, 72:80], s_qs[:, 72:80], 2.0, -1.0,
                                        OP.subtract, OP.mult), qb, qb)
            v(lambda e: e.tensor_tensor(s_qs[:, 64:72], s_qs[:, 64:72],
                                        s_qs[:, 72:80], OP.mult), qb, qb)
            sc(lambda e: e.copy(s_small2[:, scol:scol + 1], s_qs[:, 56:57]), qb, [buf("qsc")])
            sc(lambda e: e.copy(s_small2[:, scol + 1:scol + 2], s_qs[:, 64:65]), qb, [buf("qsc")])
            dbg_tap("qsdump%d" % scol, s_qs[0:1, 32:80], qb)

        def ffn():
            wfc_r = tile_kmajor(w_fc)
            wpr_r = tile_kmajor(w_proj)
            qat_scale([wpr_r[:, j:j + 1, :] for j in range(32)], 2)
            qat_scale([wfc_r[:, j // 4:j // 4 + 1, (j % 4) * 1024:(j % 4 + 1) * 1024] for j in range(32)], 0)
            dfc = d_wq[0:D, 0:FF]
            dpr = d_wq[D:D + FF, 0:D]
            srcs = []
            dsts = []
            for j in range(8):
                for ccc in range(4):
                    srcs.append(wfc_r[:, j:j + 1, ccc * 1024:(ccc + 1) * 1024])
                    dsts.append((dfc[j * 128:(j + 1) * 128, ccc * 1024:(ccc + 1) * 1024]))
            for j, (sp, dp) in enumerate(zip(srcs, dsts)):
                stg = s_wst[:, (j % 2) * 1024:(j % 2) * 1024 + 1024]
                dmas(stg, sp, writes=WSTB((j % 2) * 1024, 1024))
                q1 = s_stats[:, 0:1024]
                sq = STB(0, 1024)
                sc(lambda e, stg=stg: e.activation(
                    q1, stg, AF.Identity, scale=s_small2[:, 1:2], bias=CC(2)),
                   WSTB((j % 2) * 1024, 1024) + [buf("qsc")], sq)
                sc(lambda e: e.activation(q1, q1, AF.Identity, bias=CC(3)), sq, sq)
                v(lambda e: e.tensor_scalar(q1, q1, -128.0, 127.0, OP.max, OP.min), sq, sq)
                wqt = s_bf[:, WQS:WQS + 1024]
                v(lambda e, wqt=wqt: e.tensor_single_scalar(
                    wqt, q1, s_small2[:, 0:1], OP.mult), sq + [buf("qsc")], BFS(WQS, 1024))
                dmav(dp, wqt, BFS(WQS, 1024), [buf("dwfc")])
            for j in range(32):
                sp = wpr_r[:, j:j + 1, :]
                dp = dpr[j * 128:(j + 1) * 128, 0:1024]
                stg = s_wst[:, (j % 2) * 1024:(j % 2) * 1024 + 1024]
                dmas(stg, sp, writes=WSTB((j % 2) * 1024, 1024))
                q1 = s_stats[:, 0:1024]
                sq = STB(0, 1024)
                sc(lambda e, stg=stg: e.activation(
                    q1, stg, AF.Identity, scale=s_small2[:, 3:4], bias=CC(2)),
                   WSTB((j % 2) * 1024, 1024) + [buf("qsc")], sq)
                sc(lambda e: e.activation(q1, q1, AF.Identity, bias=CC(3)), sq, sq)
                v(lambda e: e.tensor_scalar(q1, q1, -128.0, 127.0, OP.max, OP.min), sq, sq)
                wqt = s_bf[:, WQS:WQS + 1024]
                v(lambda e, wqt=wqt: e.tensor_single_scalar(
                    wqt, q1, s_small2[:, 2:3], OP.mult), sq + [buf("qsc")], BFS(WQS, 1024))
                dmav(dp, wqt, BFS(WQS, 1024), [buf("dwpr")])

            dbg_tap("scales", s_small2[:, 0:8], [buf("qsc")])
            dbg_tap("wqfc", d_wq[0:D, 0:FF], [buf("dwfc")], dt=BF16, shape=[D, FF])
            dbg_tap("wqpr", d_wq[D:D + FF, 0:D], [buf("dwpr")], dt=BF16, shape=[FF, D])
            ln_stats(s_res, buf("x"), DT)

            dmas(s_wst[:, 0:128], tile_kmajor(a_fc),
                 writes=WSTB(0, 128))
            v(lambda e: e.tensor_copy(s_bf[:, AFB:AFB + 128], s_wst[:, 0:128]),
              WSTB(0, 128), BFS(AFB, 128))
            dmas(s_wst[:, 0:512], tile_kmajor(a_proj),
                 writes=WSTB(0, 512))
            v(lambda e: e.tensor_copy(s_bf[:, APR:APR + 512], s_wst[:, 0:512]),
              WSTB(0, 512), BFS(APR, 512))
            for hh in range(2):
                dmas(s_wst[0:16, 0:2048], b_fc_l[:, hh * 2048:(hh + 1) * 2048],
                     writes=WSTB(0, 2048))
                v(lambda e, hh=hh: e.tensor_copy(
                    s_bf[0:16, BFL + hh * 2048:BFL + (hh + 1) * 2048], s_wst[0:16, 0:2048]),
                  WSTB(0, 2048), BFS(BFL + hh * 2048, 2048))
            dmas(s_wst[0:16, 0:1024], b_proj_l, writes=WSTB(0, 1024))
            v(lambda e: e.tensor_copy(s_bf[0:16, BPL:BPL + 1024], s_wst[0:16, 0:1024]),
              WSTB(0, 1024), BFS(BPL, 1024))
            dmas(s_small2[:, 16:48], b_fc, writes=[buf("bfc")])
            dmas(s_small2[:, 48:56], b_proj, writes=[buf("bpr")])

            for c in range(4):
                norm_chunk(s_res, buf("x"), XC, DT, c, 512)
                for k in range(8):
                    mm(pb1[0:16, 0:512], s_bf[:, AFB + k * 16:AFB + (k + 1) * 16],
                       s_bf[:, XC + k * 512:XC + (k + 1) * 512], k == 0, k == 7,
                       BFS(AFB, 128) + BFS(XC, 4096), [buf("pb1")])
                sc(lambda e, c=c: e.activation(
                    s_bf[0:16, R1 + c * 512:R1 + (c + 1) * 512], pb1[0:16, 0:512],
                    AF.Identity, scale=s_const[0:16, 6:7]),
                   [buf("pb1"), buf("cf")], BFS(R1 + c * 512, 512))

            for c in range(4):
                norm_chunk(s_res, buf("x"), XC, DT, c, 512)
                for m in range(32):
                    wq = s_bf[:, WQS + (m % 2) * 1024:WQS + (m % 2) * 1024 + 1024]
                    dmas(wq, tile_kmajor(dfc[:, m * 128:(m + 1) * 128]),
                         [buf("dwfc")], BFS(WQS + (m % 2) * 1024, 1024))
                    pp = pb0 if m % 2 == 0 else pb1
                    ppb = [buf("pb0") if m % 2 == 0 else buf("pb1")]
                    for k in range(8):
                        mm(pp[:], wq[:, k * 128:(k + 1) * 128],
                           s_bf[:, XC + k * 512:XC + (k + 1) * 512],
                           k == 0, False,
                           BFS(WQS + (m % 2) * 1024, 1024) + BFS(XC, 4096), ppb)
                    mm(pp[:], s_bf[0:16, BFL + m * 128:BFL + (m + 1) * 128],
                       s_bf[0:16, R1 + c * 512:R1 + (c + 1) * 512], False, True,
                       BFS(BFL, 4096) + BFS(R1 + c * 512, 512), ppb)
                    dst = s_bf[:, H1 + m * 512:H1 + (m + 1) * 512]
                    sc(lambda e, dst=dst, pp=pp, m=m: e.activation(
                        dst, pp[:], AF.Gelu, bias=s_small2[:, 16 + m:17 + m]),
                       ppb + [buf("bfc")], BFS(H1 + m * 512, 512))
                if c == 0 and taps:
                    dbg_tap("h1c0", s_bf[:, H1:H1 + 16384], BFS(H1, 16384), dt=BF16)
                    dbg_tap("r1", s_bf[0:16, R1:R1 + 2048], BFS(R1, 2048), dt=BF16,
                            shape=[16, 2048])
                for k in range(32):
                    mm(pb1[0:16, 0:512], s_bf[:, APR + k * 16:APR + (k + 1) * 16],
                       s_bf[:, H1 + k * 512:H1 + (k + 1) * 512], k == 0, k == 31,
                       BFS(APR, 512) + BFS(H1, 16384), [buf("pb1")])
                sc(lambda e: e.activation(s_bf[0:16, R2:R2 + 512], pb1[0:16, 0:512],
                                          AF.Identity, scale=s_const[0:16, 6:7]),
                   [buf("pb1"), buf("cf")], BFS(R2, 512))
                for half in range(2):
                    for m in range(32):
                        wq = s_bf[:, WQS + (m % 2) * 512:WQS + (m % 2) * 512 + 512]
                        dmas(wq, dpr[m * 128:(m + 1) * 128, half * 512:(half + 1) * 512],
                             [buf("dwpr")], BFS(WQS + (m % 2) * 512, 512))
                        for dmq in range(4):
                            mm(p_big[:, dmq * 512:(dmq + 1) * 512],
                               wq[:, dmq * 128:(dmq + 1) * 128],
                               s_bf[:, H1 + m * 512:H1 + (m + 1) * 512],
                               m == 0, False,
                               BFS(WQS + (m % 2) * 512, 512) + BFS(H1 + m * 512, 512),
                               PBG(dmq * 512, 512))
                    for dmq in range(4):
                        dm = half * 4 + dmq
                        mm(p_big[:, dmq * 512:(dmq + 1) * 512],
                           s_bf[0:16, BPL + dm * 128:BPL + (dm + 1) * 128],
                           s_bf[0:16, R2:R2 + 512], False, True,
                           BFS(BPL, 1024) + BFS(R2, 512), PBG(dmq * 512, 512))
                        xt = s_res[:, dm * T + c * 512:dm * T + (c + 1) * 512]
                        tbx = WK(3)[:, 0:512]
                        sc(lambda e, tbx=tbx, dmq=dmq, dm=dm: e.activation(
                            tbx, p_big[:, dmq * 512:(dmq + 1) * 512], AF.Identity,
                            bias=s_small2[:, 48 + dm:49 + dm]),
                           PBG(dmq * 512, 512) + [buf("bpr")], WKB(3))
                        v(lambda e, xt=xt, tbx=tbx: e.tensor_tensor(xt, xt, tbx, OP.add),
                          WKB(3) + [buf("x")], [buf("x")])

        if upto in ("l0ffn", "m2", "all"):
            ffn()

        XT2, XD2, XINC2, STT2 = 16384, 18432, 20480, 22528
        BCC2, BCT2, M02, MH2, MV2 = 24576, 24704, 24832, 24960, 25088
        DIN2, YT2, DBC2, SQ2 = 25216, 26240, 28288, 30336
        YCH2 = XD2
        Q = 128

        def m2_layer(l):
            lb = 128 * l
            dmas(s_m2p[:], m2small[lb:lb + 128, :], writes=[buf("m2p")])
            sc(lambda e: e.activation(s_m2p[0:32, 112:113], s_m2p[0:32, 109:110], AF.Exp),
               [buf("m2p")], [buf("m2p")])
            v(lambda e: e.tensor_single_scalar(s_m2p[0:32, 112:113], s_m2p[0:32, 112:113],
                                               -1.0, OP.mult), [buf("m2p")], [buf("m2p")])

            ln_stats(s_res, buf("x"), DT, rms=True)
            for c in range(4):
                norm_chunk(s_res, buf("x"), 0, DT, c, 512, rms=True,
                           dst_stride=T, dst_coff=c * 512)
            for i in range(DT):
                sc(lambda e, i=i: e.activation(
                    s_bf[:, i * T:(i + 1) * T], s_bf[:, i * T:(i + 1) * T],
                    AF.Identity, scale=s_m2p[:, 85 + i:86 + i]),
                   BFS(i * T, T) + [buf("m2p")], BFS(i * T, T))

            wl = in_proj1[l * D:(l + 1) * D, :]
            for m in range(34):
                wdt = 128 if m < 33 else 32
                stg = s_wst[:, (m % 2) * 1024:(m % 2) * 1024 + 8 * wdt]
                dmas(stg, tile_kmajor(wl[:, m * 128:m * 128 + wdt]),
                     writes=WSTB((m % 2) * 1024, 1024))
                wap = s_wsm[:, (m % 2) * 1024:(m % 2) * 1024 + 8 * wdt]
                wmb = buf(f"wsm{m % 2}")
                v(lambda e, wap=wap, stg=stg: e.tensor_copy(wap, stg),
                  WSTB((m % 2) * 1024, 1024), [wmb])
                for c in range(4):
                    pp = p_big[:, (c % 2) * 512:(c % 2) * 512 + 512]
                    ppb = PBG((c % 2) * 512, 512)
                    for k in range(8):
                        mm(pp[0:wdt, :] if wdt < 128 else pp,
                           wap[:, k * wdt:(k + 1) * wdt],
                           s_bf[:, k * T + c * 512:k * T + (c + 1) * 512],
                           k == 0, k == 7, [wmb] + BFS(k * T + c * 512, 512), ppb)
                    if m < 16:
                        zb = s_bf[:, SQ2 + (c % 2) * 512:SQ2 + (c % 2) * 512 + 512]
                        sc(lambda e, zb=zb, pp=pp: e.copy(zb, pp), ppb,
                           BFS(SQ2 + (c % 2) * 512, 512))
                        dmav(d_gate[m * 128:(m + 1) * 128, c * 512:(c + 1) * 512], zb,
                             BFS(SQ2 + (c % 2) * 512, 512), [buf("d_gate")])
                    elif m < 33:
                        dst = convin[:, 4 + c * 512:4 + (c + 1) * 512]
                        sc(lambda e, dst=dst, pp=pp: e.copy(dst, pp), ppb, CVB)
                    else:
                        ch_ = s_xdbl[0:32, c * 512:(c + 1) * 512]
                        ab = s_stats[0:32, 1536:2048]
                        sp = STB(1536, 512)
                        sc(lambda e, ch_=ch_, pp=pp: e.activation(
                            ch_, pp[0:32, :], AF.Relu, bias=s_m2p[0:32, 110:111]),
                           ppb + [buf("m2p")], [buf("xdbl")])
                        sc(lambda e, ab=ab, pp=pp: e.activation(
                            ab, pp[0:32, :], AF.Abs, bias=s_m2p[0:32, 110:111]),
                           ppb + [buf("m2p")], sp)
                        sc(lambda e, ab=ab: e.activation(ab, ab, AF.Exp,
                                                          scale=s_const[0:32, 4:5]),
                           sp + [buf("cf")], sp)
                        sc(lambda e, ab=ab: e.activation(ab, ab, AF.Ln,
                                                         bias=s_const[0:32, 1:2]),
                           sp + [buf("cf")], sp)
                        v(lambda e, ch_=ch_, ab=ab: e.tensor_tensor(ch_, ch_, ab, OP.add),
                          sp + [buf("xdbl")], [buf("xdbl")])
                if 16 <= m < 33:
                    i = m - 16
                    v(lambda e: e.memset(convin[:, 0:4], 0.0), (), CVB)
                    v(lambda e, i=i: e.tensor_copy(s_hx3[:, 51 + 3 * i:54 + 3 * i],
                                                   convin[:, 4:7]), CVB, [buf("hx3")])
                    v(lambda e, i=i: e.tensor_copy(s_hx3[:, 3 * i:3 * i + 3],
                                                   convin[:, 4 + T - 3:4 + T]), CVB, [buf("hx3")])
                    for k in range(4):
                        srcw = convin[:, k + 1:k + 1 + T]
                        tapc = s_m2p[:, 4 * i + k:4 * i + k + 1]
                        if k == 0:
                            v(lambda e, srcw=srcw, tapc=tapc: e.tensor_single_scalar(
                                cacc, srcw, tapc, OP.mult), CVB + [buf("m2p")], CAB)
                        else:
                            v(lambda e, srcw=srcw, tapc=tapc: e.scalar_tensor_tensor(
                                cacc, srcw, tapc, cacc, OP.mult, OP.add),
                              CVB + [buf("m2p")] + CAB, CAB)
                    ub = s_bf[:, XD2:XD2 + T]
                    sc(lambda e, ub=ub, i=i: e.activation(ub, cacc, AF.Silu,
                                                          bias=s_m2p[:, 68 + i:69 + i]),
                       CAB + [buf("m2p")], BFS(XD2, T))
                    if i < 16:
                        dmav(d_u[i * 128:(i + 1) * 128, :], ub, BFS(XD2, T), [buf("d_u")])
                    else:
                        dmav(d_z[0:128, :], ub, BFS(XD2, T), [buf("d_z")])

            if l == 0 and taps:
                dbg_tap("dbcB", s_bf[:, DBC2:DBC2 + 512], BFS(DBC2, 2048), dt=BF16)
            dmag(aghx_in[:, 0:51], s_hx3[:, 0:51], [buf("hx3")], [buf("aghxi")])
            S.cc("AllGather", OP.bypass, [aghx_in], [aghx_out],
                 [buf("aghxi")], [buf("aghxo")], barrier=(agbar_in, agbar_out))
            HB = 102
            v(lambda e: e.memset(s_hx3[:, HB:HB + 51], 0.0), (), [buf("hx3")])
            for j in range(NCORES - 1):
                agj = s_wst[:, 0:51]
                dmag(agj, aghx_out[j * 128:(j + 1) * 128, 0:51],
                     [buf("aghxo")] + WSTB(0, 64), WSTB(0, 64))
                oc = s_cmask[:, 8 + j:9 + j]
                v(lambda e, agj=agj, oc=oc: e.scalar_tensor_tensor(
                    s_hx3[:, HB:HB + 51], agj, oc, s_hx3[:, HB:HB + 51], OP.mult, OP.add),
                  WSTB(0, 64) + [buf("cmask"), buf("hx3")], [buf("hx3")])
            for i in range(17):
                v(lambda e, i=i: e.tensor_copy(s_hx3[:, 153:156],
                                               s_hx3[:, HB + 3 * i:HB + 3 * i + 3]),
                  [buf("hx3")], [buf("hx3")])
                v(lambda e, i=i: e.tensor_copy(s_hx3[:, 156:159],
                                               s_hx3[:, 51 + 3 * i:54 + 3 * i]),
                  [buf("hx3")], [buf("hx3")])
                acc3 = s_qs[:, 96:99]
                for k in range(4):
                    srcw = s_hx3[:, 153 + k:156 + k]
                    tapc = s_m2p[:, 4 * i + k:4 * i + k + 1]
                    if k == 0:
                        v(lambda e, srcw=srcw, tapc=tapc: e.tensor_single_scalar(
                            acc3, srcw, tapc, OP.mult), [buf("hx3"), buf("m2p")], [buf("qs")])
                    else:
                        v(lambda e, srcw=srcw, tapc=tapc: e.scalar_tensor_tensor(
                            acc3, srcw, tapc, acc3, OP.mult, OP.add),
                          [buf("hx3"), buf("m2p"), buf("qs")], [buf("qs")])
                u3 = s_qs[:, 100:103]
                sc(lambda e, i=i, u3=u3: e.activation(u3, acc3, AF.Silu,
                                                      bias=s_m2p[:, 68 + i:69 + i]),
                   [buf("qs"), buf("m2p")], [buf("qs")])
                ub3 = s_bf[:, XD2:XD2 + 4]
                v(lambda e, u3=u3, ub3=ub3: e.tensor_copy(ub3[:, 0:3], u3),
                  [buf("qs")], BFS(XD2, 4))
                if i < 16:
                    dmav(d_u[i * 128:(i + 1) * 128, 0:3], ub3[:, 0:3], BFS(XD2, 4), [buf("d_u")])
                else:
                    dmav(d_z[0:128, 0:3], ub3[:, 0:3], BFS(XD2, 4), [buf("d_z")])

            wo = out_proj1[l * I2:(l + 1) * I2, :]
            for i in range(16):
                stg = s_wst[:, (i % 2) * 1024:(i % 2) * 1024 + D]
                dmas(stg, wo[i * 128:(i + 1) * 128, :], writes=WSTB((i % 2) * 1024, D))
                dst = s_bf[:, i * D:(i + 1) * D]
                v(lambda e, dst=dst, stg=stg: e.tensor_copy(dst, stg),
                  WSTB((i % 2) * 1024, D), BFS(i * D, D))

            if l == 0 and taps:
                dbg_tap("m2xin", d_u, [buf("d_u")], dt=BF16, shape=[E1, T])
                dbg_tap("m2bc", d_z[0:128, :], [buf("d_z")], dt=BF16, shape=[128, T])
                dbg_tap("m2gate", d_gate, [buf("d_gate")], dt=BF16, shape=[I2, T])
                dbg_tap("m2dt", s_xdbl[0:32, :], [buf("xdbl")], shape=[32, T])
                dbg_tap("m2hn", s_bf[:, 0:16384], [buf(f"bf{j}") for j in range(8)], dt=BF16,
                        shape=[128, 16384])
            dmas(WK(0)[:, 0:2048], m2drow[l:l + 1, :].broadcast_to((128, 2048)),
                 writes=WKB(0))
            v(lambda e: e.tensor_copy(s_bf[:, DBC2:DBC2 + 2048], WK(0)[:, 0:2048]),
              WKB(0), BFS(DBC2, 2048))
            CL0 = 2048
            cl_all = s_stats[0:32, CL0:CL0 + 2048]
            ones32 = s_stats[0:32, 0:128]
            v(lambda e: e.memset(ones32, 1.0), (), STB(0, 128))
            for ch in range(NCH2):
                dtach = s_stats[0:32, 128:256]
                v(lambda e, ch=ch, dtach=dtach: e.tensor_single_scalar(
                    dtach, s_xdbl[0:32, ch * Q:(ch + 1) * Q],
                    s_m2p[0:32, 112:113], OP.mult),
                  [buf("xdbl"), buf("m2p")], STB(128, 128))
                v(lambda e, ch=ch, dtach=dtach: e.tensor_tensor_scan(
                    s_stats[0:32, CL0 + ch * Q:CL0 + (ch + 1) * Q], ones32,
                    dtach, 0.0, OP.mult, OP.add),
                  STB(0, 128) + STB(128, 128), STB(CL0 + ch * Q, Q))

            def load_chunk_commons(ch, need_xhd):
                dmas(s_bf[:, XINC2:XINC2 + 2048],
                     tile_kmajor(d_u[:, ch * Q:(ch + 1) * Q]),
                     [buf("d_u")], BFS(XINC2, 2048))
                dmas(s_bf[:, BCC2:BCC2 + 128], d_z[0:128, ch * Q:(ch + 1) * Q],
                     [buf("d_z")], BFS(BCC2, 128))
                for i in range(16):
                    te(lambda e, i=i: e.transpose(
                        pt_b[:, 0:128], s_bf[:, XINC2 + i * 128:XINC2 + (i + 1) * 128],
                        s_identb[:]),
                       BFS(XINC2 + i * 128, 128) + [buf("identb")], [buf("ptb")])
                    sc(lambda e, i=i: e.copy(s_bf[:, XT2 + i * 128:XT2 + (i + 1) * 128],
                                             pt_b[:, 0:128]),
                       [buf("ptb")], BFS(XT2 + i * 128, 128))
                te(lambda e: e.transpose(pt_b[:, 0:128], s_bf[:, BCC2:BCC2 + 128],
                                         s_identb[:]),
                   BFS(BCC2, 128) + [buf("identb")], [buf("ptb")])
                sc(lambda e: e.copy(s_bf[:, BCT2:BCT2 + 128], pt_b[:, 0:128]),
                   [buf("ptb")], BFS(BCT2, 128))
                dmas(s_bf[0:64, SQ2 + 1536:SQ2 + 1664], s_bf[64:128, BCC2:BCC2 + 128],
                     BFS(BCC2, 128), BFS(SQ2 + 1536, 128))
                clc = s_stats[0:32, 2048 + ch * Q:2048 + (ch + 1) * Q]
                te(lambda e, clc=clc: e.transpose(pt_f[:, 0:32], clc, s_ident[0:32, 0:32]),
                   STB(2048 + ch * Q, Q) + [buf("ident")], [buf("ptf")])
                sc(lambda e: e.copy(WK(3)[:, 0:32], pt_f[:, 0:32]), [buf("ptf")], WKB(3))
                dtc = s_xdbl[0:32, ch * Q:(ch + 1) * Q]
                te(lambda e, dtc=dtc: e.transpose(pt_f[:, 32:64], dtc, s_ident[0:32, 0:32]),
                   [buf("xdbl"), buf("ident")], [buf("ptf")])
                sc(lambda e: e.copy(WK(3)[:, 64:96], pt_f[:, 32:64]), [buf("ptf")], WKB(3))
                if need_xhd:
                    dec = s_stats[0:32, 256:384]
                    sc(lambda e, clc=clc, ch=ch: e.activation(
                        dec, clc, AF.Exp, scale=s_const[0:32, 4:5],
                        bias=s_stats[0:32, 2048 + ch * Q + Q - 1:2048 + ch * Q + Q]),
                       STB(2048 + ch * Q, Q) + [buf("cf")], STB(256, 128))
                    v(lambda e, dtc=dtc: e.tensor_tensor(dec, dec, dtc, OP.mult),
                      [buf("xdbl")] + STB(256, 128), STB(256, 128))
                    te(lambda e: e.transpose(pt_f[:, 64:96], dec, s_ident[0:32, 0:32]),
                       STB(256, 128) + [buf("ident")], [buf("ptf")])
                    sc(lambda e: e.copy(WK(3)[:, 32:64], pt_f[:, 64:96]),
                       [buf("ptf")], WKB(3))
                fold_col = 32 if need_xhd else 64
                for pt in range(16):
                    for hh in range(2):
                        h_ = 2 * pt + hh
                        v(lambda e, pt=pt, hh=hh, h_=h_: e.tensor_single_scalar(
                            s_bf[:, XD2 + pt * 128 + hh * 64:XD2 + pt * 128 + hh * 64 + 64],
                            s_bf[:, XT2 + pt * 128 + hh * 64:XT2 + pt * 128 + hh * 64 + 64],
                            WK(3)[:, fold_col + h_:fold_col + h_ + 1], OP.mult),
                          BFS(XT2 + pt * 128, 128) + WKB(3), BFS(XD2 + pt * 128, 128))

            def acols(ch, off):
                sc(lambda e, ch=ch: e.copy(s_qs[0:32, 104:105],
                                           s_stats[0:32, 2048 + ch * Q + Q - 1:2048 + ch * Q + Q]),
                   STB(2048 + ch * Q, Q), [buf("qs")])
                dmav(d_row[3:4, 0:32], s_qs[0:32, 104:105], [buf("qs")], [buf("drow")])
                base = d_row[3:4, 0:32]
                src_top = _ap3(base, [[0, 64], [2, 16], [1, 1]])
                src_bot = _ap3(base, [[0, 64], [2, 16], [1, 1]], offset_extra=1)
                dmas(s_m2a[0:64, off:off + 16], src_top, [buf("drow")], [buf("m2a")], slow=True)
                dmas(s_m2a[64:128, off:off + 16], src_bot, [buf("drow")], [buf("m2a")], slow=True)
                sc(lambda e, off=off: e.activation(s_m2a[:, off:off + 16],
                                                   s_m2a[:, off:off + 16], AF.Exp),
                   [buf("m2a")], [buf("m2a")])

            Lst = WK(1)
            v(lambda e: e.memset(Lst[:, 0:2048], 0.0), (), WKB(1))
            v(lambda e: e.memset(s_qs[:, 105:106], 0.0), (), [buf("qs")])
            st_park_dst = lambda ch: _ap3(d_m2st[ch * 2048:(ch + 1) * 2048, :],
                                          [[64, 128], [128 * 64, 16], [1, 64]])
            for ch in range(NCH2):
                load_chunk_commons(ch, need_xhd=True)
                acols(ch, 0)
                dmav(st_park_dst(ch),
                     _ap3(Lst[:, 0:1024], [list(Lst.ap[0]), [64, 16], [1, 64]]),
                     WKB(1), [buf("d_m2st")])
                sc(lambda e, ch=ch: e.add(s_qs[0:32, 105:106], s_qs[0:32, 105:106],
                                          s_stats[0:32, 2048 + ch * Q + Q - 1:2048 + ch * Q + Q]),
                   STB(2048 + ch * Q, Q) + [buf("qs")], [buf("qs")])
                for pt in range(16):
                    pp = pb0 if pt % 2 == 0 else pb1
                    ppb = [buf("pb0") if pt % 2 == 0 else buf("pb1")]
                    mm(pp[:, 0:64], s_bf[:, XD2 + pt * 128:XD2 + (pt + 1) * 128],
                       s_bf[:, BCT2:BCT2 + 64], True, True,
                       BFS(XD2 + pt * 128, 128) + BFS(BCT2, 128), ppb)
                    v(lambda e, pt=pt, pp=pp, ch=ch: e.scalar_tensor_tensor(
                        Lst[:, pt * 64:(pt + 1) * 64], Lst[:, pt * 64:(pt + 1) * 64],
                        s_m2a[:, pt:pt + 1],
                        pp[:, 0:64], OP.mult, OP.add),
                      ppb + [buf("m2a")] + WKB(1), WKB(1))
            if l == 0 and taps:
                dbg_tap("m2cl", s_stats[0:32, 2048:4096], STB(2048, 2048), shape=[32, 2048])
                dbg_tap("m2Lend", Lst[:, 0:1024], WKB(1), shape=[128, 1024])
            if l == 0 and taps:
                dbg_tap("dbcC", s_bf[:, DBC2:DBC2 + 512], BFS(DBC2, 2048), dt=BF16)
            dmav(ag2_in[l][:, 0:1024], Lst[:, 0:1024], WKB(1), [buf("ag2i")])
            dmav(ag2_in[l][0:32, 1024:1025], s_qs[0:32, 105:106], [buf("qs")], [buf("ag2i")],
                 slow=True)
            S.cc("AllGather", OP.bypass, [ag2_in[l]], [ag2_out[l]],
                 [buf("ag2i")], [buf("ag2o")], barrier=(agbar_in, agbar_out))
            for j in range(NCORES - 1):
                stj = s_stats[:, 0:1024]
                dmag(stj, ag2_out[l][j * 128:(j + 1) * 128, 0:1024],
                     [buf("ag2o")] + STB(0, 1024), STB(0, 1024))
                basej = ag2_out[l][j * 128:j * 128 + 1, 1024:1025]
                srcjt = _ap3(basej, [[0, 64], [2 * 1056, 16], [1, 1]])
                srcjb = _ap3(basej, [[0, 64], [2 * 1056, 16], [1, 1]], offset_extra=1056)
                dmag(s_qs[0:64, 112:128], srcjt, [buf("ag2o")], [buf("qs")], slow=True)
                dmag(s_qs[64:128, 112:128], srcjb, [buf("ag2o")], [buf("qs")], slow=True)
                sc(lambda e: e.activation(s_qs[:, 112:128], s_qs[:, 112:128], AF.Exp),
                   [buf("qs")], [buf("qs")])
                mcol = s_cmask[:, j:j + 1]
                v(lambda e, mcol=mcol: e.tensor_scalar(
                    s_qs[:, 112:128], s_qs[:, 112:128], 1.0, mcol, OP.subtract, OP.mult),
                  [buf("qs"), buf("cmask")], [buf("qs")])
                v(lambda e: e.tensor_single_scalar(s_qs[:, 112:128], s_qs[:, 112:128],
                                                   1.0, OP.add), [buf("qs")], [buf("qs")])
                for i in range(16):
                    v(lambda e, i=i: e.tensor_single_scalar(
                        Lst[:, 1024 + i * 64:1024 + (i + 1) * 64],
                        Lst[:, 1024 + i * 64:1024 + (i + 1) * 64],
                        s_qs[:, 112 + i:113 + i], OP.mult),
                      [buf("qs")] + WKB(1), WKB(1))
                    v(lambda e, i=i, stj=stj, mcol=mcol: e.scalar_tensor_tensor(
                        Lst[:, 1024 + i * 64:1024 + (i + 1) * 64],
                        stj[:, i * 64:(i + 1) * 64], mcol,
                        Lst[:, 1024 + i * 64:1024 + (i + 1) * 64], OP.mult, OP.add),
                      STB(0, 1024) + [buf("cmask")] + WKB(1), WKB(1))

            rst = WK(2)
            v(lambda e: e.tensor_copy(rst[:, 0:1024], Lst[:, 1024:2048]), WKB(1), WKB(2))
            for ch in range(NCH2):
                load_chunk_commons(ch, need_xhd=False)
                if ch > 0:
                    acols(ch - 1, 16)
                    for i in range(16):
                        v(lambda e, i=i: e.tensor_single_scalar(
                            rst[:, i * 64:(i + 1) * 64], rst[:, i * 64:(i + 1) * 64],
                            s_m2a[:, 16 + i:17 + i], OP.mult),
                          [buf("m2a")] + WKB(2), WKB(2))
                stl = s_stats[:, 0:1024]
                dmas(stl, st_park_dst(ch), [buf("d_m2st")] + STB(0, 1024), STB(0, 1024))
                tc = WK(0)
                v(lambda e, stl=stl: e.tensor_tensor(tc[:, 0:1024], stl, rst[:, 0:1024],
                                                     OP.add),
                  STB(0, 1024) + WKB(2), WKB(0))
                for pt in range(16):
                    v(lambda e, pt=pt: e.tensor_copy(
                        s_bf[:, SQ2 + 1024 + 128 * (pt % 2):SQ2 + 1024 + 128 * (pt % 2) + 64],
                        tc[:, pt * 64:(pt + 1) * 64]),
                      WKB(0), BFS(SQ2 + 1024 + 128 * (pt % 2), 128))
                    te(lambda e, pt=pt: e.transpose(
                        pt_b[0:64, 128:256],
                        s_bf[:, SQ2 + 1024 + 128 * (pt % 2):SQ2 + 1024 + 128 * (pt % 2) + 64],
                        s_identb[:]),
                       BFS(SQ2 + 1024 + 128 * (pt % 2), 128) + [buf("identb")], [buf("ptb")])
                    sc(lambda e, pt=pt: e.copy(
                        s_bf[0:64, STT2 + pt * 128:STT2 + (pt + 1) * 128],
                        pt_b[0:64, 128:256]),
                       [buf("ptb")], BFS(STT2 + pt * 128, 128))
                mm(pt_f[:, 128:256], s_bf[0:64, BCC2:BCC2 + 128],
                   s_bf[0:64, SQ2 + 1536:SQ2 + 1664], True, True,
                   BFS(BCC2, 128) + BFS(SQ2 + 1536, 128), [buf("ptf")])
                v(lambda e: e.tensor_tensor(s_bf[:, M02:M02 + 128], pt_f[:, 128:256],
                                            s_tri[:], OP.mult),
                  [buf("ptf"), buf("tri")], BFS(M02, 128))
                v(lambda e: e.tensor_single_scalar(WK(3)[:, 96:128], WK(3)[:, 0:32],
                                                   -1.0, OP.mult), WKB(3), WKB(3))
                dmav(d_row[2:3, 0:4096], s_stats[0:32, 2048 + ch * Q:2048 + (ch + 1) * Q],
                     STB(2048 + ch * Q, Q), [buf("drow")])
                for grp in range(4):
                    clbc = WK(0)[:, 1024:2048]
                    srcg = _ap3(d_row[2:3, 0:4096],
                                [[0, 128], [128, 8], [1, 128]], offset_extra=grp * 8 * 128)
                    dmas(clbc, srcg, [buf("drow")] + WKB(0), WKB(0))
                    sc(lambda e, clbc=clbc: e.activation(
                        s_bf[:, DIN2:DIN2 + 1024], clbc, AF.Exp),
                       WKB(0), BFS(DIN2, 1024))
                    for hh in range(8):
                        h_ = grp * 8 + hh
                        pt = h_ // 2
                        argt = WK(3)[:, 128:256]
                        v(lambda e, hh=hh, h_=h_, clbc=clbc, argt=argt: e.tensor_scalar(
                            argt, clbc[:, hh * 128:(hh + 1) * 128],
                            WK(3)[:, 96 + h_:97 + h_], 0.0, OP.add, OP.min),
                          WKB(0) + WKB(3), WKB(3))
                        sc(lambda e, argt=argt: e.activation(
                            s_bf[:, MH2:MH2 + 128], argt, AF.Exp),
                           WKB(3), BFS(MH2, 128))
                        if l == 0 and ch == 0 and h_ == 0 and taps:
                            dbg_tap("m2stT", s_bf[0:64, STT2:STT2 + 2048],
                                    [buf(f"bf{j}") for j in range(STT2 // 2048, (STT2 + 2048) // 2048 + 1)],
                                    dt=BF16, shape=[64, 2048])
                            dbg_tap("m2mv0", s_bf[0:64, MV2:MV2 + 128], BFS(MV2, 128),
                                    dt=BF16, shape=[64, 128])
                            dbg_tap("m2arg0", argt, WKB(3))
                            dbg_tap("m2lp0", s_bf[:, MH2:MH2 + 128], BFS(MH2, 128), dt=BF16)
                            dbg_tap("m2xhat0", s_bf[:, XD2:XD2 + 128], BFS(XD2, 128), dt=BF16)
                            dbg_tap("m2m00", s_bf[:, M02:M02 + 128], BFS(M02, 128), dt=BF16)
                        MH2B = SQ2 + 1792
                        v(lambda e: e.tensor_tensor(s_bf[:, MH2B:MH2B + 128],
                                                    s_bf[:, MH2:MH2 + 128],
                                                    s_bf[:, M02:M02 + 128], OP.mult),
                          BFS(M02, 128) + BFS(MH2, 128), BFS(MH2B, 128))
                        v(lambda e, hh=hh: e.tensor_tensor(
                            s_bf[0:64, MV2:MV2 + 128], s_bf[0:64, SQ2 + 1536:SQ2 + 1664],
                            s_bf[0:64, DIN2 + hh * 128:DIN2 + (hh + 1) * 128], OP.mult),
                          BFS(SQ2 + 1536, 128) + BFS(DIN2, 1024), BFS(MV2, 128))
                        bank = h_ % 4
                        pslot = p_big[:, bank * 512:bank * 512 + 64]
                        ppb = PBG(bank * 512, 512)
                        mm(pslot, s_bf[:, MH2B:MH2B + 128],
                           s_bf[:, XD2 + pt * 128 + (h_ % 2) * 64:XD2 + pt * 128 + (h_ % 2) * 64 + 64],
                           True, TEST_NO_INTER, BFS(MH2B, 128) + BFS(XD2 + pt * 128, 128), ppb)
                        if not TEST_NO_INTER:
                            mm(pslot, s_bf[0:64, MV2:MV2 + 128],
                               s_bf[0:64, STT2 + pt * 128 + (h_ % 2) * 64:STT2 + pt * 128 + (h_ % 2) * 64 + 64],
                               False, True, BFS(MV2, 128) + BFS(STT2 + pt * 128, 128), ppb)
                        v(lambda e, pt=pt, h_=h_: e.tensor_tensor(
                            s_bf[:, SQ2 + 1280:SQ2 + 1344],
                            s_bf[:, XT2 + pt * 128 + (h_ % 2) * 64:XT2 + pt * 128 + (h_ % 2) * 64 + 64],
                            s_bf[:, DBC2 + h_ * 64:DBC2 + (h_ + 1) * 64], OP.mult),
                          BFS(XT2 + pt * 128, 128) + BFS(DBC2, 2048), BFS(SQ2 + 1280, 64))
                        if l == 0 and ch == 0 and h_ == 0 and taps:
                            sc(lambda e: e.copy(s_stats[:, 1152:1216], pslot),
                               ppb, STB(1152, 64))
                            dbg_tap("m2psum0", s_stats[:, 1152:1216], STB(1152, 64))
                            dbg_tap("m2mh0", s_bf[:, SQ2 + 1792:SQ2 + 1920], BFS(SQ2 + 1792, 128), dt=BF16)
                            dbg_tap("m2tmp0", s_bf[:, SQ2 + 1280:SQ2 + 1344], BFS(SQ2 + 1280, 64), dt=BF16)
                            dbg_tap("m2dbc", s_bf[:, DBC2:DBC2 + 128], BFS(DBC2, 2048), dt=BF16)
                        v(lambda e, h_=h_, pslot=pslot: e.tensor_tensor(
                            s_bf[:, YT2 + h_ * 64:YT2 + (h_ + 1) * 64],
                            pslot, s_bf[:, SQ2 + 1280:SQ2 + 1344], OP.add),
                          ppb + BFS(SQ2 + 1280, 64), BFS(YT2 + h_ * 64, 64))
                if l == 0 and ch == 0 and taps:
                    dbg_tap("m2yt0", s_bf[:, YT2:YT2 + 2048], BFS(YT2, 2048), dt=BF16,
                            shape=[128, 2048])
                    dbg_tap("m2tc0", tc[:, 0:1024], WKB(0), shape=[128, 1024])
                for pt in range(16):
                    te(lambda e, pt=pt: e.transpose(
                        pt_b[:, 0:128], s_bf[:, YT2 + pt * 128:YT2 + (pt + 1) * 128],
                        s_identb[:]),
                       BFS(YT2 + pt * 128, 128) + [buf("identb")], [buf("ptb")])
                    sc(lambda e, pt=pt: e.copy(
                        s_bf[:, YCH2 + pt * 128:YCH2 + (pt + 1) * 128], pt_b[:, 0:128]),
                       [buf("ptb")], BFS(YCH2 + pt * 128, 128))
                dmas(s_bf[:, XINC2:XINC2 + 2048],
                     tile_kmajor(d_gate[:, ch * Q:(ch + 1) * Q]),
                     [buf("d_gate")], BFS(XINC2, 2048))
                for i in range(16):
                    sg = s_bf[:, MH2:MH2 + 128]
                    sc(lambda e, i=i, sg=sg: e.activation(
                        sg, s_bf[:, XINC2 + i * 128:XINC2 + (i + 1) * 128], AF.Silu),
                       BFS(XINC2 + i * 128, 128), BFS(MH2, 128))
                    v(lambda e, i=i, sg=sg: e.tensor_tensor(
                        s_stats[:, i * 128:(i + 1) * 128],
                        s_bf[:, YCH2 + i * 128:YCH2 + (i + 1) * 128], sg, OP.mult),
                      BFS(YCH2 + i * 128, 128) + BFS(MH2, 128), STB(i * 128, 128))
                    sqb2 = s_bf[:, SQ2 + 1408:SQ2 + 1536]
                    sc(lambda e, i=i, sqb2=sqb2: e.activation(
                        sqb2, s_stats[:, i * 128:(i + 1) * 128], AF.Square),
                       STB(i * 128, 128), BFS(SQ2 + 1408, 128))
                    mm(pt_f[:, 128:256], s_onesb[:], sqb2, i == 0, i == 15,
                       BFS(SQ2 + 1408, 128) + [buf("onesb")], [buf("ptf")])
                rsr = WK(3)[:, 128:256]
                v(lambda e, rsr=rsr: e.tensor_single_scalar(
                    rsr, pt_f[:, 128:256], 1.0 / I2, OP.mult),
                  [buf("ptf")], WKB(3))
                sc(lambda e, rsr=rsr: e.activation(rsr, rsr, AF.Ln, bias=CC(0)),
                   WKB(3) + [buf("cf")], WKB(3))
                sc(lambda e, rsr=rsr: e.activation(rsr, rsr, AF.Exp, scale=CC(5)),
                   WKB(3) + [buf("cf")], WKB(3))
                for i in range(16):
                    tmpf = WK(3)[:, 0:128]
                    v(lambda e, i=i, tmpf=tmpf, rsr=rsr: e.tensor_tensor(
                        tmpf, s_stats[:, i * 128:(i + 1) * 128], rsr, OP.mult),
                      STB(i * 128, 128) + WKB(3), WKB(3))
                    sc(lambda e, i=i, tmpf=tmpf: e.activation(
                        s_bf[:, XT2 + i * 128:XT2 + (i + 1) * 128], tmpf,
                        AF.Identity, scale=s_m2p[:, 93 + i:94 + i]),
                       WKB(3) + [buf("m2p")], BFS(XT2 + i * 128, 128))
                if l == 0 and ch == 0 and taps:
                    dbg_tap("m2ys0", s_stats[:, 0:2048], STB(0, 2048), shape=[128, 2048])
                    dbg_tap("m2yhat0", s_bf[:, XT2:XT2 + 2048], BFS(XT2, 2048), dt=BF16,
                            shape=[128, 2048])
                for m_ in range(8):
                    pp = pb0 if m_ % 2 == 0 else pb1
                    ppb = [buf("pb0") if m_ % 2 == 0 else buf("pb1")]
                    for k in range(16):
                        mm(pp[:, 0:128], s_bf[:, k * D + m_ * 128:k * D + (m_ + 1) * 128],
                           s_bf[:, XT2 + k * 128:XT2 + (k + 1) * 128],
                           k == 0, k == 15,
                           BFS(k * D, D) + BFS(XT2 + k * 128, 128), ppb)
                    xt_ = s_res[:, m_ * T + ch * Q:m_ * T + (ch + 1) * Q]
                    v(lambda e, xt_=xt_, pp=pp: e.tensor_tensor(xt_, xt_, pp[:, 0:128],
                                                                OP.add),
                      ppb + [buf("x")], [buf("x")])

        def m2_final():
            ln_stats(s_res, buf("x"), DT, rms=True)
            dmas(s_qs[:, 128:136] if False else s_qs[:, 96:104], normf_cols,
                 writes=[buf("qs")])
            for c in range(4):
                for i in range(DT):
                    tmpf = WK(3)[:, 0:512]
                    v(lambda e, i=i, c=c, tmpf=tmpf: e.tensor_tensor(
                        tmpf, s_res[:, i * T + c * 512:i * T + (c + 1) * 512],
                        s_stats[:, T + c * 512:T + (c + 1) * 512], OP.mult),
                      [buf("x")] + STB(2048, 2048), WKB(3))
                    sc(lambda e, i=i, tmpf=tmpf: e.activation(
                        tmpf, tmpf, AF.Identity, scale=s_qs[:, 96 + i:97 + i]),
                       WKB(3) + [buf("qs")], WKB(3))
                    xl = WK(0)[:, 0:512]
                    dmas(xl, d_xpark[i * 128:(i + 1) * 128, c * 512:(c + 1) * 512],
                         [buf("d_xp")] + WKB(0), WKB(0))
                    v(lambda e, i=i, c=c, tmpf=tmpf, xl=xl: e.tensor_tensor(
                        s_res[:, i * T + c * 512:i * T + (c + 1) * 512], xl, tmpf, OP.add),
                      WKB(0) + WKB(3), [buf("x")])

        if upto in ("m2", "all"):
            dmav(d_xpark, tile_kmajor(d_xpark) if False else d_xpark, [buf("x")], [buf("d_xp")]) if False else None
            dmav(tile_kmajor(d_xpark), s_res[:], [buf("x")], [buf("d_xp")])
            ln_stats(s_res, buf("x"), DT)
            for c in range(4):
                for i in range(DT):
                    tmpf = WK(3)[:, 0:512]
                    v(lambda e, i=i, c=c, tmpf=tmpf: e.tensor_tensor(
                        tmpf, s_res[:, i * T + c * 512:i * T + (c + 1) * 512],
                        s_stats[:, c * 512:(c + 1) * 512], OP.subtract),
                      [buf("x")] + STB(0, 4096), WKB(3))
                    v(lambda e, i=i, c=c, tmpf=tmpf: e.tensor_tensor(
                        s_res[:, i * T + c * 512:i * T + (c + 1) * 512], tmpf,
                        s_stats[:, T + c * 512:T + (c + 1) * 512], OP.mult),
                      WKB(3) + STB(0, 4096), [buf("x")])
            dbg_tap("m2h", s_res[:], [buf("x")])
            m2_layer(0)
            dbg_tap("m2hL0", s_res[:], [buf("x")])
            m2_layer(1)
            m2_final()

        def moe():
            for e in range(NEXP):
                for kk in range(8):
                    for hcol in range(2):
                        stg = s_wst[:, 0:1024]
                        dmas(stg, w1e[e, kk * 128:(kk + 1) * 128,
                                      hcol * 1024:(hcol + 1) * 1024],
                             writes=WSTB(0, 1024))
                        wqt = s_bf[:, 0:1024]
                        v(lambda en, stg=stg, wqt=wqt: en.tensor_copy(wqt, stg),
                          WSTB(0, 1024), BFS(0, 1024))
                        dmav(d_wqm[e * 3072 + kk * 128:e * 3072 + (kk + 1) * 128,
                                   hcol * 1024:(hcol + 1) * 1024], wqt,
                             BFS(0, 1024), [buf("dwqm")])
                for kk in range(16):
                    stg = s_wst[:, 0:1024]
                    dmas(stg, w2e[e, kk * 128:(kk + 1) * 128, :], writes=WSTB(0, 1024))
                    wqt = s_bf[:, 0:1024]
                    v(lambda en, stg=stg, wqt=wqt: en.tensor_copy(wqt, stg),
                      WSTB(0, 1024), BFS(0, 1024))
                    for dm in range(8):
                        dmav(d_wqm[e * 3072 + 1024 + dm * 128:e * 3072 + 1024 + (dm + 1) * 128,
                                   kk * 128:(kk + 1) * 128],
                             wqt[:, dm * 128:(dm + 1) * 128],
                             BFS(0, 1024), [buf("dwqm")], slow=True)
            dmas(s_wst[0:8, 0:1024], b2e, writes=WSTB(0, 1024))
            v(lambda en: en.tensor_copy(s_bf[0:8, 30720:31744], s_wst[0:8, 0:1024]),
              WSTB(0, 1024), BFS(30720, 1024))
            dmas(s_m2p[:], b1cols, writes=[buf("m2p")])
            dmas(s_hx3[:, 64:128], tile_kmajor(wg), writes=[buf("hx3")])

            ln_stats(s_res, buf("x"), DT)
            for c in range(4):
                norm_chunk(s_res, buf("x"), 0, DT, c, 512)
                for i in range(DT):
                    dmav(d_u[i * 128:(i + 1) * 128, c * 512:(c + 1) * 512],
                         s_bf[:, i * 512:(i + 1) * 512],
                         BFS(0, 4096), [buf("d_u")])

            for j in range(16):
                te(lambda en, j=j: en.transpose(pt_f[:, 128:256],
                                                s_stats[:, j * 128:(j + 1) * 128],
                                                s_ident[:]),
                   STB(j * 128, 128) + [buf("ident")], [buf("ptf")])
                sc(lambda en, j=j: en.copy(s_hx3[:, 128 + j:129 + j], pt_f[:, 128:129]),
                   [buf("ptf")], [buf("hx3")])
                te(lambda en, j=j: en.transpose(pt_f[:, 128:256],
                                                s_stats[:, T + j * 128:T + (j + 1) * 128],
                                                s_ident[:]),
                   STB(T + j * 128, 128) + [buf("ident")], [buf("ptf")])
                sc(lambda en, j=j: en.copy(s_hx3[:, 144 + j:145 + j], pt_f[:, 128:129]),
                   [buf("ptf")], [buf("hx3")])

            onesf = s_qs[:, 174:175]
            v(lambda en, onesf=onesf: en.memset(onesf, 1.0), (), [buf("qs")])
            for kk in range(8):
                mm(pt_f[0:8, 0:1], s_hx3[:, 64 + kk * 8:72 + kk * 8], onesf,
                   kk == 0, kk == 7, [buf("hx3"), buf("qs")], [buf("ptf")])
            sc(lambda en: en.copy(s_qs[0:8, 175:176], pt_f[0:8, 0:1]),
               [buf("ptf")], [buf("qs")])
            dmav(d_row[4:5, 0:8], s_qs[0:8, 175:176], [buf("qs")], [buf("drow")],
                 slow=True)
            dmas(s_qs[:, 160:168], d_row[4:5, 0:8].broadcast_to((128, 8)),
                 [buf("drow")], [buf("qs")])
            v(lambda en: en.tensor_single_scalar(s_qs[:, 160:168], s_qs[:, 160:168],
                                                 -1.0, OP.mult),
              [buf("qs")], [buf("qs")])

            def Q(k, a, b):
                return s_qs[:, k * 48 + a:k * 48 + b]
            for i in range(8):
                pjs = [(0, 2 * i, pb0), (1, 2 * i + 1, pb1)]
                for k, j, pp in pjs:
                    ppb = [buf("pb0") if k == 0 else buf("pb1")]
                    for kk in range(8):
                        mm(pp[:, 0:8], s_res[:, kk * T + j * 128:kk * T + (j + 1) * 128],
                           s_hx3[:, 64 + kk * 8:72 + kk * 8], kk == 0, kk == 7,
                           [buf("x"), buf("hx3")], ppb)
                def st(fn, reads, writes):
                    for k, j, pp in pjs:
                        fn(k, j, pp, [buf("pb0") if k == 0 else buf("pb1")],
                           reads, writes)
                qb = [buf("qs")]
                st(lambda k, j, pp, ppb, r_, w_: v(
                    lambda en, k=k, j=j, pp=pp: en.scalar_tensor_tensor(
                        Q(k, 0, 8), s_qs[:, 160:168], s_hx3[:, 128 + j:129 + j],
                        pp[:, 0:8], OP.mult, OP.add),
                    qb + [buf("hx3")] + ppb, qb), (), ())
                st(lambda k, j, pp, ppb, r_, w_: v(
                    lambda en, k=k, j=j: en.tensor_single_scalar(
                        Q(k, 0, 8), Q(k, 0, 8), s_hx3[:, 144 + j:145 + j], OP.mult),
                    qb + [buf("hx3")], qb), (), ())
                st(lambda k, j, pp, ppb, r_, w_: v(
                    lambda en, k=k: en.tensor_reduce(Q(k, 32, 33), Q(k, 0, 8),
                                                     AX, OP.max), qb, qb), (), ())
                st(lambda k, j, pp, ppb, r_, w_: v(
                    lambda en, k=k: en.tensor_single_scalar(
                        Q(k, 0, 8), Q(k, 0, 8), Q(k, 32, 33), OP.subtract),
                    qb, qb), (), ())
                st(lambda k, j, pp, ppb, r_, w_: sc(
                    lambda en, k=k: en.activation(Q(k, 0, 8), Q(k, 0, 8), AF.Exp),
                    qb, qb), (), ())
                st(lambda k, j, pp, ppb, r_, w_: v(
                    lambda en, k=k: en.tensor_reduce(Q(k, 33, 34), Q(k, 0, 8),
                                                     AX, OP.add), qb, qb), (), ())
                st(lambda k, j, pp, ppb, r_, w_: sc(
                    lambda en, k=k: en.activation(Q(k, 33, 34), Q(k, 33, 34), AF.Ln),
                    qb, qb), (), ())
                st(lambda k, j, pp, ppb, r_, w_: sc(
                    lambda en, k=k: en.activation(Q(k, 33, 34), Q(k, 33, 34), AF.Exp,
                                                  scale=CC(4)),
                    qb + [buf("cf")], qb), (), ())
                st(lambda k, j, pp, ppb, r_, w_: v(
                    lambda en, k=k: en.tensor_single_scalar(
                        Q(k, 8, 16), Q(k, 0, 8), Q(k, 33, 34), OP.mult),
                    qb, qb), (), ())
                st(lambda k, j, pp, ppb, r_, w_: v(
                    lambda en, k=k: en.tensor_reduce(Q(k, 32, 33), Q(k, 8, 16),
                                                     AX, OP.max), qb, qb), (), ())
                st(lambda k, j, pp, ppb, r_, w_: v(
                    lambda en, k=k: en.tensor_single_scalar(
                        Q(k, 16, 24), Q(k, 8, 16), Q(k, 32, 33), OP.subtract),
                    qb, qb), (), ())
                st(lambda k, j, pp, ppb, r_, w_: v(
                    lambda en, k=k: en.tensor_single_scalar(
                        Q(k, 24, 32), Q(k, 16, 24), 0.0, OP.is_ge), qb, qb), (), ())
                st(lambda k, j, pp, ppb, r_, w_: v(
                    lambda en, k=k: en.scalar_tensor_tensor(
                        Q(k, 16, 24), Q(k, 24, 32), -2.0, Q(k, 8, 16),
                        OP.mult, OP.add), qb, qb), (), ())
                st(lambda k, j, pp, ppb, r_, w_: v(
                    lambda en, k=k: en.tensor_reduce(Q(k, 34, 35), Q(k, 16, 24),
                                                     AX, OP.max), qb, qb), (), ())
                st(lambda k, j, pp, ppb, r_, w_: v(
                    lambda en, k=k: en.tensor_single_scalar(
                        Q(k, 16, 24), Q(k, 16, 24), Q(k, 34, 35), OP.subtract),
                    qb, qb), (), ())
                st(lambda k, j, pp, ppb, r_, w_: v(
                    lambda en, k=k: en.tensor_single_scalar(
                        Q(k, 8, 16), Q(k, 16, 24), 0.0, OP.is_ge), qb, qb), (), ())
                st(lambda k, j, pp, ppb, r_, w_: v(
                    lambda en, k=k: en.tensor_single_scalar(
                        Q(k, 35, 43), Q(k, 24, 32), Q(k, 32, 33), OP.mult),
                    qb, qb), (), ())
                st(lambda k, j, pp, ppb, r_, w_: v(
                    lambda en, k=k: en.scalar_tensor_tensor(
                        Q(k, 35, 43), Q(k, 8, 16), Q(k, 34, 35), Q(k, 35, 43),
                        OP.mult, OP.add), qb, qb), (), ())
                for k, j, pp in pjs:
                    dbg_tap(f"moe_msk{j}", Q(k, 35, 43), [buf("qs")], shape=[128, 8])
                    te(lambda en, k=k: en.transpose(pt_f[0:8, 128:256], Q(k, 35, 43),
                                                    s_ident[:]),
                       qb + [buf("ident")], [buf("ptf")])
                    sc(lambda en, j=j: en.copy(s_stats[0:8, j * 128:(j + 1) * 128],
                                               pt_f[0:8, 128:256]),
                       [buf("ptf")], STB(j * 128, 128))
            dbg_tap("moe_mskf", s_stats[0:8, 0:2048], STB(0, 2048), shape=[8, 2048])
            v(lambda en: en.tensor_copy(s_bf[0:8, 0:2048], s_stats[0:8, 0:2048]),
              STB(0, 2048), BFS(0, 2048))
            dmav(d_bc1[0:8, :], s_bf[0:8, 0:2048], BFS(0, 2048), [buf("d_bc1")])

            for c in range(4):
                for dm in range(8):
                    mm(pb0[:], s_bf[0:8, 30720 + dm * 128:30720 + (dm + 1) * 128],
                       s_bf[0:8, c * 512:(c + 1) * 512], True, True,
                       BFS(30720, 1024) + BFS(0, 2048), [buf("pb0")])
                    yl = WK(0)[:, 0:512]
                    sc(lambda en, yl=yl: en.copy(yl, pb0[:]), [buf("pb0")], WKB(0))
                    dmas(d_y1[dm * 128:(dm + 1) * 128, c * 512:(c + 1) * 512], yl,
                         WKB(0), [buf("d_y1")])

            XCC, W1R, H1M, WST2, MBC = 0, 4096, 20480, 28672, 31744
            for e in range(NEXP):
                for kk in range(8):
                    dmas(s_bf[:, W1R + kk * 2048:W1R + (kk + 1) * 2048],
                         d_wqm[e * 3072 + kk * 128:e * 3072 + (kk + 1) * 128, 0:2048],
                         [buf("dwqm")], BFS(W1R + kk * 2048, 2048))
                for c in range(4):
                    dmas(s_bf[:, XCC:XCC + 4096],
                         tile_kmajor(d_u[0:1024, c * 512:(c + 1) * 512]),
                         [buf("d_u")], BFS(XCC, 4096))
                    dmas(s_bf[:, MBC:MBC + 512],
                         d_bc1[e:e + 1, c * 512:(c + 1) * 512].broadcast_to((128, 512)),
                         [buf("d_bc1")], BFS(MBC, 512))
                    for m_ in range(16):
                        pp = pb0 if m_ % 2 == 0 else pb1
                        ppb = [buf("pb0") if m_ % 2 == 0 else buf("pb1")]
                        for kk in range(8):
                            mm(pp[:], s_bf[:, W1R + kk * 2048 + m_ * 128:
                                           W1R + kk * 2048 + (m_ + 1) * 128],
                               s_bf[:, XCC + kk * 512:XCC + (kk + 1) * 512],
                               kk == 0, kk == 7,
                               BFS(W1R + kk * 2048, 2048) + BFS(XCC, 4096), ppb)
                        hb1 = s_bf[:, H1M + m_ * 512:H1M + (m_ + 1) * 512]
                        sc(lambda en, hb1=hb1, pp=pp, e=e, m_=m_: en.activation(
                            hb1, pp[:], AF.Relu,
                            bias=s_m2p[:, e * 16 + m_:e * 16 + m_ + 1]),
                           ppb + [buf("m2p")], BFS(H1M + m_ * 512, 512))
                        v(lambda en, hb1=hb1: en.tensor_tensor(
                            hb1, hb1, s_bf[:, MBC:MBC + 512], OP.mult),
                          BFS(MBC, 512) + BFS(H1M + m_ * 512, 512),
                          BFS(H1M + m_ * 512, 512))
                    for dm in range(8):
                        pq = p_big[:, (dm % 4) * 512:(dm % 4) * 512 + 512]
                        pqb = PBG((dm % 4) * 512, 512)
                        dmas(s_bf[:, WST2:WST2 + 2048],
                             d_wqm[e * 3072 + 1024 + dm * 128:
                                   e * 3072 + 1024 + (dm + 1) * 128, 0:2048],
                             [buf("dwqm")] + BFS(WST2, 2048), BFS(WST2, 2048))
                        for kk in range(16):
                            mm(pq, s_bf[:, WST2 + kk * 128:WST2 + (kk + 1) * 128],
                               s_bf[:, H1M + kk * 512:H1M + (kk + 1) * 512],
                               kk == 0, kk == 15,
                               BFS(WST2, 2048) + BFS(H1M + kk * 512, 512), pqb)
                        yl = WK(0)[:, 0:512]
                        dmas(yl, d_y1[dm * 128:(dm + 1) * 128, c * 512:(c + 1) * 512],
                             [buf("d_y1")] + WKB(0), WKB(0))
                        v(lambda en, yl=yl, pq=pq: en.tensor_tensor(
                            yl, yl, pq, OP.add), pqb + WKB(0), WKB(0))
                        dmas(d_y1[dm * 128:(dm + 1) * 128, c * 512:(c + 1) * 512], yl,
                             WKB(0), [buf("d_y1")])

            dbg_tap("moe_xh", d_u[0:1024, :], [buf("d_u")], dt=BF16, shape=[1024, T])
            dbg_tap("moe_mask", d_bc1[0:8, :], [buf("d_bc1")], dt=BF16, shape=[8, T])
            dbg_tap("moe_y", d_y1[0:1024, :], [buf("d_y1")], shape=[1024, T])
            dbg_tap("moe_hx3", s_hx3[:, :], [buf("hx3")], shape=[128, 160])
            for c in range(4):
                for dm in range(8):
                    yl = WK(3)[:, 0:512]
                    dmas(yl, d_y1[dm * 128:(dm + 1) * 128, c * 512:(c + 1) * 512],
                         [buf("d_y1")] + WKB(3), WKB(3))
                    v(lambda en, yl=yl, dm=dm, c=c: en.tensor_tensor(
                        s_res[:, dm * T + c * 512:dm * T + (c + 1) * 512],
                        s_res[:, dm * T + c * 512:dm * T + (c + 1) * 512], yl,
                        OP.add),
                      WKB(3) + [buf("x")], [buf("x")])

        if upto == "all":
            moe()

        for dmt in range(DT):
            dmas(out[dmt * 128:(dmt + 1) * 128, :], s_res[:, dmt * T:(dmt + 1) * T],
                 reads=[buf("x")])
        S.prog["sync"].append(
            lambda e, sem=S.sem["qs"], val=S.cnt["qs"]: e.wait_ge(sem, val))

        for eng in ENGINES:
            deco = getattr(blk, eng)
            prog = S.prog[eng]
            def make(prog):
                def body(h):
                    for fn in prog:
                        fn(h)
                return body
            deco(make(prog))

    return nc


def _colmajor(vec, p=128):
    v = np.asarray(vec).reshape(-1)
    return np.ascontiguousarray(v.reshape(-1, p).T)


def _m2small(inputs):
    A = lambda k: np.asarray(inputs[k]).astype(np.float32)
    blks = []
    for l in range(2):
        b = np.zeros((128, 128), np.float32)
        conv = A("L1_attn_conv_w")[l]
        b[:, 0:68] = conv.reshape(17, 128, 4).transpose(1, 0, 2).reshape(128, 68)
        b[:, 68:85] = _colmajor(A("L1_attn_conv_b")[l])
        b[:, 85:93] = _colmajor(A("L1_attn_norm_w")[l])
        b[:, 93:109] = _colmajor(A("L1_attn_gate_norm_w")[l])
        b[0:32, 109] = A("L1_attn_A_log")[l]
        b[0:32, 110] = A("L1_attn_dt_bias")[l]
        b[0:32, 111] = A("L1_attn_D")[l]
        blks.append(b)
    return np.ascontiguousarray(np.concatenate(blks, 0))


def _stage_common(inputs):
    cT = np.ascontiguousarray
    tr = lambda a: np.ascontiguousarray(np.transpose(np.asarray(a)))
    A = lambda k: np.asarray(inputs[k])
    return dict(
        identf=np.eye(128, dtype=np.float32),
        constf=np.array([[1e-5, 1.0, 12582912.0, -12582912.0, -1.0, -0.5, 2.0, 1.0 / 127.0]], np.float32),
        trimask=np.triu(np.ones((128, 128), dtype=np.float32)),
        W_in=cT(A("L0_attn_W_in")),
        conv_w0=cT(A("L0_attn_conv_w")[:, 0, :].reshape(16, 128, 4).transpose(1, 0, 2).reshape(128, 64)),
        conv_b0=_colmajor(A("L0_attn_conv_b")),
        W_xp=cT(A("L0_attn_W_xp")),
        W_dt=cT(A("L0_attn_W_dt")),
        dt_bias0=_colmajor(A("L0_attn_dt_bias")),
        A_log0=cT(A("L0_attn_A_log")),
        D_skip0=_colmajor(A("L0_attn_D_skip")),
        W_out0=cT(A("L0_attn_W_out")),
        W_fc_t=tr(A("L0_ffn_W_fc")),
        b_fc=_colmajor(A("L0_ffn_b_fc")),
        A_fc_t=tr(A("L0_ffn_A_fc")),
        B_fc_t=tr(A("L0_ffn_B_fc")),
        W_proj_t=tr(A("L0_ffn_W_proj")),
        b_proj=_colmajor(A("L0_ffn_b_proj")),
        A_proj_t=tr(A("L0_ffn_A_proj")),
        B_proj_t=tr(A("L0_ffn_B_proj")),
        in_proj1=cT(np.transpose(A("L1_attn_in_proj_w"), (0, 2, 1)).reshape(2 * 1024, 4256)),
        out_proj1=cT(np.transpose(A("L1_attn_out_proj_w"), (0, 2, 1)).reshape(2 * 2048, 1024)),
        m2small=_m2small(inputs),
        m2drow=cT(np.repeat(A("L1_attn_D").astype(np.float32), 64, axis=1)),
        normf_cols=_colmajor(A("L1_attn_norm_f_w")),
        Wg=cT(A("L1_ffn_Wg")),
        W1e=cT(A("L1_ffn_W1")),
        b1e=cT(A("L1_ffn_b1")),
        W2e=cT(A("L1_ffn_W2")),
        b2e=cT(A("L1_ffn_b2")),
        b1cols=np.ascontiguousarray(
            np.asarray(inputs["L1_ffn_b1"]).astype(np.float32).reshape(8, 16, 128)
            .transpose(2, 0, 1).reshape(128, 128)),
    )


def _in_maps(inputs):
    T_ALL = 16384
    x_full = np.asarray(inputs["L0_attn_x"]).reshape(T_ALL, D).astype(np.float32)
    common = _stage_common(inputs)
    maps = []
    for k in range(NCORES):
        lo = k * T
        xs = np.ascontiguousarray(x_full[lo:lo + T, :].T)

        cs = np.zeros((1, 32), np.float32)
        cs[0, 0:8] = (np.arange(8) < k)
        cs[0, 8:16] = (np.arange(8) == k - 1)
        cs[0, 16] = k
        m = dict(common)
        m.update(x_in=xs, consts=cs)
        maps.append(m)
    return maps


def kernel(**inputs):
    nc = build_nc()
    res = run_bass_kernel_spmd(nc, _in_maps(inputs), core_ids=list(range(NCORES)))
    o = np.concatenate([np.transpose(r["out"]) for r in res.results], axis=0)
    return o.reshape(1, 16384, D)


if __name__ == "__main__":
    nc = build_nc("l0ffn")
    print("built")


# version 07
# speedup vs baseline: 1.0065x; 1.0065x over previous
import numpy as np
import ml_dtypes
from contextlib import ExitStack

import concourse.bass as bass
import concourse.mybir as mybir
from concourse.bass_utils import run_bass_kernel_spmd
from concourse.alu_op_type import AluOpType as OP

F32 = mybir.dt.float32
BF16 = mybir.dt.bfloat16
AF = mybir.ActivationFunctionType
AX = mybir.AxisListType.X

NCORES = 8
T = 2048
D = 1024
DT = D // 128
BF = np.dtype(ml_dtypes.bfloat16)

E1 = 2048
NE1 = 16
NS1 = 16
FF = 4096
LORA_SC = 2.0
MAGIC = 12582912.0

I2 = 2048
NH2 = 32
P2 = 64
N2 = 64
CONV2 = 2176
PROJ2 = 4256
Q2 = 128
NCH2 = 16

NEXP = 8
HME = 2048
TEST_NO_INTER = False

ENGINES = ("tensor", "vector", "scalar", "gpsimd", "sync")

def _ap3(ap2d, dims, offset_extra=0):
    return type(ap2d)(ap2d.tensor, ap2d.offset + offset_extra, [list(d) for d in dims])


def tile_kmajor(ap2d, p=128):
    (rs, R), (cs, C) = ap2d.ap
    assert R % p == 0
    return _ap3(ap2d, [[rs, p], [rs * p, R // p], [cs, C]])


def tile_colmajor(ap2d, p=128):
    (rs, H), (cs, Cc) = ap2d.ap
    assert Cc % p == 0
    return _ap3(ap2d, [[cs, p], [cs * p, Cc // p], [rs, H]])




class Tok:
    __slots__ = ("sem", "val")
    def __init__(self, sem, val):
        self.sem, self.val = sem, val


class Buf:
    __slots__ = ("name", "w", "r")
    def __init__(self, name):
        self.name, self.w, self.r = name, None, []


class Sched:
    def __init__(self, nc):
        self.nc = nc
        self.prog = {e: [] for e in ENGINES}
        self.sem = {}
        self.cnt = {}
        self.waited = {e: {} for e in ENGINES}
        self.es = None
        self.bufs = {}

    def buf(self, name):
        if name not in self.bufs:
            self.bufs[name] = Buf(name)
        return self.bufs[name]

    def new_sem(self, name):
        self.sem[name] = self.es.enter_context(self.nc.semaphore(name))
        self.cnt[name] = 0

    DMA_QS = ("qs", "qg", "qv")
    SAFE = False

    def _safe_pre(self, eng):
        if not self.SAFE:
            return
        for sem in ("tensor", "vector", "scalar", "gpsimd", "sync", "qs", "qg", "qv", "cc"):
            if sem == eng:
                continue
            c = self.cnt.get(sem, 0)
            if c > 0:
                self._wait(eng, Tok(sem, c))

    def _wait(self, eng, tok):
        if tok is None or tok.sem == eng:
            return
        val = tok.val
        if tok.sem in self.DMA_QS:
            val = self.cnt[tok.sem]
        w = self.waited[eng]
        if w.get(tok.sem, 0) >= val:
            return
        w[tok.sem] = val
        sem = self.sem[tok.sem]
        self.prog[eng].append(lambda e, val=val: e.wait_ge(sem, val))

    def _pre(self, eng, reads, writes):
        for b in reads:
            self._wait(eng, b.w)
        for b in writes:
            self._wait(eng, b.w)
            for t_ in b.r:
                self._wait(eng, t_)

    def _post(self, tok, reads, writes):
        for b in reads:
            b.r.append(tok)
        for b in writes:
            b.w, b.r = tok, []

    def op(self, eng, fn, reads=(), writes=()):
        self._safe_pre(eng)
        self._pre(eng, reads, writes)
        self.cnt[eng] += 1
        sem = self.sem[eng]
        tok = Tok(eng, self.cnt[eng])

        def run(e, fn=fn, sem=sem):
            fn(e).then_inc(sem, 1)
        self.prog[eng].append(run)
        self._post(tok, reads, writes)
        return tok

    def dma(self, eng, qsem, out, in_, reads=(), writes=(), slow=False, accum=False):
        self._safe_pre(eng)
        self._pre(eng, reads, writes)
        self.cnt[qsem] += 16
        sem = self.sem[qsem]
        tok = Tok(qsem, self.cnt[qsem])

        def run(e, out=out, in_=in_, sem=sem, slow=slow, accum=accum):
            kw = dict(allow_slow_non_contiguous=slow)
            if accum:
                kw["accum_op"] = OP.add
            e.dma_start(out=out, in_=in_, **kw).then_inc(sem, 16)
        self.prog[eng].append(run)
        self._post(tok, reads, writes)
        return tok

    def cc(self, kind, op, ins, outs, reads=(), writes=(), barrier=None):
        eng = "gpsimd"
        self._safe_pre(eng)
        self._pre(eng, reads, writes)
        sem = self.sem["cc"]
        self.cnt["cc"] += 1

        def run(e, sem=sem):
            e.collective_compute(kind, op, replica_groups=[list(range(NCORES))],
                                 ins=ins, outs=outs).then_inc(sem, 1)
        self.prog[eng].append(run)
        self.cnt["cc"] += 1
        self.prog[eng].append(run)
        if barrier is not None:
            bin_, bout = barrier
            self.cnt["cc"] += 1

            def run2(e, sem=sem):
                e.collective_compute("AllGather", OP.bypass,
                                     replica_groups=[list(range(NCORES))],
                                     ins=[bin_], outs=[bout]).then_inc(sem, 1)
            self.prog[eng].append(run2)
        tok = Tok("cc", self.cnt["cc"])
        self._post(tok, reads, writes)
        return tok


def build_nc(upto="all", taps=False):
    nc = bass.Bass("TRN2", target_bir_lowering=False)
    S = Sched(nc)
    tap_list = []

    def inp(name, shape, dt=F32):
        return nc.dram_tensor(name, shape, dt, kind="ExternalInput").ap()

    x_in = inp("x_in", [D, T])
    consts = inp("consts", [1, 32])
    identf_in = inp("identf", [128, 128])
    trimask_in = inp("trimask", [128, 128])
    constf_in = inp("constf", [1, 8])

    w_in = inp("W_in", [D, 2 * E1])
    conv_w0 = inp("conv_w0", [128, 64])
    conv_b0 = inp("conv_b0", [128, 16])
    w_xp = inp("W_xp", [E1, 96])
    w_dt = inp("W_dt", [64, E1])
    dt_bias0 = inp("dt_bias0", [128, 16])
    a_log0 = inp("A_log0", [E1, NS1])
    d_skip0 = inp("D_skip0", [128, 16])
    w_out0 = inp("W_out0", [E1, D])

    w_fc = inp("W_fc_t", [D, FF])
    b_fc = inp("b_fc", [128, 32])
    a_fc = inp("A_fc_t", [D, 16])
    b_fc_l = inp("B_fc_t", [16, FF])
    w_proj = inp("W_proj_t", [FF, D])
    b_proj = inp("b_proj", [128, 8])
    a_proj = inp("A_proj_t", [FF, 16])
    b_proj_l = inp("B_proj_t", [16, D])

    in_proj1 = inp("in_proj1", [2 * D, PROJ2])
    out_proj1 = inp("out_proj1", [2 * I2, D])
    m2small = inp("m2small", [256, 128])
    m2drow = inp("m2drow", [2, 2048])
    normf_cols = inp("normf_cols", [128, 8])

    wg = inp("Wg", [D, NEXP])
    w1e = inp("W1e", [NEXP, D, HME])
    b1e = inp("b1e", [NEXP, HME])
    w2e = inp("W2e", [NEXP, HME, D])
    b2e = inp("b2e", [NEXP, D])
    b1cols = inp("b1cols", [128, 128])

    out = nc.dram_tensor("out", [D, T], F32, kind="ExternalOutput").ap()

    def scratch(name, shape, dt=F32):
        return nc.dram_tensor(name, shape, dt).ap()

    d_xpark = scratch("d_xpark", [D, T])
    d_u = scratch("d_u", [E1, T], BF16)
    d_z = scratch("d_z", [E1, T], BF16)
    d_y1 = scratch("d_y1", [E1, T])
    d_yhat = scratch("d_yhat", [E1, T], BF16)
    d_bc1 = scratch("d_bc1", [32, T], BF16)
    d_row = scratch("d_row", [8, 4096])
    d_wq = scratch("d_wq", [FF + D, max(FF, D)], BF16)
    d_gate = scratch("d_gate", [I2, T], BF16)
    d_m2st = scratch("d_m2st", [NCH2 * 16 * 128, N2])
    d_wqm = scratch("d_wqm", [NEXP * 3072, 2048], BF16)

    agbar_in = scratch("agbar_in", [1, 16])
    agbar_out = scratch("agbar_out", [NCORES, 16])
    aghx_in = scratch("aghx_in", [128, 64])
    aghx_out = scratch("aghx_out", [NCORES * 128, 64])
    ag1_in = scratch("ag1_in", [128, 272])
    ag1_out = scratch("ag1_out", [NCORES * 128, 272])
    agh_in = [scratch(f"agh{i}_in", [3, D]) for i in range(2)]
    agh_out = [scratch(f"agh{i}_out", [NCORES * 3, D]) for i in range(2)]
    ag2_in = [scratch(f"ag2{i}_in", [128, 1056]) for i in range(2)]
    ag2_out = [scratch(f"ag2{i}_out", [NCORES * 128, 1056]) for i in range(2)]

    es = ExitStack()
    S.es = es
    SBA = lambda name, shape, dt=F32: es.enter_context(nc.sbuf_tensor(name, shape, dt))
    PSA = lambda name, shape, dt=F32: es.enter_context(nc.psum_tensor(name, shape, dt))

    with es:
        for e in ENGINES:
            S.new_sem(e)
        for q in ("qs", "qg", "qv", "cc"):
            S.new_sem(q)

        s_res = SBA("s_res", [128, DT * T])
        s_bf = SBA("s_bf", [128, 32768], BF16)
        s_wk = SBA("s_wk", [128, 4 * T + 8])
        s_wst = SBA("s_wst", [128, 2048])
        s_wsm = SBA("s_wsm", [128, 4096], BF16)
        s_stats = SBA("s_stats", [128, 2 * T])

        s_ident = SBA("s_ident", [128, 128])
        s_identb = SBA("s_identb", [128, 128], BF16)
        s_onesb = SBA("s_onesb", [128, 128], BF16)
        s_tri = SBA("s_tri", [128, 128])
        s_small = SBA("s_small", [128, 128])
        s_small2 = SBA("s_small2", [128, 128])
        s_cmask = SBA("s_cmask", [128, 32])
        s_const = SBA("s_const", [128, 8])
        s_hx3 = SBA("s_hx3", [128, 160])
        s_qs = SBA("s_qs", [128, 176])
        s_m2a = SBA("s_m2a", [128, 32])
        s_m2p = SBA("s_m2p", [128, 128])
        s_A = SBA("s_A", [128, 256])
        s_hend = SBA("s_hend", [128, 272])
        s_h0 = SBA("s_h0", [128, 256])
        s_xdbl = SBA("s_xdbl", [96, T])

        p_big = PSA("p_big", [128, 2048])
        pb0 = PSA("pb0", [128, 512])
        pb1 = PSA("pb1", [128, 512])
        pt_f = PSA("pt_f", [128, 256])
        pt_b = PSA("pt_b", [128, 256], BF16)

        blk = es.enter_context(nc.Block())

        buf = S.buf
        def BFS(off, ncols):
            return [buf(f"bf{j}") for j in range(off // 2048, (off + ncols + 2047) // 2048)]
        def WKB(i):
            return [buf(f"wk{i}")]
        def WSTB(off, ncols):
            return [buf(f"wst{j}") for j in range(off // 1024, (off + ncols + 1023) // 1024)]
        def STB(off, ncols):
            return [buf(f"st{j}") for j in range(off // 512, (off + ncols + 511) // 512)]
        def PBG(off, ncols):
            return [buf(f"pB{j}") for j in range(off // 512, (off + ncols + 511) // 512)]

        WK = lambda i: s_wk[:, i * T:(i + 1) * T]

        v = lambda fn, reads=(), writes=(): S.op("vector", fn, reads, writes)
        sc = lambda fn, reads=(), writes=(): S.op("scalar", fn, reads, writes)
        te = lambda fn, reads=(), writes=(): S.op("tensor", fn, reads, writes)
        g = lambda fn, reads=(), writes=(): S.op("gpsimd", fn, reads, writes)
        dmas = lambda o, i, reads=(), writes=(), slow=False: S.dma("sync", "qs", o, i, reads, writes, slow)
        dmag = lambda o, i, reads=(), writes=(), slow=False: S.dma("gpsimd", "qg", o, i, reads, writes, slow)
        dmav = lambda o, i, reads=(), writes=(), slow=False: S.dma("gpsimd", "qv", o, i, reads, writes, slow)

        def dbg_tap(name, src_ap, reads, dt=F32, shape=None):
            if not taps:
                return
            shp = shape or [src_ap.shape[0], src_ap.shape[-1]]
            tdr = nc.dram_tensor("tap_" + name, shp, dt, kind="ExternalOutput").ap()
            dmag(tdr, src_ap, reads=list(reads))
            tap_list.append(name)

        def mm(pp, lhsT, rhs, first, last, reads, pwb):
            te(lambda e, pp=pp, lhsT=lhsT, rhs=rhs, first=first, last=last: e.matmul(
                pp, lhsT, rhs, start=first, stop=last),
               reads, pwb if (first or last) else [])

        dmas(s_ident[:], identf_in, writes=[buf("ident")])
        dmas(s_tri[:], trimask_in, writes=[buf("tri")])
        dmas(s_cmask[:], consts.broadcast_to((128, 32)), writes=[buf("cmask")])
        dmas(s_const[:], constf_in.broadcast_to((128, 8)), writes=[buf("cf")])
        CC = lambda i: s_const[:, i:i + 1]


        v(lambda e: e.tensor_copy(s_identb[:], s_ident[:]), [buf("ident")], [buf("identb")])
        v(lambda e: e.memset(s_onesb[:], 1.0), (), [buf("onesb")])
        dmas(s_res[:], tile_kmajor(x_in), writes=[buf("x")])
        dmas(s_A[:], tile_kmajor(a_log0), writes=[buf("A")])
        sc(lambda e: e.activation(s_A[:], s_A[:], AF.Exp), [buf("A")], [buf("A")])
        v(lambda e: e.tensor_single_scalar(s_A[:], s_A[:], -1.0, OP.mult), [buf("A")], [buf("A")])
        dmas(s_small[:, 0:64], conv_w0, writes=[buf("cw0")])
        dmas(s_small[:, 64:80], conv_b0, writes=[buf("cb0")])
        dmas(s_small[:, 80:96], dt_bias0, writes=[buf("dtb0")])
        dmas(s_small[:, 96:112], d_skip0, writes=[buf("dsk0")])

        SB_SQ = 20480

        def ln_stats(src_sb, srcb, ntiles, rms=False, eps=1e-5, statb_name="stats"):
            st = STB(0, 4096)
            Dw = ntiles * 128
            for c in range(T // 512):
                w0 = c * 512
                xb = s_bf[:, SB_SQ:SB_SQ + ntiles * 512]
                qb = s_bf[:, SB_SQ + 4096:SB_SQ + 4096 + ntiles * 512]
                for i in range(ntiles):
                    sc(lambda e, i=i, w0=w0, xb=xb: e.activation(
                        xb[:, i * 512:(i + 1) * 512],
                        src_sb[:, i * T + w0:i * T + w0 + 512], AF.Identity),
                       [srcb], BFS(SB_SQ, 4096))
                    sc(lambda e, i=i, w0=w0, qb=qb: e.activation(
                        qb[:, i * 512:(i + 1) * 512],
                        src_sb[:, i * T + w0:i * T + w0 + 512], AF.Square),
                       [srcb], BFS(SB_SQ + 4096, 4096))
                if not rms:
                    for i in range(ntiles):
                        mm(p_big[:, 0:512], s_onesb[:], xb[:, i * 512:(i + 1) * 512],
                           i == 0, i == ntiles - 1,
                           BFS(SB_SQ, 4096) + [buf("onesb")], PBG(0, 512))
                for i in range(ntiles):
                    mm(p_big[:, 512:1024], s_onesb[:], qb[:, i * 512:(i + 1) * 512],
                       i == 0, i == ntiles - 1,
                       BFS(SB_SQ + 4096, 4096) + [buf("onesb")], PBG(512, 512))
                mu = s_stats[:, w0:w0 + 512]
                rs = s_stats[:, T + w0:T + w0 + 512]
                if rms:
                    v(lambda e, rs=rs: e.tensor_single_scalar(
                        rs, p_big[:, 512:1024], 1.0 / Dw, OP.mult), PBG(512, 512), st)
                else:
                    v(lambda e, mu=mu: e.tensor_single_scalar(
                        mu, p_big[:, 0:512], 1.0 / Dw, OP.mult), PBG(0, 512), st)
                    v(lambda e, mu=mu, rs=rs: e.tensor_tensor(rs, mu, mu, OP.mult), st, st)
                    v(lambda e, rs=rs: e.scalar_tensor_tensor(
                        rs, p_big[:, 512:1024], 1.0 / Dw, rs, OP.mult, OP.subtract),
                      PBG(512, 512) + st, st)
                sc(lambda e, rs=rs: e.activation(rs, rs, AF.Ln, bias=CC(0)), st + [buf("cf")], st)
                sc(lambda e, rs=rs: e.activation(rs, rs, AF.Exp, scale=CC(5)), st + [buf("cf")], st)

        def norm_chunk(src_sb, srcb, dst_off, ntiles, c, width, rms=False,
                       statb_name="stats", tmp_slot=3, dst_stride=None, dst_coff=0):
            st = STB(0, 4096)
            if dst_stride is None:
                dst_stride = width
            w0 = c * width
            for i in range(ntiles):
                src_t = src_sb[:, i * T + w0:i * T + w0 + width]
                d0c = dst_off + i * dst_stride + dst_coff
                dst_t = s_bf[:, d0c:d0c + width]
                if rms:
                    v(lambda e, s_=src_t, d=dst_t: e.tensor_tensor(
                        d, s_, s_stats[:, T + w0:T + w0 + width], OP.mult),
                      [srcb] + st, BFS(d0c, width))
                else:
                    tmp = s_wk[:, tmp_slot * T:tmp_slot * T + width]
                    v(lambda e, s_=src_t, tmp=tmp: e.tensor_tensor(
                        tmp, s_, s_stats[:, w0:w0 + width], OP.subtract),
                      [srcb] + st, WKB(tmp_slot))
                    v(lambda e, tmp=tmp, d=dst_t: e.tensor_tensor(
                        d, tmp, s_stats[:, T + w0:T + w0 + width], OP.mult),
                      WKB(tmp_slot) + st, BFS(d0c, width))

        XH = 0
        UZST = 16384
        HXH = 30720

        ln_stats(s_res, buf("x"), DT)
        for c in range(4):
            norm_chunk(s_res, buf("x"), XH, DT, c, 512, dst_stride=T, dst_coff=c * 512)
        dbg_tap("stats", s_stats[:], [buf("st%d" % i) for i in range(8)])
        dbg_tap("xhat", s_bf[:, XH:XH + 16384], BFS(XH, 16384), dt=BF16)


        convin = s_wk[:, 0:T + 8]
        cacc = WK(2)
        CVB = WKB(0)
        CAB = WKB(2)

        def conv_tile_m1(m, wap, wmb):
            v(lambda e: e.memset(convin[:, 0:4], 0.0), (), CVB)
            v(lambda e, m=m: e.tensor_copy(s_hx3[:, 48 + 3 * m:51 + 3 * m], convin[:, 4:7]),
              CVB, [buf("hx3")])
            v(lambda e, m=m: e.tensor_copy(s_hx3[:, 3 * m:3 * m + 3], convin[:, 4 + T - 3:4 + T]),
              CVB, [buf("hx3")])
            for k in range(4):
                src = convin[:, k + 1:k + 1 + T]
                tap = s_small[:, 4 * m + k:4 * m + k + 1]
                if k == 0:
                    v(lambda e, src=src, tap=tap: e.tensor_single_scalar(cacc, src, tap, OP.mult),
                      CVB + [buf("cw0")], CAB)
                else:
                    v(lambda e, src=src, tap=tap: e.scalar_tensor_tensor(
                        cacc, src, tap, cacc, OP.mult, OP.add),
                      CVB + [buf("cw0")] + CAB, CAB)
            if m == 0 and taps:
                dbg_tap("convin0", convin, CVB)
            ub = s_bf[:, UZST + (m % 2) * 2048:UZST + (m % 2) * 2048 + T]
            sc(lambda e, ub=ub, m=m: e.activation(ub, cacc, AF.Silu,
                                                  bias=s_small[:, 64 + m:65 + m]),
               CAB + [buf("cb0")], BFS(UZST + (m % 2) * 2048, T))
            dmav(d_u[m * 128:(m + 1) * 128, :], ub, BFS(UZST + (m % 2) * 2048, T), [buf("d_u")])

        def mamba1_inproj():
            for m in range(32):
                wmap = tile_kmajor(w_in[:, m * 128:(m + 1) * 128])
                stg = s_wst[:, (m % 2) * 1024:(m % 2) * 1024 + 1024]
                dmas(stg, wmap, writes=WSTB((m % 2) * 1024, 1024))
                wap = s_wsm[:, (m % 2) * 1024:(m % 2) * 1024 + 1024]
                wmb = buf(f"wsm{m % 2}")
                v(lambda e, wap=wap, stg=stg: e.tensor_copy(wap, stg),
                  WSTB((m % 2) * 1024, 1024), [wmb])
                for c in range(4):
                    pp = p_big[:, (c % 2) * 512:(c % 2) * 512 + 512]
                    ppb = PBG((c % 2) * 512, 512)
                    for k in range(8):
                        mm(pp, wap[:, k * 128:(k + 1) * 128],
                           s_bf[:, XH + k * T + c * 512:XH + k * T + (c + 1) * 512],
                           k == 0, k == 7, [wmb] + BFS(XH + k * T + c * 512, 512), ppb)
                    if m < 16:
                        dst = convin[:, 4 + c * 512:4 + (c + 1) * 512]
                        sc(lambda e, dst=dst, pp=pp: e.copy(dst, pp), ppb, CVB)
                    else:
                        zb = s_bf[:, UZST + 2048 + (c % 2) * 512:UZST + 2048 + (c % 2) * 512 + 512]
                        sc(lambda e, zb=zb, pp=pp: e.copy(zb, pp), ppb,
                           BFS(UZST + 2048 + (c % 2) * 512, 512))
                        dmav(d_z[(m - 16) * 128:(m - 15) * 128, c * 512:(c + 1) * 512], zb,
                             BFS(UZST + 2048 + (c % 2) * 512, 512), [buf("d_z")])
                if m < 16:
                    conv_tile_m1(m, wap, wmb)
        mamba1_inproj()

        dmag(aghx_in[:, 0:48], s_hx3[:, 0:48], [buf("hx3")], [buf("aghxi")])
        S.cc("AllGather", OP.bypass, [aghx_in], [aghx_out], [buf("aghxi")], [buf("aghxo")], barrier=(agbar_in, agbar_out))
        HB = 96
        v(lambda e: e.memset(s_hx3[:, HB:HB + 48], 0.0), (), [buf("hx3")])
        for j in range(NCORES - 1):
            agj = s_wst[:, 0:48]
            dmag(agj, aghx_out[j * 128:(j + 1) * 128, 0:48],
                 [buf("aghxo")] + WSTB(0, 48), WSTB(0, 48))
            oc = s_cmask[:, 8 + j:9 + j]
            v(lambda e, agj=agj, oc=oc: e.scalar_tensor_tensor(
                s_hx3[:, HB:HB + 48], agj, oc, s_hx3[:, HB:HB + 48], OP.mult, OP.add),
              WSTB(0, 48) + [buf("cmask"), buf("hx3")], [buf("hx3")])
        for m in range(NE1):
            v(lambda e, m=m: e.tensor_copy(s_hx3[:, 144:147], s_hx3[:, HB + 3 * m:HB + 3 * m + 3]),
              [buf("hx3")], [buf("hx3")])
            v(lambda e, m=m: e.tensor_copy(s_hx3[:, 147:150], s_hx3[:, 48 + 3 * m:51 + 3 * m]),
              [buf("hx3")], [buf("hx3")])
            acc3 = s_hx3[:, 152:155]
            for k in range(4):
                srcw = s_hx3[:, 144 + k:147 + k]
                tapc = s_small[:, 4 * m + k:4 * m + k + 1]
                if k == 0:
                    v(lambda e, srcw=srcw, tapc=tapc: e.tensor_single_scalar(
                        acc3, srcw, tapc, OP.mult),
                      [buf("hx3"), buf("cw0")], [buf("hx3")])
                else:
                    v(lambda e, srcw=srcw, tapc=tapc: e.scalar_tensor_tensor(
                        acc3, srcw, tapc, acc3, OP.mult, OP.add),
                      [buf("hx3"), buf("cw0")], [buf("hx3")])
            u3 = s_hx3[:, 156:159]
            sc(lambda e, m=m, u3=u3: e.activation(u3, acc3, AF.Silu,
                                                  bias=s_small[:, 64 + m:65 + m]),
               [buf("hx3"), buf("cb0")], [buf("hx3")])
            ub3 = s_bf[:, UZST:UZST + 4]
            v(lambda e, u3=u3, ub3=ub3: e.tensor_copy(ub3[:, 0:3], u3),
              [buf("hx3")], BFS(UZST, 4))
            dmav(d_u[m * 128:(m + 1) * 128, 0:3], ub3[:, 0:3], BFS(UZST, 4), [buf("d_u")])

        dmas(s_wst[:, 0:NE1 * 96], tile_kmajor(w_xp),
             writes=WSTB(0, NE1 * 96))
        wxp_b = s_wsm[:, 0:NE1 * 96]
        v(lambda e: e.tensor_copy(wxp_b, s_wst[:, 0:NE1 * 96]),
          WSTB(0, NE1 * 96), [buf("wsm0"), buf("wsm1")])
        for c in range(4):
            for k in range(NE1):
                ub = s_bf[:, UZST + (k % 2) * 2048:UZST + (k % 2) * 2048 + 512]
                dmas(ub, d_u[k * 128:(k + 1) * 128, c * 512:(c + 1) * 512],
                     [buf("d_u")], BFS(UZST + (k % 2) * 2048, 512))
                mm(p_big[0:96, 0:512], wxp_b[:, k * 96:(k + 1) * 96], ub,
                   k == 0, k == NE1 - 1,
                   [buf("wsm0"), buf("wsm1")] + BFS(UZST + (k % 2) * 2048, 512), PBG(0, 512))
            sc(lambda e, c=c: e.copy(s_xdbl[:, c * 512:(c + 1) * 512], p_big[0:96, 0:512]),
               PBG(0, 512), [buf("xdbl")])
        dbg_tap("xdbl", s_xdbl[:], [buf("xdbl")])
        dmav(d_bc1, s_xdbl[64:96, :], [buf("xdbl")], [buf("d_bc1")])
        dmav(tile_kmajor(d_xpark), s_res[:], [buf("x")], [buf("d_xp")])

        def delta_tile(i, dst, dstb):
            wdtt = s_wst[:, 1024:1152]
            dmas(wdtt[0:64, :], w_dt[:, i * 128:(i + 1) * 128], writes=WSTB(1024, 128))
            for c in range(4):
                te(lambda e, c=c, wdtt=wdtt: e.matmul(
                    pb0[:], wdtt[0:64, :],
                    s_xdbl[0:64, c * 512:(c + 1) * 512], start=True, stop=True),
                   WSTB(1024, 128) + [buf("xdbl")], [buf("pb0")])
                ch = dst[:, c * 512:(c + 1) * 512]
                ab = s_stats[:, 1536:2048]
                sp = STB(1536, 512)
                sc(lambda e, ch=ch, i=i: e.activation(ch, pb0[:], AF.Relu,
                                                      bias=s_small[:, 80 + i:81 + i]),
                   [buf("pb0"), buf("dtb0")], list(dstb))
                sc(lambda e, ab=ab, i=i: e.activation(ab, pb0[:], AF.Abs,
                                                      bias=s_small[:, 80 + i:81 + i]),
                   [buf("pb0"), buf("dtb0")], sp)
                sc(lambda e, ab=ab: e.activation(ab, ab, AF.Exp, scale=CC(4)), sp + [buf("cf")], sp)
                sc(lambda e, ab=ab: e.activation(ab, ab, AF.Ln, bias=CC(1)), sp + [buf("cf")], sp)
                v(lambda e, ch=ch, ab=ab: e.tensor_tensor(ch, ch, ab, OP.add),
                  sp + list(dstb), list(dstb))

        UB, BB, CB, DBU, TMP = 16384, 18432, 20480, 22528, 24576

        def mamba1_scan():
            v(lambda e: e.memset(s_hend[:], 0.0), (), [buf("hend")])
            for i in range(NE1):
                dl = WK(0)
                delta_tile(i, dl, WKB(0))
                v(lambda e, i=i: e.tensor_reduce(s_hend[:, 256 + i:257 + i], dl, AX, OP.add),
                  WKB(0), [buf("hend")])
                ub = s_bf[:, UB:UB + T]
                dmas(ub, d_u[i * 128:(i + 1) * 128, :], [buf("d_u")], BFS(UB, T))
                du = WK(1)
                v(lambda e, du=du, dl=dl, ub=ub: e.tensor_tensor(du, dl, ub, OP.mult),
                  WKB(0) + BFS(UB, T), WKB(1))
                yt = WK(2)
                for n in range(NS1):
                    bb = s_bf[:, BB:BB + T]
                    cb_ = s_bf[:, CB:CB + T]
                    dmas(bb, d_bc1[n:n + 1, :].broadcast_to((128, T)), [buf("d_bc1")], BFS(BB, T))
                    dmas(cb_, d_bc1[16 + n:17 + n, :].broadcast_to((128, T)), [buf("d_bc1")], BFS(CB, T))
                    sc(lambda e, dl=dl, i=i, n=n: e.activation(
                        p_big[:], dl, AF.Exp, scale=s_A[:, 16 * i + n:16 * i + n + 1]),
                       WKB(0) + [buf("A")], PBG(0, 2048))
                    dbu = s_bf[:, DBU:DBU + T]
                    v(lambda e, dbu=dbu, du=du, bb=bb: e.tensor_tensor(dbu, du, bb, OP.mult),
                      WKB(1) + BFS(BB, T), BFS(DBU, T))
                    h = WK(3)
                    v(lambda e, h=h, dbu=dbu: e.tensor_tensor_scan(
                        h, p_big[:], dbu, 0.0, OP.mult, OP.add),
                      PBG(0, 2048) + BFS(DBU, T), WKB(3))
                    if i == 0 and n == 0 and taps:
                        dbg_tap("h00", h, WKB(3))
                        dbg_tap("dbu00", s_bf[:, DBU:DBU + T], BFS(DBU, T), dt=BF16)
                    sc(lambda e, i=i, n=n, h=h: e.copy(
                        s_hend[:, 16 * i + n:16 * i + n + 1], h[:, T - 1:T]),
                       WKB(3), [buf("hend")])
                    if i == 0 and n == 0 and taps:
                        dbg_tap("hend00", s_hend[:, 0:16], [buf("hend")])
                    if n == 0:
                        v(lambda e, yt=yt, h=h, cb_=cb_: e.tensor_tensor(yt, h, cb_, OP.mult),
                          WKB(3) + BFS(CB, T), WKB(2))
                    else:
                        tmp = s_bf[:, TMP:TMP + T]
                        v(lambda e, tmp=tmp, h=h, cb_=cb_: e.tensor_tensor(tmp, h, cb_, OP.mult),
                          WKB(3) + BFS(CB, T), BFS(TMP, T))
                        v(lambda e, yt=yt, tmp=tmp: e.tensor_tensor(yt, yt, tmp, OP.add),
                          BFS(TMP, T) + WKB(2), WKB(2))
                dmav(d_y1[i * 128:(i + 1) * 128, :], yt, WKB(2), [buf("d_y1")])
        mamba1_scan()
        dbg_tap("u", d_u, [buf("d_u")], dt=BF16, shape=[E1, T])
        dbg_tap("z", d_z, [buf("d_z")], dt=BF16, shape=[E1, T])
        dbg_tap("y1", d_y1, [buf("d_y1")], shape=[E1, T])
        dbg_tap("hend", s_hend[:], [buf("hend")])

        dmag(ag1_in, s_hend[:], [buf("hend")], [buf("ag1i")])
        S.cc("AllGather", OP.bypass, [ag1_in], [ag1_out], [buf("ag1i")], [buf("ag1o")], barrier=(agbar_in, agbar_out))

        def mamba1_prefix():
            v(lambda e: e.memset(s_h0[:], 0.0), (), [buf("h0")])
            Tj = s_stats[:, 0:256]
            tmp = s_stats[:, 256:512]
            tb = STB(0, 512)
            for j in range(NCORES - 1):
                agj = s_wst[:, 0:272]
                dmag(agj, ag1_out[j * 128:(j + 1) * 128, :],
                     [buf("ag1o")] + WSTB(0, 272), WSTB(0, 272))
                for i in range(NE1):
                    v(lambda e, i=i, agj=agj: e.tensor_single_scalar(
                        Tj[:, 16 * i:16 * i + 16], s_A[:, 16 * i:16 * i + 16],
                        agj[:, 256 + i:257 + i], OP.mult),
                      [buf("A")] + WSTB(0, 272), tb)
                sc(lambda e: e.activation(Tj, Tj, AF.Exp), tb, tb)
                mcol = s_cmask[:, j:j + 1]
                v(lambda e, mcol=mcol: e.tensor_scalar(Tj, Tj, 1.0, mcol, OP.subtract, OP.mult),
                  tb + [buf("cmask")], tb)
                v(lambda e: e.tensor_single_scalar(Tj, Tj, 1.0, OP.add), tb, tb)
                v(lambda e: e.tensor_tensor(s_h0[:], s_h0[:], Tj, OP.mult),
                  tb + [buf("h0")], [buf("h0")])
                v(lambda e, mcol=mcol, agj=agj: e.scalar_tensor_tensor(
                    tmp, agj[:, 0:256], mcol, s_h0[:], OP.mult, OP.add),
                  WSTB(0, 272) + [buf("cmask"), buf("h0")] + tb, tb)
                v(lambda e: e.tensor_copy(s_h0[:], tmp), tb, [buf("h0")])
        mamba1_prefix()
        dbg_tap("h0", s_h0[:], [buf("h0")])
        dbg_tap("ag1o", ag1_out, [buf("ag1o")], shape=[NCORES * 128, 272])

        def mamba1_fix():
            for i in range(NE1):
                dl = WK(0)
                delta_tile(i, dl, WKB(0))
                onesw = WK(1)
                v(lambda e: e.memset(onesw, 1.0), (), WKB(1))
                Sw = WK(3)
                v(lambda e, Sw=Sw, dl=dl: e.tensor_tensor_scan(
                    Sw, onesw, dl, 0.0, OP.mult, OP.add),
                  WKB(0) + WKB(1), WKB(3))
                yt = WK(2)
                dmas(yt, d_y1[i * 128:(i + 1) * 128, :], [buf("d_y1")], WKB(2))
                for n in range(NS1):
                    cb_ = s_bf[:, CB:CB + T]
                    dmas(cb_, d_bc1[16 + n:17 + n, :].broadcast_to((128, T)),
                         [buf("d_bc1")], BFS(CB, T))
                    sc(lambda e, Sw=Sw, i=i, n=n: e.activation(
                        p_big[:], Sw, AF.Exp, scale=s_A[:, 16 * i + n:16 * i + n + 1]),
                       WKB(3) + [buf("A")], PBG(0, 2048))
                    tmp = s_bf[:, TMP:TMP + T]
                    v(lambda e, tmp=tmp, cb_=cb_, i=i, n=n: e.scalar_tensor_tensor(
                        tmp, p_big[:], s_h0[:, 16 * i + n:16 * i + n + 1], cb_,
                        OP.mult, OP.mult),
                      PBG(0, 2048) + [buf("h0")] + BFS(CB, T), BFS(TMP, T))
                    v(lambda e, yt=yt, tmp=tmp: e.tensor_tensor(yt, yt, tmp, OP.add),
                      BFS(TMP, T) + WKB(2), WKB(2))
                ub = s_bf[:, UB:UB + T]
                dmas(ub, d_u[i * 128:(i + 1) * 128, :], [buf("d_u")], BFS(UB, T))
                v(lambda e, yt=yt, ub=ub, i=i: e.scalar_tensor_tensor(
                    yt, ub, s_small[:, 96 + i:97 + i], yt, OP.mult, OP.add),
                  BFS(UB, T) + [buf("dsk0")] + WKB(2), WKB(2))
                zb = s_bf[:, BB:BB + T]
                dmas(zb, d_z[i * 128:(i + 1) * 128, :], [buf("d_z")], BFS(BB, T))
                sz = WK(1)
                sc(lambda e, sz=sz, zb=zb: e.activation(sz, zb, AF.Silu),
                   BFS(BB, T), WKB(1))
                yhb = s_bf[:, DBU:DBU + T]
                v(lambda e, yhb=yhb, yt=yt, sz=sz: e.tensor_tensor(yhb, yt, sz, OP.mult),
                  WKB(1) + WKB(2), BFS(DBU, T))
                dmav(d_yhat[i * 128:(i + 1) * 128, :], yhb, BFS(DBU, T), [buf("d_yh")])
        mamba1_fix()

        def mamba1_outproj():
            for i in range(NE1):
                stg = s_wst[:, (i % 2) * 1024:(i % 2) * 1024 + D]
                dmas(stg, w_out0[i * 128:(i + 1) * 128, :], writes=WSTB((i % 2) * 1024, D))
                dst = s_bf[:, i * D:(i + 1) * D]
                v(lambda e, dst=dst, stg=stg: e.tensor_copy(dst, stg),
                  WSTB((i % 2) * 1024, D), BFS(i * D, D))
            woutb = [buf(f"bf{j}") for j in range(8)]
            for c in range(T // 512):
                rh = s_bf[:, 16384 + (c % 2) * 8192:16384 + (c % 2) * 8192 + NE1 * 512]
                rhb = BFS(16384 + (c % 2) * 8192, 8192)
                for i in range(NE1):
                    dmas(rh[:, i * 512:(i + 1) * 512],
                         d_yhat[i * 128:(i + 1) * 128, c * 512:(c + 1) * 512],
                         [buf("d_yh")], [rhb[(i * 512) // 2048]])
                for dmt in range(8):
                    pp = p_big[:, (dmt % 4) * 512:(dmt % 4) * 512 + 512]
                    ppb = PBG((dmt % 4) * 512, 512)
                    for i in range(NE1):
                        mm(pp, s_bf[:, i * D + dmt * 128:i * D + (dmt + 1) * 128],
                           rh[:, i * 512:(i + 1) * 512], i == 0, i == NE1 - 1,
                           woutb + rhb, ppb)
                    xt = s_res[:, dmt * T + c * 512:dmt * T + (c + 1) * 512]
                    v(lambda e, xt=xt, pp=pp: e.tensor_tensor(xt, xt, pp, OP.add),
                      ppb + [buf("x")], [buf("x")])
        mamba1_outproj()
        dbg_tap("yhat", d_yhat, [buf("d_yh")], dt=BF16, shape=[E1, T])
        dbg_tap("xattn", s_res[:], [buf("x")])

        XC = 0
        H1 = 4096
        R1 = 20480
        R2 = 22528
        BFL = 23040
        BPL = 27136
        AFB = 28160
        APR = 28288
        WQS = 30720

        def qat_scale(tiles, scol):
            qb = [buf("qs")]
            for j, ap_ in enumerate(tiles):
                stg = s_wst[:, (j % 2) * 1024:(j % 2) * 1024 + 1024]
                dmas(stg, ap_, writes=WSTB((j % 2) * 1024, 1024))
                v(lambda e, stg=stg, j=j: e.tensor_reduce(
                    s_qs[:, j:j + 1], stg, AX, OP.max, apply_absolute_value=True),
                  WSTB((j % 2) * 1024, 1024), qb)
            v(lambda e: e.tensor_reduce(s_qs[:, 32:33], s_qs[:, 0:32], AX, OP.max), qb, qb)
            dmav(d_row[0:1, 0:128], s_qs[:, 32:33], qb, [buf("drow")])
            dmas(s_qs[0:1, 40:168], d_row[0:1, 0:128], [buf("drow")], qb)
            v(lambda e: e.tensor_reduce(s_qs[0:1, 33:34], s_qs[0:1, 40:168], AX, OP.max), qb, qb)
            gmsrc = s_qs[0:1, 33:34]
            gmrep = type(gmsrc)(gmsrc.tensor, gmsrc.offset,
                                [list(gmsrc.ap[0]), [0, 8], list(gmsrc.ap[1])])
            dmav(d_row[1:2, 0:8], gmrep, qb, [buf("drow")])
            dmas(s_qs[:, 48:56], d_row[1:2, 0:8].broadcast_to((128, 8)),
                 [buf("drow")], qb)
            sc(lambda e: e.activation(s_qs[:, 56:64], s_qs[:, 48:56], AF.Identity,
                                      scale=CC(7)), qb + [buf("cf")], qb)
            sc(lambda e: e.activation(s_qs[:, 64:72], s_qs[:, 56:64], AF.Ln), qb, qb)
            sc(lambda e: e.activation(s_qs[:, 64:72], s_qs[:, 64:72], AF.Exp,
                                      scale=CC(4)), qb + [buf("cf")], qb)
            v(lambda e: e.tensor_tensor(s_qs[:, 72:80], s_qs[:, 56:64],
                                        s_qs[:, 64:72], OP.mult), qb, qb)
            v(lambda e: e.tensor_scalar(s_qs[:, 72:80], s_qs[:, 72:80], 2.0, -1.0,
                                        OP.subtract, OP.mult), qb, qb)
            v(lambda e: e.tensor_tensor(s_qs[:, 64:72], s_qs[:, 64:72],
                                        s_qs[:, 72:80], OP.mult), qb, qb)
            sc(lambda e: e.copy(s_small2[:, scol:scol + 1], s_qs[:, 56:57]), qb, [buf("qsc")])
            sc(lambda e: e.copy(s_small2[:, scol + 1:scol + 2], s_qs[:, 64:65]), qb, [buf("qsc")])
            dbg_tap("qsdump%d" % scol, s_qs[0:1, 32:80], qb)

        def ffn():
            wfc_r = tile_kmajor(w_fc)
            wpr_r = tile_kmajor(w_proj)
            qat_scale([wpr_r[:, j:j + 1, :] for j in range(32)], 2)
            qat_scale([wfc_r[:, j // 4:j // 4 + 1, (j % 4) * 1024:(j % 4 + 1) * 1024] for j in range(32)], 0)
            dfc = d_wq[0:D, 0:FF]
            dpr = d_wq[D:D + FF, 0:D]
            srcs = []
            dsts = []
            for j in range(8):
                for ccc in range(4):
                    srcs.append(wfc_r[:, j:j + 1, ccc * 1024:(ccc + 1) * 1024])
                    dsts.append((dfc[j * 128:(j + 1) * 128, ccc * 1024:(ccc + 1) * 1024]))
            for j, (sp, dp) in enumerate(zip(srcs, dsts)):
                stg = s_wst[:, (j % 2) * 1024:(j % 2) * 1024 + 1024]
                dmas(stg, sp, writes=WSTB((j % 2) * 1024, 1024))
                q1 = s_stats[:, 0:1024]
                sq = STB(0, 1024)
                sc(lambda e, stg=stg: e.activation(
                    q1, stg, AF.Identity, scale=s_small2[:, 1:2], bias=CC(2)),
                   WSTB((j % 2) * 1024, 1024) + [buf("qsc")], sq)
                sc(lambda e: e.activation(q1, q1, AF.Identity, bias=CC(3)), sq, sq)
                v(lambda e: e.tensor_scalar(q1, q1, -128.0, 127.0, OP.max, OP.min), sq, sq)
                wqt = s_bf[:, WQS:WQS + 1024]
                v(lambda e, wqt=wqt: e.tensor_single_scalar(
                    wqt, q1, s_small2[:, 0:1], OP.mult), sq + [buf("qsc")], BFS(WQS, 1024))
                dmav(dp, wqt, BFS(WQS, 1024), [buf("dwfc")])
            for j in range(32):
                sp = wpr_r[:, j:j + 1, :]
                dp = dpr[j * 128:(j + 1) * 128, 0:1024]
                stg = s_wst[:, (j % 2) * 1024:(j % 2) * 1024 + 1024]
                dmas(stg, sp, writes=WSTB((j % 2) * 1024, 1024))
                q1 = s_stats[:, 0:1024]
                sq = STB(0, 1024)
                sc(lambda e, stg=stg: e.activation(
                    q1, stg, AF.Identity, scale=s_small2[:, 3:4], bias=CC(2)),
                   WSTB((j % 2) * 1024, 1024) + [buf("qsc")], sq)
                sc(lambda e: e.activation(q1, q1, AF.Identity, bias=CC(3)), sq, sq)
                v(lambda e: e.tensor_scalar(q1, q1, -128.0, 127.0, OP.max, OP.min), sq, sq)
                wqt = s_bf[:, WQS:WQS + 1024]
                v(lambda e, wqt=wqt: e.tensor_single_scalar(
                    wqt, q1, s_small2[:, 2:3], OP.mult), sq + [buf("qsc")], BFS(WQS, 1024))
                dmav(dp, wqt, BFS(WQS, 1024), [buf("dwpr")])

            dbg_tap("scales", s_small2[:, 0:8], [buf("qsc")])
            dbg_tap("wqfc", d_wq[0:D, 0:FF], [buf("dwfc")], dt=BF16, shape=[D, FF])
            dbg_tap("wqpr", d_wq[D:D + FF, 0:D], [buf("dwpr")], dt=BF16, shape=[FF, D])
            ln_stats(s_res, buf("x"), DT)

            dmas(s_wst[:, 0:128], tile_kmajor(a_fc),
                 writes=WSTB(0, 128))
            v(lambda e: e.tensor_copy(s_bf[:, AFB:AFB + 128], s_wst[:, 0:128]),
              WSTB(0, 128), BFS(AFB, 128))
            dmas(s_wst[:, 0:512], tile_kmajor(a_proj),
                 writes=WSTB(0, 512))
            v(lambda e: e.tensor_copy(s_bf[:, APR:APR + 512], s_wst[:, 0:512]),
              WSTB(0, 512), BFS(APR, 512))
            for hh in range(2):
                dmas(s_wst[0:16, 0:2048], b_fc_l[:, hh * 2048:(hh + 1) * 2048],
                     writes=WSTB(0, 2048))
                v(lambda e, hh=hh: e.tensor_copy(
                    s_bf[0:16, BFL + hh * 2048:BFL + (hh + 1) * 2048], s_wst[0:16, 0:2048]),
                  WSTB(0, 2048), BFS(BFL + hh * 2048, 2048))
            dmas(s_wst[0:16, 0:1024], b_proj_l, writes=WSTB(0, 1024))
            v(lambda e: e.tensor_copy(s_bf[0:16, BPL:BPL + 1024], s_wst[0:16, 0:1024]),
              WSTB(0, 1024), BFS(BPL, 1024))
            dmas(s_small2[:, 16:48], b_fc, writes=[buf("bfc")])
            dmas(s_small2[:, 48:56], b_proj, writes=[buf("bpr")])

            for c in range(4):
                norm_chunk(s_res, buf("x"), XC, DT, c, 512)
                for k in range(8):
                    mm(pb1[0:16, 0:512], s_bf[:, AFB + k * 16:AFB + (k + 1) * 16],
                       s_bf[:, XC + k * 512:XC + (k + 1) * 512], k == 0, k == 7,
                       BFS(AFB, 128) + BFS(XC, 4096), [buf("pb1")])
                sc(lambda e, c=c: e.activation(
                    s_bf[0:16, R1 + c * 512:R1 + (c + 1) * 512], pb1[0:16, 0:512],
                    AF.Identity, scale=s_const[0:16, 6:7]),
                   [buf("pb1"), buf("cf")], BFS(R1 + c * 512, 512))

            for c in range(4):
                norm_chunk(s_res, buf("x"), XC, DT, c, 512)
                for m in range(32):
                    wq = s_bf[:, WQS + (m % 2) * 1024:WQS + (m % 2) * 1024 + 1024]
                    dmas(wq, tile_kmajor(dfc[:, m * 128:(m + 1) * 128]),
                         [buf("dwfc")], BFS(WQS + (m % 2) * 1024, 1024))
                    pp = pb0 if m % 2 == 0 else pb1
                    ppb = [buf("pb0") if m % 2 == 0 else buf("pb1")]
                    for k in range(8):
                        mm(pp[:], wq[:, k * 128:(k + 1) * 128],
                           s_bf[:, XC + k * 512:XC + (k + 1) * 512],
                           k == 0, False,
                           BFS(WQS + (m % 2) * 1024, 1024) + BFS(XC, 4096), ppb)
                    mm(pp[:], s_bf[0:16, BFL + m * 128:BFL + (m + 1) * 128],
                       s_bf[0:16, R1 + c * 512:R1 + (c + 1) * 512], False, True,
                       BFS(BFL, 4096) + BFS(R1 + c * 512, 512), ppb)
                    dst = s_bf[:, H1 + m * 512:H1 + (m + 1) * 512]
                    sc(lambda e, dst=dst, pp=pp, m=m: e.activation(
                        dst, pp[:], AF.Gelu, bias=s_small2[:, 16 + m:17 + m]),
                       ppb + [buf("bfc")], BFS(H1 + m * 512, 512))
                if c == 0 and taps:
                    dbg_tap("h1c0", s_bf[:, H1:H1 + 16384], BFS(H1, 16384), dt=BF16)
                    dbg_tap("r1", s_bf[0:16, R1:R1 + 2048], BFS(R1, 2048), dt=BF16,
                            shape=[16, 2048])
                for k in range(32):
                    mm(pb1[0:16, 0:512], s_bf[:, APR + k * 16:APR + (k + 1) * 16],
                       s_bf[:, H1 + k * 512:H1 + (k + 1) * 512], k == 0, k == 31,
                       BFS(APR, 512) + BFS(H1, 16384), [buf("pb1")])
                sc(lambda e: e.activation(s_bf[0:16, R2:R2 + 512], pb1[0:16, 0:512],
                                          AF.Identity, scale=s_const[0:16, 6:7]),
                   [buf("pb1"), buf("cf")], BFS(R2, 512))
                for half in range(2):
                    for m in range(32):
                        wq = s_bf[:, WQS + (m % 2) * 512:WQS + (m % 2) * 512 + 512]
                        dmas(wq, dpr[m * 128:(m + 1) * 128, half * 512:(half + 1) * 512],
                             [buf("dwpr")], BFS(WQS + (m % 2) * 512, 512))
                        for dmq in range(4):
                            mm(p_big[:, dmq * 512:(dmq + 1) * 512],
                               wq[:, dmq * 128:(dmq + 1) * 128],
                               s_bf[:, H1 + m * 512:H1 + (m + 1) * 512],
                               m == 0, False,
                               BFS(WQS + (m % 2) * 512, 512) + BFS(H1 + m * 512, 512),
                               PBG(dmq * 512, 512))
                    for dmq in range(4):
                        dm = half * 4 + dmq
                        mm(p_big[:, dmq * 512:(dmq + 1) * 512],
                           s_bf[0:16, BPL + dm * 128:BPL + (dm + 1) * 128],
                           s_bf[0:16, R2:R2 + 512], False, True,
                           BFS(BPL, 1024) + BFS(R2, 512), PBG(dmq * 512, 512))
                        xt = s_res[:, dm * T + c * 512:dm * T + (c + 1) * 512]
                        tbx = WK(3)[:, 0:512]
                        sc(lambda e, tbx=tbx, dmq=dmq, dm=dm: e.activation(
                            tbx, p_big[:, dmq * 512:(dmq + 1) * 512], AF.Identity,
                            bias=s_small2[:, 48 + dm:49 + dm]),
                           PBG(dmq * 512, 512) + [buf("bpr")], WKB(3))
                        v(lambda e, xt=xt, tbx=tbx: e.tensor_tensor(xt, xt, tbx, OP.add),
                          WKB(3) + [buf("x")], [buf("x")])

        if upto in ("l0ffn", "m2", "all"):
            ffn()

        XT2, XD2, XINC2, STT2 = 16384, 18432, 20480, 22528
        BCC2, BCT2, M02, MH2, MV2 = 24576, 24704, 24832, 24960, 25088
        DIN2, YT2, DBC2, SQ2 = 25216, 26240, 28288, 30336
        YCH2 = XD2
        Q = 128

        def m2_layer(l):
            lb = 128 * l
            dmas(s_m2p[:], m2small[lb:lb + 128, :], writes=[buf("m2p")])
            sc(lambda e: e.activation(s_m2p[0:32, 112:113], s_m2p[0:32, 109:110], AF.Exp),
               [buf("m2p")], [buf("m2p")])
            v(lambda e: e.tensor_single_scalar(s_m2p[0:32, 112:113], s_m2p[0:32, 112:113],
                                               -1.0, OP.mult), [buf("m2p")], [buf("m2p")])

            ln_stats(s_res, buf("x"), DT, rms=True)
            for c in range(4):
                norm_chunk(s_res, buf("x"), 0, DT, c, 512, rms=True,
                           dst_stride=T, dst_coff=c * 512)
            for i in range(DT):
                sc(lambda e, i=i: e.activation(
                    s_bf[:, i * T:(i + 1) * T], s_bf[:, i * T:(i + 1) * T],
                    AF.Identity, scale=s_m2p[:, 85 + i:86 + i]),
                   BFS(i * T, T) + [buf("m2p")], BFS(i * T, T))

            wl = in_proj1[l * D:(l + 1) * D, :]
            for m in range(34):
                wdt = 128 if m < 33 else 32
                stg = s_wst[:, (m % 2) * 1024:(m % 2) * 1024 + 8 * wdt]
                dmas(stg, tile_kmajor(wl[:, m * 128:m * 128 + wdt]),
                     writes=WSTB((m % 2) * 1024, 1024))
                wap = s_wsm[:, (m % 2) * 1024:(m % 2) * 1024 + 8 * wdt]
                wmb = buf(f"wsm{m % 2}")
                v(lambda e, wap=wap, stg=stg: e.tensor_copy(wap, stg),
                  WSTB((m % 2) * 1024, 1024), [wmb])
                for c in range(4):
                    pp = p_big[:, (c % 2) * 512:(c % 2) * 512 + 512]
                    ppb = PBG((c % 2) * 512, 512)
                    for k in range(8):
                        mm(pp[0:wdt, :] if wdt < 128 else pp,
                           wap[:, k * wdt:(k + 1) * wdt],
                           s_bf[:, k * T + c * 512:k * T + (c + 1) * 512],
                           k == 0, k == 7, [wmb] + BFS(k * T + c * 512, 512), ppb)
                    if m < 16:
                        zb = s_bf[:, SQ2 + (c % 2) * 512:SQ2 + (c % 2) * 512 + 512]
                        sc(lambda e, zb=zb, pp=pp: e.copy(zb, pp), ppb,
                           BFS(SQ2 + (c % 2) * 512, 512))
                        dmav(d_gate[m * 128:(m + 1) * 128, c * 512:(c + 1) * 512], zb,
                             BFS(SQ2 + (c % 2) * 512, 512), [buf("d_gate")])
                    elif m < 33:
                        dst = convin[:, 4 + c * 512:4 + (c + 1) * 512]
                        sc(lambda e, dst=dst, pp=pp: e.copy(dst, pp), ppb, CVB)
                    else:
                        ch_ = s_xdbl[0:32, c * 512:(c + 1) * 512]
                        ab = s_stats[0:32, 1536:2048]
                        sp = STB(1536, 512)
                        sc(lambda e, ch_=ch_, pp=pp: e.activation(
                            ch_, pp[0:32, :], AF.Relu, bias=s_m2p[0:32, 110:111]),
                           ppb + [buf("m2p")], [buf("xdbl")])
                        sc(lambda e, ab=ab, pp=pp: e.activation(
                            ab, pp[0:32, :], AF.Abs, bias=s_m2p[0:32, 110:111]),
                           ppb + [buf("m2p")], sp)
                        sc(lambda e, ab=ab: e.activation(ab, ab, AF.Exp,
                                                          scale=s_const[0:32, 4:5]),
                           sp + [buf("cf")], sp)
                        sc(lambda e, ab=ab: e.activation(ab, ab, AF.Ln,
                                                         bias=s_const[0:32, 1:2]),
                           sp + [buf("cf")], sp)
                        v(lambda e, ch_=ch_, ab=ab: e.tensor_tensor(ch_, ch_, ab, OP.add),
                          sp + [buf("xdbl")], [buf("xdbl")])
                if 16 <= m < 33:
                    i = m - 16
                    v(lambda e: e.memset(convin[:, 0:4], 0.0), (), CVB)
                    v(lambda e, i=i: e.tensor_copy(s_hx3[:, 51 + 3 * i:54 + 3 * i],
                                                   convin[:, 4:7]), CVB, [buf("hx3")])
                    v(lambda e, i=i: e.tensor_copy(s_hx3[:, 3 * i:3 * i + 3],
                                                   convin[:, 4 + T - 3:4 + T]), CVB, [buf("hx3")])
                    for k in range(4):
                        srcw = convin[:, k + 1:k + 1 + T]
                        tapc = s_m2p[:, 4 * i + k:4 * i + k + 1]
                        if k == 0:
                            v(lambda e, srcw=srcw, tapc=tapc: e.tensor_single_scalar(
                                cacc, srcw, tapc, OP.mult), CVB + [buf("m2p")], CAB)
                        else:
                            v(lambda e, srcw=srcw, tapc=tapc: e.scalar_tensor_tensor(
                                cacc, srcw, tapc, cacc, OP.mult, OP.add),
                              CVB + [buf("m2p")] + CAB, CAB)
                    ub = s_bf[:, XD2:XD2 + T]
                    sc(lambda e, ub=ub, i=i: e.activation(ub, cacc, AF.Silu,
                                                          bias=s_m2p[:, 68 + i:69 + i]),
                       CAB + [buf("m2p")], BFS(XD2, T))
                    if i < 16:
                        dmav(d_u[i * 128:(i + 1) * 128, :], ub, BFS(XD2, T), [buf("d_u")])
                    else:
                        dmav(d_z[0:128, :], ub, BFS(XD2, T), [buf("d_z")])

            if l == 0 and taps:
                dbg_tap("dbcB", s_bf[:, DBC2:DBC2 + 512], BFS(DBC2, 2048), dt=BF16)
            dmag(aghx_in[:, 0:51], s_hx3[:, 0:51], [buf("hx3")], [buf("aghxi")])
            S.cc("AllGather", OP.bypass, [aghx_in], [aghx_out],
                 [buf("aghxi")], [buf("aghxo")], barrier=(agbar_in, agbar_out))
            HB = 102
            v(lambda e: e.memset(s_hx3[:, HB:HB + 51], 0.0), (), [buf("hx3")])
            for j in range(NCORES - 1):
                agj = s_wst[:, 0:51]
                dmag(agj, aghx_out[j * 128:(j + 1) * 128, 0:51],
                     [buf("aghxo")] + WSTB(0, 64), WSTB(0, 64))
                oc = s_cmask[:, 8 + j:9 + j]
                v(lambda e, agj=agj, oc=oc: e.scalar_tensor_tensor(
                    s_hx3[:, HB:HB + 51], agj, oc, s_hx3[:, HB:HB + 51], OP.mult, OP.add),
                  WSTB(0, 64) + [buf("cmask"), buf("hx3")], [buf("hx3")])
            for i in range(17):
                v(lambda e, i=i: e.tensor_copy(s_hx3[:, 153:156],
                                               s_hx3[:, HB + 3 * i:HB + 3 * i + 3]),
                  [buf("hx3")], [buf("hx3")])
                v(lambda e, i=i: e.tensor_copy(s_hx3[:, 156:159],
                                               s_hx3[:, 51 + 3 * i:54 + 3 * i]),
                  [buf("hx3")], [buf("hx3")])
                acc3 = s_qs[:, 96:99]
                for k in range(4):
                    srcw = s_hx3[:, 153 + k:156 + k]
                    tapc = s_m2p[:, 4 * i + k:4 * i + k + 1]
                    if k == 0:
                        v(lambda e, srcw=srcw, tapc=tapc: e.tensor_single_scalar(
                            acc3, srcw, tapc, OP.mult), [buf("hx3"), buf("m2p")], [buf("qs")])
                    else:
                        v(lambda e, srcw=srcw, tapc=tapc: e.scalar_tensor_tensor(
                            acc3, srcw, tapc, acc3, OP.mult, OP.add),
                          [buf("hx3"), buf("m2p"), buf("qs")], [buf("qs")])
                u3 = s_qs[:, 100:103]
                sc(lambda e, i=i, u3=u3: e.activation(u3, acc3, AF.Silu,
                                                      bias=s_m2p[:, 68 + i:69 + i]),
                   [buf("qs"), buf("m2p")], [buf("qs")])
                ub3 = s_bf[:, XD2:XD2 + 4]
                v(lambda e, u3=u3, ub3=ub3: e.tensor_copy(ub3[:, 0:3], u3),
                  [buf("qs")], BFS(XD2, 4))
                if i < 16:
                    dmav(d_u[i * 128:(i + 1) * 128, 0:3], ub3[:, 0:3], BFS(XD2, 4), [buf("d_u")])
                else:
                    dmav(d_z[0:128, 0:3], ub3[:, 0:3], BFS(XD2, 4), [buf("d_z")])

            wo = out_proj1[l * I2:(l + 1) * I2, :]
            for i in range(16):
                stg = s_wst[:, (i % 2) * 1024:(i % 2) * 1024 + D]
                dmas(stg, wo[i * 128:(i + 1) * 128, :], writes=WSTB((i % 2) * 1024, D))
                dst = s_bf[:, i * D:(i + 1) * D]
                v(lambda e, dst=dst, stg=stg: e.tensor_copy(dst, stg),
                  WSTB((i % 2) * 1024, D), BFS(i * D, D))

            if l == 0 and taps:
                dbg_tap("m2xin", d_u, [buf("d_u")], dt=BF16, shape=[E1, T])
                dbg_tap("m2bc", d_z[0:128, :], [buf("d_z")], dt=BF16, shape=[128, T])
                dbg_tap("m2gate", d_gate, [buf("d_gate")], dt=BF16, shape=[I2, T])
                dbg_tap("m2dt", s_xdbl[0:32, :], [buf("xdbl")], shape=[32, T])
                dbg_tap("m2hn", s_bf[:, 0:16384], [buf(f"bf{j}") for j in range(8)], dt=BF16,
                        shape=[128, 16384])
            dmas(WK(0)[:, 0:2048], m2drow[l:l + 1, :].broadcast_to((128, 2048)),
                 writes=WKB(0))
            v(lambda e: e.tensor_copy(s_bf[:, DBC2:DBC2 + 2048], WK(0)[:, 0:2048]),
              WKB(0), BFS(DBC2, 2048))
            CL0 = 2048
            cl_all = s_stats[0:32, CL0:CL0 + 2048]
            ones32 = s_stats[0:32, 0:128]
            v(lambda e: e.memset(ones32, 1.0), (), STB(0, 128))
            for ch in range(NCH2):
                dtach = s_stats[0:32, 128:256]
                v(lambda e, ch=ch, dtach=dtach: e.tensor_single_scalar(
                    dtach, s_xdbl[0:32, ch * Q:(ch + 1) * Q],
                    s_m2p[0:32, 112:113], OP.mult),
                  [buf("xdbl"), buf("m2p")], STB(128, 128))
                v(lambda e, ch=ch, dtach=dtach: e.tensor_tensor_scan(
                    s_stats[0:32, CL0 + ch * Q:CL0 + (ch + 1) * Q], ones32,
                    dtach, 0.0, OP.mult, OP.add),
                  STB(0, 128) + STB(128, 128), STB(CL0 + ch * Q, Q))

            def load_chunk_commons(ch, need_xhd):
                dmas(s_bf[:, XINC2:XINC2 + 2048],
                     tile_kmajor(d_u[:, ch * Q:(ch + 1) * Q]),
                     [buf("d_u")], BFS(XINC2, 2048))
                dmas(s_bf[:, BCC2:BCC2 + 128], d_z[0:128, ch * Q:(ch + 1) * Q],
                     [buf("d_z")], BFS(BCC2, 128))
                for i in range(16):
                    te(lambda e, i=i: e.transpose(
                        pt_b[:, 0:128], s_bf[:, XINC2 + i * 128:XINC2 + (i + 1) * 128],
                        s_identb[:]),
                       BFS(XINC2 + i * 128, 128) + [buf("identb")], [buf("ptb")])
                    sc(lambda e, i=i: e.copy(s_bf[:, XT2 + i * 128:XT2 + (i + 1) * 128],
                                             pt_b[:, 0:128]),
                       [buf("ptb")], BFS(XT2 + i * 128, 128))
                te(lambda e: e.transpose(pt_b[:, 0:128], s_bf[:, BCC2:BCC2 + 128],
                                         s_identb[:]),
                   BFS(BCC2, 128) + [buf("identb")], [buf("ptb")])
                sc(lambda e: e.copy(s_bf[:, BCT2:BCT2 + 128], pt_b[:, 0:128]),
                   [buf("ptb")], BFS(BCT2, 128))
                dmas(s_bf[0:64, SQ2 + 1536:SQ2 + 1664], s_bf[64:128, BCC2:BCC2 + 128],
                     BFS(BCC2, 128), BFS(SQ2 + 1536, 128))
                clc = s_stats[0:32, 2048 + ch * Q:2048 + (ch + 1) * Q]
                te(lambda e, clc=clc: e.transpose(pt_f[:, 0:32], clc, s_ident[0:32, 0:32]),
                   STB(2048 + ch * Q, Q) + [buf("ident")], [buf("ptf")])
                sc(lambda e: e.copy(WK(3)[:, 0:32], pt_f[:, 0:32]), [buf("ptf")], WKB(3))
                dtc = s_xdbl[0:32, ch * Q:(ch + 1) * Q]
                te(lambda e, dtc=dtc: e.transpose(pt_f[:, 32:64], dtc, s_ident[0:32, 0:32]),
                   [buf("xdbl"), buf("ident")], [buf("ptf")])
                sc(lambda e: e.copy(WK(3)[:, 64:96], pt_f[:, 32:64]), [buf("ptf")], WKB(3))
                if need_xhd:
                    dec = s_stats[0:32, 256:384]
                    sc(lambda e, clc=clc, ch=ch: e.activation(
                        dec, clc, AF.Exp, scale=s_const[0:32, 4:5],
                        bias=s_stats[0:32, 2048 + ch * Q + Q - 1:2048 + ch * Q + Q]),
                       STB(2048 + ch * Q, Q) + [buf("cf")], STB(256, 128))
                    v(lambda e, dtc=dtc: e.tensor_tensor(dec, dec, dtc, OP.mult),
                      [buf("xdbl")] + STB(256, 128), STB(256, 128))
                    te(lambda e: e.transpose(pt_f[:, 64:96], dec, s_ident[0:32, 0:32]),
                       STB(256, 128) + [buf("ident")], [buf("ptf")])
                    sc(lambda e: e.copy(WK(3)[:, 32:64], pt_f[:, 64:96]),
                       [buf("ptf")], WKB(3))
                fold_col = 32 if need_xhd else 64
                for pt in range(16):
                    for hh in range(2):
                        h_ = 2 * pt + hh
                        v(lambda e, pt=pt, hh=hh, h_=h_: e.tensor_single_scalar(
                            s_bf[:, XD2 + pt * 128 + hh * 64:XD2 + pt * 128 + hh * 64 + 64],
                            s_bf[:, XT2 + pt * 128 + hh * 64:XT2 + pt * 128 + hh * 64 + 64],
                            WK(3)[:, fold_col + h_:fold_col + h_ + 1], OP.mult),
                          BFS(XT2 + pt * 128, 128) + WKB(3), BFS(XD2 + pt * 128, 128))

            def acols(ch, off):
                sc(lambda e, ch=ch: e.copy(s_qs[0:32, 104:105],
                                           s_stats[0:32, 2048 + ch * Q + Q - 1:2048 + ch * Q + Q]),
                   STB(2048 + ch * Q, Q), [buf("qs")])
                dmav(d_row[3:4, 0:32], s_qs[0:32, 104:105], [buf("qs")], [buf("drow")])
                base = d_row[3:4, 0:32]
                src_top = _ap3(base, [[0, 64], [2, 16], [1, 1]])
                src_bot = _ap3(base, [[0, 64], [2, 16], [1, 1]], offset_extra=1)
                dmas(s_m2a[0:64, off:off + 16], src_top, [buf("drow")], [buf("m2a")], slow=True)
                dmas(s_m2a[64:128, off:off + 16], src_bot, [buf("drow")], [buf("m2a")], slow=True)
                sc(lambda e, off=off: e.activation(s_m2a[:, off:off + 16],
                                                   s_m2a[:, off:off + 16], AF.Exp),
                   [buf("m2a")], [buf("m2a")])

            Lst = WK(1)
            v(lambda e: e.memset(Lst[:, 0:2048], 0.0), (), WKB(1))
            v(lambda e: e.memset(s_qs[:, 105:106], 0.0), (), [buf("qs")])
            st_park_dst = lambda ch: _ap3(d_m2st[ch * 2048:(ch + 1) * 2048, :],
                                          [[64, 128], [128 * 64, 16], [1, 64]])
            for ch in range(NCH2):
                load_chunk_commons(ch, need_xhd=True)
                acols(ch, 0)
                dmav(st_park_dst(ch),
                     _ap3(Lst[:, 0:1024], [list(Lst.ap[0]), [64, 16], [1, 64]]),
                     WKB(1), [buf("d_m2st")])
                sc(lambda e, ch=ch: e.add(s_qs[0:32, 105:106], s_qs[0:32, 105:106],
                                          s_stats[0:32, 2048 + ch * Q + Q - 1:2048 + ch * Q + Q]),
                   STB(2048 + ch * Q, Q) + [buf("qs")], [buf("qs")])
                for pt in range(16):
                    pp = pb0 if pt % 2 == 0 else pb1
                    ppb = [buf("pb0") if pt % 2 == 0 else buf("pb1")]
                    mm(pp[:, 0:64], s_bf[:, XD2 + pt * 128:XD2 + (pt + 1) * 128],
                       s_bf[:, BCT2:BCT2 + 64], True, True,
                       BFS(XD2 + pt * 128, 128) + BFS(BCT2, 128), ppb)
                    v(lambda e, pt=pt, pp=pp, ch=ch: e.scalar_tensor_tensor(
                        Lst[:, pt * 64:(pt + 1) * 64], Lst[:, pt * 64:(pt + 1) * 64],
                        s_m2a[:, pt:pt + 1],
                        pp[:, 0:64], OP.mult, OP.add),
                      ppb + [buf("m2a")] + WKB(1), WKB(1))
            if l == 0 and taps:
                dbg_tap("m2cl", s_stats[0:32, 2048:4096], STB(2048, 2048), shape=[32, 2048])
                dbg_tap("m2Lend", Lst[:, 0:1024], WKB(1), shape=[128, 1024])
            if l == 0 and taps:
                dbg_tap("dbcC", s_bf[:, DBC2:DBC2 + 512], BFS(DBC2, 2048), dt=BF16)
            dmav(ag2_in[l][:, 0:1024], Lst[:, 0:1024], WKB(1), [buf("ag2i")])
            dmav(ag2_in[l][0:32, 1024:1025], s_qs[0:32, 105:106], [buf("qs")], [buf("ag2i")],
                 slow=True)
            S.cc("AllGather", OP.bypass, [ag2_in[l]], [ag2_out[l]],
                 [buf("ag2i")], [buf("ag2o")], barrier=(agbar_in, agbar_out))
            for j in range(NCORES - 1):
                stj = s_stats[:, 0:1024]
                dmag(stj, ag2_out[l][j * 128:(j + 1) * 128, 0:1024],
                     [buf("ag2o")] + STB(0, 1024), STB(0, 1024))
                basej = ag2_out[l][j * 128:j * 128 + 1, 1024:1025]
                srcjt = _ap3(basej, [[0, 64], [2 * 1056, 16], [1, 1]])
                srcjb = _ap3(basej, [[0, 64], [2 * 1056, 16], [1, 1]], offset_extra=1056)
                dmag(s_qs[0:64, 112:128], srcjt, [buf("ag2o")], [buf("qs")], slow=True)
                dmag(s_qs[64:128, 112:128], srcjb, [buf("ag2o")], [buf("qs")], slow=True)
                sc(lambda e: e.activation(s_qs[:, 112:128], s_qs[:, 112:128], AF.Exp),
                   [buf("qs")], [buf("qs")])
                mcol = s_cmask[:, j:j + 1]
                v(lambda e, mcol=mcol: e.tensor_scalar(
                    s_qs[:, 112:128], s_qs[:, 112:128], 1.0, mcol, OP.subtract, OP.mult),
                  [buf("qs"), buf("cmask")], [buf("qs")])
                v(lambda e: e.tensor_single_scalar(s_qs[:, 112:128], s_qs[:, 112:128],
                                                   1.0, OP.add), [buf("qs")], [buf("qs")])
                for i in range(16):
                    v(lambda e, i=i: e.tensor_single_scalar(
                        Lst[:, 1024 + i * 64:1024 + (i + 1) * 64],
                        Lst[:, 1024 + i * 64:1024 + (i + 1) * 64],
                        s_qs[:, 112 + i:113 + i], OP.mult),
                      [buf("qs")] + WKB(1), WKB(1))
                    v(lambda e, i=i, stj=stj, mcol=mcol: e.scalar_tensor_tensor(
                        Lst[:, 1024 + i * 64:1024 + (i + 1) * 64],
                        stj[:, i * 64:(i + 1) * 64], mcol,
                        Lst[:, 1024 + i * 64:1024 + (i + 1) * 64], OP.mult, OP.add),
                      STB(0, 1024) + [buf("cmask")] + WKB(1), WKB(1))

            rst = WK(2)
            v(lambda e: e.tensor_copy(rst[:, 0:1024], Lst[:, 1024:2048]), WKB(1), WKB(2))
            for ch in range(NCH2):
                load_chunk_commons(ch, need_xhd=False)
                if ch > 0:
                    acols(ch - 1, 16)
                    for i in range(16):
                        v(lambda e, i=i: e.tensor_single_scalar(
                            rst[:, i * 64:(i + 1) * 64], rst[:, i * 64:(i + 1) * 64],
                            s_m2a[:, 16 + i:17 + i], OP.mult),
                          [buf("m2a")] + WKB(2), WKB(2))
                stl = s_stats[:, 0:1024]
                dmas(stl, st_park_dst(ch), [buf("d_m2st")] + STB(0, 1024), STB(0, 1024))
                tc = WK(0)
                v(lambda e, stl=stl: e.tensor_tensor(tc[:, 0:1024], stl, rst[:, 0:1024],
                                                     OP.add),
                  STB(0, 1024) + WKB(2), WKB(0))
                for pt in range(16):
                    v(lambda e, pt=pt: e.tensor_copy(
                        s_bf[:, SQ2 + 1024 + 128 * (pt % 2):SQ2 + 1024 + 128 * (pt % 2) + 64],
                        tc[:, pt * 64:(pt + 1) * 64]),
                      WKB(0), BFS(SQ2 + 1024 + 128 * (pt % 2), 128))
                    te(lambda e, pt=pt: e.transpose(
                        pt_b[0:64, 128:256],
                        s_bf[:, SQ2 + 1024 + 128 * (pt % 2):SQ2 + 1024 + 128 * (pt % 2) + 64],
                        s_identb[:]),
                       BFS(SQ2 + 1024 + 128 * (pt % 2), 128) + [buf("identb")], [buf("ptb")])
                    sc(lambda e, pt=pt: e.copy(
                        s_bf[0:64, STT2 + pt * 128:STT2 + (pt + 1) * 128],
                        pt_b[0:64, 128:256]),
                       [buf("ptb")], BFS(STT2 + pt * 128, 128))
                mm(pt_f[:, 128:256], s_bf[0:64, BCC2:BCC2 + 128],
                   s_bf[0:64, SQ2 + 1536:SQ2 + 1664], True, True,
                   BFS(BCC2, 128) + BFS(SQ2 + 1536, 128), [buf("ptf")])
                v(lambda e: e.tensor_tensor(s_bf[:, M02:M02 + 128], pt_f[:, 128:256],
                                            s_tri[:], OP.mult),
                  [buf("ptf"), buf("tri")], BFS(M02, 128))
                v(lambda e: e.tensor_single_scalar(WK(3)[:, 96:128], WK(3)[:, 0:32],
                                                   -1.0, OP.mult), WKB(3), WKB(3))
                dmav(d_row[2:3, 0:4096], s_stats[0:32, 2048 + ch * Q:2048 + (ch + 1) * Q],
                     STB(2048 + ch * Q, Q), [buf("drow")])
                for grp in range(4):
                    clbc = WK(0)[:, 1024:2048]
                    srcg = _ap3(d_row[2:3, 0:4096],
                                [[0, 128], [128, 8], [1, 128]], offset_extra=grp * 8 * 128)
                    dmas(clbc, srcg, [buf("drow")] + WKB(0), WKB(0))
                    sc(lambda e, clbc=clbc: e.activation(
                        s_bf[:, DIN2:DIN2 + 1024], clbc, AF.Exp),
                       WKB(0), BFS(DIN2, 1024))
                    for hh in range(8):
                        h_ = grp * 8 + hh
                        pt = h_ // 2
                        argt = WK(3)[:, 128:256]
                        v(lambda e, hh=hh, h_=h_, clbc=clbc, argt=argt: e.tensor_scalar(
                            argt, clbc[:, hh * 128:(hh + 1) * 128],
                            WK(3)[:, 96 + h_:97 + h_], 0.0, OP.add, OP.min),
                          WKB(0) + WKB(3), WKB(3))
                        sc(lambda e, argt=argt: e.activation(
                            s_bf[:, MH2:MH2 + 128], argt, AF.Exp),
                           WKB(3), BFS(MH2, 128))
                        if l == 0 and ch == 0 and h_ == 0 and taps:
                            dbg_tap("m2stT", s_bf[0:64, STT2:STT2 + 2048],
                                    [buf(f"bf{j}") for j in range(STT2 // 2048, (STT2 + 2048) // 2048 + 1)],
                                    dt=BF16, shape=[64, 2048])
                            dbg_tap("m2mv0", s_bf[0:64, MV2:MV2 + 128], BFS(MV2, 128),
                                    dt=BF16, shape=[64, 128])
                            dbg_tap("m2arg0", argt, WKB(3))
                            dbg_tap("m2lp0", s_bf[:, MH2:MH2 + 128], BFS(MH2, 128), dt=BF16)
                            dbg_tap("m2xhat0", s_bf[:, XD2:XD2 + 128], BFS(XD2, 128), dt=BF16)
                            dbg_tap("m2m00", s_bf[:, M02:M02 + 128], BFS(M02, 128), dt=BF16)
                        MH2B = SQ2 + 1792
                        v(lambda e: e.tensor_tensor(s_bf[:, MH2B:MH2B + 128],
                                                    s_bf[:, MH2:MH2 + 128],
                                                    s_bf[:, M02:M02 + 128], OP.mult),
                          BFS(M02, 128) + BFS(MH2, 128), BFS(MH2B, 128))
                        v(lambda e, hh=hh: e.tensor_tensor(
                            s_bf[0:64, MV2:MV2 + 128], s_bf[0:64, SQ2 + 1536:SQ2 + 1664],
                            s_bf[0:64, DIN2 + hh * 128:DIN2 + (hh + 1) * 128], OP.mult),
                          BFS(SQ2 + 1536, 128) + BFS(DIN2, 1024), BFS(MV2, 128))
                        bank = h_ % 4
                        pslot = p_big[:, bank * 512:bank * 512 + 64]
                        ppb = PBG(bank * 512, 512)
                        mm(pslot, s_bf[:, MH2B:MH2B + 128],
                           s_bf[:, XD2 + pt * 128 + (h_ % 2) * 64:XD2 + pt * 128 + (h_ % 2) * 64 + 64],
                           True, TEST_NO_INTER, BFS(MH2B, 128) + BFS(XD2 + pt * 128, 128), ppb)
                        if not TEST_NO_INTER:
                            mm(pslot, s_bf[0:64, MV2:MV2 + 128],
                               s_bf[0:64, STT2 + pt * 128 + (h_ % 2) * 64:STT2 + pt * 128 + (h_ % 2) * 64 + 64],
                               False, True, BFS(MV2, 128) + BFS(STT2 + pt * 128, 128), ppb)
                        v(lambda e, pt=pt, h_=h_: e.tensor_tensor(
                            s_bf[:, SQ2 + 1280:SQ2 + 1344],
                            s_bf[:, XT2 + pt * 128 + (h_ % 2) * 64:XT2 + pt * 128 + (h_ % 2) * 64 + 64],
                            s_bf[:, DBC2 + h_ * 64:DBC2 + (h_ + 1) * 64], OP.mult),
                          BFS(XT2 + pt * 128, 128) + BFS(DBC2, 2048), BFS(SQ2 + 1280, 64))
                        if l == 0 and ch == 0 and h_ == 0 and taps:
                            sc(lambda e: e.copy(s_stats[:, 1152:1216], pslot),
                               ppb, STB(1152, 64))
                            dbg_tap("m2psum0", s_stats[:, 1152:1216], STB(1152, 64))
                            dbg_tap("m2mh0", s_bf[:, SQ2 + 1792:SQ2 + 1920], BFS(SQ2 + 1792, 128), dt=BF16)
                            dbg_tap("m2tmp0", s_bf[:, SQ2 + 1280:SQ2 + 1344], BFS(SQ2 + 1280, 64), dt=BF16)
                            dbg_tap("m2dbc", s_bf[:, DBC2:DBC2 + 128], BFS(DBC2, 2048), dt=BF16)
                        v(lambda e, h_=h_, pslot=pslot: e.tensor_tensor(
                            s_bf[:, YT2 + h_ * 64:YT2 + (h_ + 1) * 64],
                            pslot, s_bf[:, SQ2 + 1280:SQ2 + 1344], OP.add),
                          ppb + BFS(SQ2 + 1280, 64), BFS(YT2 + h_ * 64, 64))
                if l == 0 and ch == 0 and taps:
                    dbg_tap("m2yt0", s_bf[:, YT2:YT2 + 2048], BFS(YT2, 2048), dt=BF16,
                            shape=[128, 2048])
                    dbg_tap("m2tc0", tc[:, 0:1024], WKB(0), shape=[128, 1024])
                for pt in range(16):
                    te(lambda e, pt=pt: e.transpose(
                        pt_b[:, 0:128], s_bf[:, YT2 + pt * 128:YT2 + (pt + 1) * 128],
                        s_identb[:]),
                       BFS(YT2 + pt * 128, 128) + [buf("identb")], [buf("ptb")])
                    sc(lambda e, pt=pt: e.copy(
                        s_bf[:, YCH2 + pt * 128:YCH2 + (pt + 1) * 128], pt_b[:, 0:128]),
                       [buf("ptb")], BFS(YCH2 + pt * 128, 128))
                dmas(s_bf[:, XINC2:XINC2 + 2048],
                     tile_kmajor(d_gate[:, ch * Q:(ch + 1) * Q]),
                     [buf("d_gate")], BFS(XINC2, 2048))
                for i in range(16):
                    sg = s_bf[:, MH2:MH2 + 128]
                    sc(lambda e, i=i, sg=sg: e.activation(
                        sg, s_bf[:, XINC2 + i * 128:XINC2 + (i + 1) * 128], AF.Silu),
                       BFS(XINC2 + i * 128, 128), BFS(MH2, 128))
                    v(lambda e, i=i, sg=sg: e.tensor_tensor(
                        s_stats[:, i * 128:(i + 1) * 128],
                        s_bf[:, YCH2 + i * 128:YCH2 + (i + 1) * 128], sg, OP.mult),
                      BFS(YCH2 + i * 128, 128) + BFS(MH2, 128), STB(i * 128, 128))
                    sqb2 = s_bf[:, SQ2 + 1408:SQ2 + 1536]
                    sc(lambda e, i=i, sqb2=sqb2: e.activation(
                        sqb2, s_stats[:, i * 128:(i + 1) * 128], AF.Square),
                       STB(i * 128, 128), BFS(SQ2 + 1408, 128))
                    mm(pt_f[:, 128:256], s_onesb[:], sqb2, i == 0, i == 15,
                       BFS(SQ2 + 1408, 128) + [buf("onesb")], [buf("ptf")])
                rsr = WK(3)[:, 128:256]
                v(lambda e, rsr=rsr: e.tensor_single_scalar(
                    rsr, pt_f[:, 128:256], 1.0 / I2, OP.mult),
                  [buf("ptf")], WKB(3))
                sc(lambda e, rsr=rsr: e.activation(rsr, rsr, AF.Ln, bias=CC(0)),
                   WKB(3) + [buf("cf")], WKB(3))
                sc(lambda e, rsr=rsr: e.activation(rsr, rsr, AF.Exp, scale=CC(5)),
                   WKB(3) + [buf("cf")], WKB(3))
                for i in range(16):
                    tmpf = WK(3)[:, 0:128]
                    v(lambda e, i=i, tmpf=tmpf, rsr=rsr: e.tensor_tensor(
                        tmpf, s_stats[:, i * 128:(i + 1) * 128], rsr, OP.mult),
                      STB(i * 128, 128) + WKB(3), WKB(3))
                    sc(lambda e, i=i, tmpf=tmpf: e.activation(
                        s_bf[:, XT2 + i * 128:XT2 + (i + 1) * 128], tmpf,
                        AF.Identity, scale=s_m2p[:, 93 + i:94 + i]),
                       WKB(3) + [buf("m2p")], BFS(XT2 + i * 128, 128))
                if l == 0 and ch == 0 and taps:
                    dbg_tap("m2ys0", s_stats[:, 0:2048], STB(0, 2048), shape=[128, 2048])
                    dbg_tap("m2yhat0", s_bf[:, XT2:XT2 + 2048], BFS(XT2, 2048), dt=BF16,
                            shape=[128, 2048])
                for m_ in range(8):
                    pp = pb0 if m_ % 2 == 0 else pb1
                    ppb = [buf("pb0") if m_ % 2 == 0 else buf("pb1")]
                    for k in range(16):
                        mm(pp[:, 0:128], s_bf[:, k * D + m_ * 128:k * D + (m_ + 1) * 128],
                           s_bf[:, XT2 + k * 128:XT2 + (k + 1) * 128],
                           k == 0, k == 15,
                           BFS(k * D, D) + BFS(XT2 + k * 128, 128), ppb)
                    xt_ = s_res[:, m_ * T + ch * Q:m_ * T + (ch + 1) * Q]
                    v(lambda e, xt_=xt_, pp=pp: e.tensor_tensor(xt_, xt_, pp[:, 0:128],
                                                                OP.add),
                      ppb + [buf("x")], [buf("x")])

        def m2_final():
            ln_stats(s_res, buf("x"), DT, rms=True)
            dmas(s_qs[:, 128:136] if False else s_qs[:, 96:104], normf_cols,
                 writes=[buf("qs")])
            for c in range(4):
                for i in range(DT):
                    tmpf = WK(3)[:, 0:512]
                    v(lambda e, i=i, c=c, tmpf=tmpf: e.tensor_tensor(
                        tmpf, s_res[:, i * T + c * 512:i * T + (c + 1) * 512],
                        s_stats[:, T + c * 512:T + (c + 1) * 512], OP.mult),
                      [buf("x")] + STB(2048, 2048), WKB(3))
                    sc(lambda e, i=i, tmpf=tmpf: e.activation(
                        tmpf, tmpf, AF.Identity, scale=s_qs[:, 96 + i:97 + i]),
                       WKB(3) + [buf("qs")], WKB(3))
                    xl = WK(0)[:, 0:512]
                    dmas(xl, d_xpark[i * 128:(i + 1) * 128, c * 512:(c + 1) * 512],
                         [buf("d_xp")] + WKB(0), WKB(0))
                    v(lambda e, i=i, c=c, tmpf=tmpf, xl=xl: e.tensor_tensor(
                        s_res[:, i * T + c * 512:i * T + (c + 1) * 512], xl, tmpf, OP.add),
                      WKB(0) + WKB(3), [buf("x")])

        if upto in ("m2", "all"):
            dmav(d_xpark, tile_kmajor(d_xpark) if False else d_xpark, [buf("x")], [buf("d_xp")]) if False else None
            dmav(tile_kmajor(d_xpark), s_res[:], [buf("x")], [buf("d_xp")])
            ln_stats(s_res, buf("x"), DT)
            for c in range(4):
                for i in range(DT):
                    tmpf = WK(3)[:, 0:512]
                    v(lambda e, i=i, c=c, tmpf=tmpf: e.tensor_tensor(
                        tmpf, s_res[:, i * T + c * 512:i * T + (c + 1) * 512],
                        s_stats[:, c * 512:(c + 1) * 512], OP.subtract),
                      [buf("x")] + STB(0, 4096), WKB(3))
                    v(lambda e, i=i, c=c, tmpf=tmpf: e.tensor_tensor(
                        s_res[:, i * T + c * 512:i * T + (c + 1) * 512], tmpf,
                        s_stats[:, T + c * 512:T + (c + 1) * 512], OP.mult),
                      WKB(3) + STB(0, 4096), [buf("x")])
            dbg_tap("m2h", s_res[:], [buf("x")])
            m2_layer(0)
            dbg_tap("m2hL0", s_res[:], [buf("x")])
            m2_layer(1)
            m2_final()

        def moe():
            for e in range(NEXP):
                for kk in range(8):
                    for hcol in range(2):
                        par = (kk * 2 + hcol) % 2
                        stg = s_wst[:, par * 1024:(par + 1) * 1024]
                        dmas(stg, w1e[e, kk * 128:(kk + 1) * 128,
                                      hcol * 1024:(hcol + 1) * 1024],
                             writes=WSTB(par * 1024, 1024))
                        wqt = s_wsm[:, par * 1024:(par + 1) * 1024]
                        v(lambda en, stg=stg, wqt=wqt: en.tensor_copy(wqt, stg),
                          WSTB(par * 1024, 1024), [buf(f"wsm{par}")])
                        dmav(d_wqm[e * 3072 + kk * 128:e * 3072 + (kk + 1) * 128,
                                   hcol * 1024:(hcol + 1) * 1024], wqt,
                             [buf(f"wsm{par}")], [buf("dwqmA")])
            for e in range(NEXP):
                for kk in range(16):
                    par = kk % 2
                    stg = s_wst[:, par * 1024:(par + 1) * 1024]
                    dmas(stg, w2e[e, kk * 128:(kk + 1) * 128, :],
                         writes=WSTB(par * 1024, 1024))
                    wqt = s_wsm[:, par * 1024:(par + 1) * 1024]
                    v(lambda en, stg=stg, wqt=wqt: en.tensor_copy(wqt, stg),
                      WSTB(par * 1024, 1024), [buf(f"wsm{par}")])
                    for dm in range(8):
                        dmav(d_wqm[e * 3072 + 1024 + dm * 128:
                                   e * 3072 + 1024 + (dm + 1) * 128,
                                   kk * 128:(kk + 1) * 128],
                             wqt[:, dm * 128:(dm + 1) * 128],
                             [buf(f"wsm{par}")], [buf("dwqmB")], slow=True)
            dmas(s_wst[0:8, 0:1024], b2e, writes=WSTB(0, 1024))
            v(lambda en: en.tensor_copy(s_bf[0:8, 30720:31744], s_wst[0:8, 0:1024]),
              WSTB(0, 1024), BFS(30720, 1024))
            dmas(s_m2p[:], b1cols, writes=[buf("m2p")])
            dmas(s_hx3[:, 64:128], tile_kmajor(wg), writes=[buf("hx3")])

            ln_stats(s_res, buf("x"), DT)
            for c in range(4):
                norm_chunk(s_res, buf("x"), 0, DT, c, 512)
                for i in range(DT):
                    dmav(d_u[i * 128:(i + 1) * 128, c * 512:(c + 1) * 512],
                         s_bf[:, i * 512:(i + 1) * 512],
                         BFS(0, 4096), [buf("d_u")])

            for j in range(16):
                te(lambda en, j=j: en.transpose(pt_f[:, 128:256],
                                                s_stats[:, j * 128:(j + 1) * 128],
                                                s_ident[:]),
                   STB(j * 128, 128) + [buf("ident")], [buf("ptf")])
                sc(lambda en, j=j: en.copy(s_hx3[:, 128 + j:129 + j], pt_f[:, 128:129]),
                   [buf("ptf")], [buf("hx3")])
                te(lambda en, j=j: en.transpose(pt_f[:, 128:256],
                                                s_stats[:, T + j * 128:T + (j + 1) * 128],
                                                s_ident[:]),
                   STB(T + j * 128, 128) + [buf("ident")], [buf("ptf")])
                sc(lambda en, j=j: en.copy(s_hx3[:, 144 + j:145 + j], pt_f[:, 128:129]),
                   [buf("ptf")], [buf("hx3")])

            onesf = s_qs[:, 174:175]
            v(lambda en, onesf=onesf: en.memset(onesf, 1.0), (), [buf("qs")])
            for kk in range(8):
                mm(pt_f[0:8, 0:1], s_hx3[:, 64 + kk * 8:72 + kk * 8], onesf,
                   kk == 0, kk == 7, [buf("hx3"), buf("qs")], [buf("ptf")])
            sc(lambda en: en.copy(s_qs[0:8, 175:176], pt_f[0:8, 0:1]),
               [buf("ptf")], [buf("qs")])
            dmav(d_row[4:5, 0:8], s_qs[0:8, 175:176], [buf("qs")], [buf("drow")],
                 slow=True)
            dmas(s_qs[:, 160:168], d_row[4:5, 0:8].broadcast_to((128, 8)),
                 [buf("drow")], [buf("qs")])
            v(lambda en: en.tensor_single_scalar(s_qs[:, 160:168], s_qs[:, 160:168],
                                                 -1.0, OP.mult),
              [buf("qs")], [buf("qs")])

            def Q(k, a, b):
                return s_qs[:, k * 48 + a:k * 48 + b]
            for i in range(8):
                pjs = [(0, 2 * i, pb0), (1, 2 * i + 1, pb1)]
                for k, j, pp in pjs:
                    ppb = [buf("pb0") if k == 0 else buf("pb1")]
                    for kk in range(8):
                        mm(pp[:, 0:8], s_res[:, kk * T + j * 128:kk * T + (j + 1) * 128],
                           s_hx3[:, 64 + kk * 8:72 + kk * 8], kk == 0, kk == 7,
                           [buf("x"), buf("hx3")], ppb)
                def st(fn, reads, writes):
                    for k, j, pp in pjs:
                        fn(k, j, pp, [buf("pb0") if k == 0 else buf("pb1")],
                           reads, writes)
                qb = [buf("qs")]
                st(lambda k, j, pp, ppb, r_, w_: v(
                    lambda en, k=k, j=j, pp=pp: en.scalar_tensor_tensor(
                        Q(k, 0, 8), s_qs[:, 160:168], s_hx3[:, 128 + j:129 + j],
                        pp[:, 0:8], OP.mult, OP.add),
                    qb + [buf("hx3")] + ppb, qb), (), ())
                st(lambda k, j, pp, ppb, r_, w_: v(
                    lambda en, k=k, j=j: en.tensor_single_scalar(
                        Q(k, 0, 8), Q(k, 0, 8), s_hx3[:, 144 + j:145 + j], OP.mult),
                    qb + [buf("hx3")], qb), (), ())
                st(lambda k, j, pp, ppb, r_, w_: v(
                    lambda en, k=k: en.tensor_reduce(Q(k, 32, 33), Q(k, 0, 8),
                                                     AX, OP.max), qb, qb), (), ())
                st(lambda k, j, pp, ppb, r_, w_: v(
                    lambda en, k=k: en.tensor_single_scalar(
                        Q(k, 0, 8), Q(k, 0, 8), Q(k, 32, 33), OP.subtract),
                    qb, qb), (), ())
                st(lambda k, j, pp, ppb, r_, w_: sc(
                    lambda en, k=k: en.activation(Q(k, 0, 8), Q(k, 0, 8), AF.Exp),
                    qb, qb), (), ())
                st(lambda k, j, pp, ppb, r_, w_: v(
                    lambda en, k=k: en.tensor_reduce(Q(k, 33, 34), Q(k, 0, 8),
                                                     AX, OP.add), qb, qb), (), ())
                st(lambda k, j, pp, ppb, r_, w_: sc(
                    lambda en, k=k: en.activation(Q(k, 33, 34), Q(k, 33, 34), AF.Ln),
                    qb, qb), (), ())
                st(lambda k, j, pp, ppb, r_, w_: sc(
                    lambda en, k=k: en.activation(Q(k, 33, 34), Q(k, 33, 34), AF.Exp,
                                                  scale=CC(4)),
                    qb + [buf("cf")], qb), (), ())
                st(lambda k, j, pp, ppb, r_, w_: v(
                    lambda en, k=k: en.tensor_single_scalar(
                        Q(k, 8, 16), Q(k, 0, 8), Q(k, 33, 34), OP.mult),
                    qb, qb), (), ())
                st(lambda k, j, pp, ppb, r_, w_: v(
                    lambda en, k=k: en.tensor_reduce(Q(k, 32, 33), Q(k, 8, 16),
                                                     AX, OP.max), qb, qb), (), ())
                st(lambda k, j, pp, ppb, r_, w_: v(
                    lambda en, k=k: en.tensor_single_scalar(
                        Q(k, 16, 24), Q(k, 8, 16), Q(k, 32, 33), OP.subtract),
                    qb, qb), (), ())
                st(lambda k, j, pp, ppb, r_, w_: v(
                    lambda en, k=k: en.tensor_single_scalar(
                        Q(k, 24, 32), Q(k, 16, 24), 0.0, OP.is_ge), qb, qb), (), ())
                st(lambda k, j, pp, ppb, r_, w_: v(
                    lambda en, k=k: en.scalar_tensor_tensor(
                        Q(k, 16, 24), Q(k, 24, 32), -2.0, Q(k, 8, 16),
                        OP.mult, OP.add), qb, qb), (), ())
                st(lambda k, j, pp, ppb, r_, w_: v(
                    lambda en, k=k: en.tensor_reduce(Q(k, 34, 35), Q(k, 16, 24),
                                                     AX, OP.max), qb, qb), (), ())
                st(lambda k, j, pp, ppb, r_, w_: v(
                    lambda en, k=k: en.tensor_single_scalar(
                        Q(k, 16, 24), Q(k, 16, 24), Q(k, 34, 35), OP.subtract),
                    qb, qb), (), ())
                st(lambda k, j, pp, ppb, r_, w_: v(
                    lambda en, k=k: en.tensor_single_scalar(
                        Q(k, 8, 16), Q(k, 16, 24), 0.0, OP.is_ge), qb, qb), (), ())
                st(lambda k, j, pp, ppb, r_, w_: v(
                    lambda en, k=k: en.tensor_single_scalar(
                        Q(k, 35, 43), Q(k, 24, 32), Q(k, 32, 33), OP.mult),
                    qb, qb), (), ())
                st(lambda k, j, pp, ppb, r_, w_: v(
                    lambda en, k=k: en.scalar_tensor_tensor(
                        Q(k, 35, 43), Q(k, 8, 16), Q(k, 34, 35), Q(k, 35, 43),
                        OP.mult, OP.add), qb, qb), (), ())
                for k, j, pp in pjs:
                    dbg_tap(f"moe_msk{j}", Q(k, 35, 43), [buf("qs")], shape=[128, 8])
                    te(lambda en, k=k: en.transpose(pt_f[0:8, 128:256], Q(k, 35, 43),
                                                    s_ident[:]),
                       qb + [buf("ident")], [buf("ptf")])
                    sc(lambda en, j=j: en.copy(s_stats[0:8, j * 128:(j + 1) * 128],
                                               pt_f[0:8, 128:256]),
                       [buf("ptf")], STB(j * 128, 128))
            dbg_tap("moe_mskf", s_stats[0:8, 0:2048], STB(0, 2048), shape=[8, 2048])
            v(lambda en: en.tensor_copy(s_bf[0:8, 0:2048], s_stats[0:8, 0:2048]),
              STB(0, 2048), BFS(0, 2048))
            dmav(d_bc1[0:8, :], s_bf[0:8, 0:2048], BFS(0, 2048), [buf("d_bc1")])

            for c in range(4):
                for dm in range(8):
                    mm(pb0[:], s_bf[0:8, 30720 + dm * 128:30720 + (dm + 1) * 128],
                       s_bf[0:8, c * 512:(c + 1) * 512], True, True,
                       BFS(30720, 1024) + BFS(0, 2048), [buf("pb0")])
                    yl = WK(dm % 2)[:, 0:512]
                    sc(lambda en, yl=yl: en.copy(yl, pb0[:]), [buf("pb0")], WKB(dm % 2))
                    dmas(d_y1[dm * 128:(dm + 1) * 128, c * 512:(c + 1) * 512], yl,
                         WKB(dm % 2) + [buf("d_y1")], [buf(f"dy1{dm}")])

            XCC, W1R, H1M, WST2, MBC = 0, 4096, 20480, 28672, 31744
            for e in range(NEXP):
                for kk in range(8):
                    dmas(s_bf[:, W1R + kk * 2048:W1R + (kk + 1) * 2048],
                         d_wqm[e * 3072 + kk * 128:e * 3072 + (kk + 1) * 128, 0:2048],
                         [buf("dwqmA")], BFS(W1R + kk * 2048, 2048))
                for c in range(4):
                    dmas(s_bf[:, XCC:XCC + 4096],
                         tile_kmajor(d_u[0:1024, c * 512:(c + 1) * 512]),
                         [buf("d_u")], BFS(XCC, 4096))
                    dmav(s_stats[:, 1536:2048],
                         d_bc1[e:e + 1, c * 512:(c + 1) * 512].broadcast_to((128, 512)),
                         [buf("d_bc1")], STB(1536, 512))
                    for m_ in range(16):
                        pp = pb0 if m_ % 2 == 0 else pb1
                        ppb = [buf("pb0") if m_ % 2 == 0 else buf("pb1")]
                        for kk in range(8):
                            mm(pp[:], s_bf[:, W1R + kk * 2048 + m_ * 128:
                                           W1R + kk * 2048 + (m_ + 1) * 128],
                               s_bf[:, XCC + kk * 512:XCC + (kk + 1) * 512],
                               kk == 0, kk == 7,
                               BFS(W1R + kk * 2048, 2048) + BFS(XCC, 4096), ppb)
                        hb1 = s_bf[:, H1M + m_ * 512:H1M + (m_ + 1) * 512]
                        sc(lambda en, hb1=hb1, pp=pp, e=e, m_=m_: en.activation(
                            hb1, pp[:], AF.Relu,
                            bias=s_m2p[:, e * 16 + m_:e * 16 + m_ + 1]),
                           ppb + [buf("m2p")], BFS(H1M + m_ * 512, 512))
                        v(lambda en, hb1=hb1: en.tensor_tensor(
                            hb1, hb1, s_stats[:, 1536:2048], OP.mult),
                          STB(1536, 512) + BFS(H1M + m_ * 512, 512),
                          BFS(H1M + m_ * 512, 512))
                    for dm in range(8):
                        pq = p_big[:, (dm % 4) * 512:(dm % 4) * 512 + 512]
                        pqb = PBG((dm % 4) * 512, 512)
                        W2S = WST2 + (dm % 2) * 2048
                        dmas(s_bf[:, W2S:W2S + 2048],
                             d_wqm[e * 3072 + 1024 + dm * 128:
                                   e * 3072 + 1024 + (dm + 1) * 128, 0:2048],
                             [buf("dwqmB")] + BFS(W2S, 2048), BFS(W2S, 2048))
                        for kk in range(16):
                            mm(pq, s_bf[:, W2S + kk * 128:W2S + (kk + 1) * 128],
                               s_bf[:, H1M + kk * 512:H1M + (kk + 1) * 512],
                               kk == 0, kk == 15,
                               BFS(W2S, 2048) + BFS(H1M + kk * 512, 512), pqb)
                        yl = WK(dm % 2)[:, 0:512]
                        dmas(yl, d_y1[dm * 128:(dm + 1) * 128, c * 512:(c + 1) * 512],
                             [buf(f"dy1{dm}")] + WKB(dm % 2), WKB(dm % 2))
                        v(lambda en, yl=yl, pq=pq: en.tensor_tensor(
                            yl, yl, pq, OP.add), pqb + WKB(dm % 2), WKB(dm % 2))
                        dmas(d_y1[dm * 128:(dm + 1) * 128, c * 512:(c + 1) * 512], yl,
                             WKB(dm % 2), [buf(f"dy1{dm}")])

            dbg_tap("moe_xh", d_u[0:1024, :], [buf("d_u")], dt=BF16, shape=[1024, T])
            dbg_tap("moe_mask", d_bc1[0:8, :], [buf("d_bc1")], dt=BF16, shape=[8, T])
            dbg_tap("moe_y", d_y1[0:1024, :], [buf(f"dy1{i}") for i in range(8)], shape=[1024, T])
            dbg_tap("moe_hx3", s_hx3[:, :], [buf("hx3")], shape=[128, 160])
            for c in range(4):
                for dm in range(8):
                    yl = WK(2 + dm % 2)[:, 0:512]
                    dmas(yl, d_y1[dm * 128:(dm + 1) * 128, c * 512:(c + 1) * 512],
                         [buf(f"dy1{dm}")] + WKB(2 + dm % 2), WKB(2 + dm % 2))
                    v(lambda en, yl=yl, dm=dm, c=c: en.tensor_tensor(
                        s_res[:, dm * T + c * 512:dm * T + (c + 1) * 512],
                        s_res[:, dm * T + c * 512:dm * T + (c + 1) * 512], yl,
                        OP.add),
                      WKB(2 + dm % 2) + [buf("x")], [buf("x")])

        if upto == "all":
            moe()

        for dmt in range(DT):
            dmas(out[dmt * 128:(dmt + 1) * 128, :], s_res[:, dmt * T:(dmt + 1) * T],
                 reads=[buf("x")])
        S.prog["sync"].append(
            lambda e, sem=S.sem["qs"], val=S.cnt["qs"]: e.wait_ge(sem, val))

        for eng in ENGINES:
            deco = getattr(blk, eng)
            prog = S.prog[eng]
            def make(prog):
                def body(h):
                    for fn in prog:
                        fn(h)
                return body
            deco(make(prog))

    return nc


def _colmajor(vec, p=128):
    v = np.asarray(vec).reshape(-1)
    return np.ascontiguousarray(v.reshape(-1, p).T)


def _m2small(inputs):
    A = lambda k: np.asarray(inputs[k]).astype(np.float32)
    blks = []
    for l in range(2):
        b = np.zeros((128, 128), np.float32)
        conv = A("L1_attn_conv_w")[l]
        b[:, 0:68] = conv.reshape(17, 128, 4).transpose(1, 0, 2).reshape(128, 68)
        b[:, 68:85] = _colmajor(A("L1_attn_conv_b")[l])
        b[:, 85:93] = _colmajor(A("L1_attn_norm_w")[l])
        b[:, 93:109] = _colmajor(A("L1_attn_gate_norm_w")[l])
        b[0:32, 109] = A("L1_attn_A_log")[l]
        b[0:32, 110] = A("L1_attn_dt_bias")[l]
        b[0:32, 111] = A("L1_attn_D")[l]
        blks.append(b)
    return np.ascontiguousarray(np.concatenate(blks, 0))


def _stage_common(inputs):
    cT = np.ascontiguousarray
    tr = lambda a: np.ascontiguousarray(np.transpose(np.asarray(a)))
    A = lambda k: np.asarray(inputs[k])
    return dict(
        identf=np.eye(128, dtype=np.float32),
        constf=np.array([[1e-5, 1.0, 12582912.0, -12582912.0, -1.0, -0.5, 2.0, 1.0 / 127.0]], np.float32),
        trimask=np.triu(np.ones((128, 128), dtype=np.float32)),
        W_in=cT(A("L0_attn_W_in")),
        conv_w0=cT(A("L0_attn_conv_w")[:, 0, :].reshape(16, 128, 4).transpose(1, 0, 2).reshape(128, 64)),
        conv_b0=_colmajor(A("L0_attn_conv_b")),
        W_xp=cT(A("L0_attn_W_xp")),
        W_dt=cT(A("L0_attn_W_dt")),
        dt_bias0=_colmajor(A("L0_attn_dt_bias")),
        A_log0=cT(A("L0_attn_A_log")),
        D_skip0=_colmajor(A("L0_attn_D_skip")),
        W_out0=cT(A("L0_attn_W_out")),
        W_fc_t=tr(A("L0_ffn_W_fc")),
        b_fc=_colmajor(A("L0_ffn_b_fc")),
        A_fc_t=tr(A("L0_ffn_A_fc")),
        B_fc_t=tr(A("L0_ffn_B_fc")),
        W_proj_t=tr(A("L0_ffn_W_proj")),
        b_proj=_colmajor(A("L0_ffn_b_proj")),
        A_proj_t=tr(A("L0_ffn_A_proj")),
        B_proj_t=tr(A("L0_ffn_B_proj")),
        in_proj1=cT(np.transpose(A("L1_attn_in_proj_w"), (0, 2, 1)).reshape(2 * 1024, 4256)),
        out_proj1=cT(np.transpose(A("L1_attn_out_proj_w"), (0, 2, 1)).reshape(2 * 2048, 1024)),
        m2small=_m2small(inputs),
        m2drow=cT(np.repeat(A("L1_attn_D").astype(np.float32), 64, axis=1)),
        normf_cols=_colmajor(A("L1_attn_norm_f_w")),
        Wg=cT(A("L1_ffn_Wg")),
        W1e=cT(A("L1_ffn_W1")),
        b1e=cT(A("L1_ffn_b1")),
        W2e=cT(A("L1_ffn_W2")),
        b2e=cT(A("L1_ffn_b2")),
        b1cols=np.ascontiguousarray(
            np.asarray(inputs["L1_ffn_b1"]).astype(np.float32).reshape(8, 16, 128)
            .transpose(2, 0, 1).reshape(128, 128)),
    )


def _in_maps(inputs):
    T_ALL = 16384
    x_full = np.asarray(inputs["L0_attn_x"]).reshape(T_ALL, D).astype(np.float32)
    common = _stage_common(inputs)
    maps = []
    for k in range(NCORES):
        lo = k * T
        xs = np.ascontiguousarray(x_full[lo:lo + T, :].T)

        cs = np.zeros((1, 32), np.float32)
        cs[0, 0:8] = (np.arange(8) < k)
        cs[0, 8:16] = (np.arange(8) == k - 1)
        cs[0, 16] = k
        m = dict(common)
        m.update(x_in=xs, consts=cs)
        maps.append(m)
    return maps


def kernel(**inputs):
    nc = build_nc()
    res = run_bass_kernel_spmd(nc, _in_maps(inputs), core_ids=list(range(NCORES)))
    o = np.concatenate([np.transpose(r["out"]) for r in res.results], axis=0)
    return o.reshape(1, 16384, D)


if __name__ == "__main__":
    nc = build_nc("l0ffn")
    print("built")
